# version 02
# baseline (speedup 1.0000x reference)
import math
import numpy as np
import concourse.bass as bass
import concourse.mybir as mybir
from concourse.ap import AP
from concourse.bass_utils import run_bass_kernel_spmd

F32 = mybir.dt.float32
BF16 = mybir.dt.bfloat16
I32 = mybir.dt.int32
AF = mybir.ActivationFunctionType
ALU = mybir.AluOpType
AX = mybir.AxisListType

NCORES = 8
B_FULL, T, C = 16, 2048, 1024
SEQ_PER_CORE = B_FULL // NCORES
NTOK = SEQ_PER_CORE * T
NT = NTOK // 128
TPS = T // 128
LN_EPS = 1e-5
SETTLE = 512


SETTLE_NOP = True


def settle(e, cycles=SETTLE):
    if SETTLE_NOP:
        e.nop(cycle_cnt=cycles)


class Stepper:

    def __init__(self, b, block, tag, extra_waits=()):
        self.b, self.block, self.tag = b, block, tag
        self.s = b.sem(f"st_{tag}")
        self.extra = list(extra_waits)

    def step(self, engine, fn, waits=(), settle_cycles=SETTLE):
        s = self.s
        prev = s.n
        tgt = s.inc()
        extra = self.extra
        self.extra = []

        def emit(e):
            if prev:
                e.wait_ge(s.h, prev)
            for sem, cnt in (list(extra) + list(waits)):
                e.wait_ge(sem.h, cnt)
            settle(e, settle_cycles)
            ins = fn(e)
            assert ins is not None, "step fn must return the last instruction"
            ins.then_inc(s.h, 1)
        getattr(self.block, engine)(emit)
        return tgt


class Sem:

    def __init__(self, nc, name):
        self.h = nc.alloc_semaphore(name=name)
        self.n = 0

    def inc(self, k=1):
        self.n += k
        return self.n


class Bld:

    NSEM = 100

    def __init__(self, name="mk"):
        self.nc = bass.Bass("TRN2", target_bir_lowering=False, name=name)
        self._sems = {}
        self._block = None
        self.inputs = {}

    def sem(self, name) -> Sem:
        if name not in self._sems:
            assert self._pool, "sem pool exhausted"
            s = Sem.__new__(Sem)
            s.h = self._pool.pop(0)
            s.n = 0
            self._sems[name] = s
        return self._sems[name]

    def inp(self, name, arr, dtype=F32):
        h = self.nc.dram_tensor(name, list(arr.shape), dtype, kind="ExternalInput")
        self.inputs[name] = arr
        return h.ap()

    def out(self, name, shape, dtype=F32):
        return self.nc.dram_tensor(name, list(shape), dtype, kind="ExternalOutput").ap()

    def scratch(self, name, shape, dtype=F32):
        return self.nc.dram_tensor(name, list(shape), dtype, kind="Internal").ap()

    def sb(self, name, shape, dtype=F32):
        if not hasattr(self, "_sb_cache"):
            self._sb_cache = {}
        if name in self._sb_cache:
            return self._sb_cache[name]
        t = self.nc.alloc_sbuf_tensor(name, list(shape), dtype)
        self._sb_cache[name] = t
        return t

    def ps(self, name, shape, dtype=F32):
        return self.nc.alloc_psum_tensor(name, list(shape), dtype)

    def __enter__(self):
        self._cleanup = self.nc.cleanup_on_exit()
        self._cleanup.__enter__()
        self._pool = [self.nc.alloc_semaphore(name=f"s{i}") for i in range(self.NSEM)]
        nums = [h.num for h in self._pool]
        self.sem_range = range(min(nums), max(nums) + 1)
        assert len(nums) == len(set(nums)) and max(nums) - min(nums) + 1 == len(nums)
        self.nc.gpsimd.dma_reset(self.sem_range)
        self.nc.gpsimd.sem_clear(self.sem_range)
        self._block = self.nc.Block()
        self.block = self._block.__enter__()
        s_boot = self.sem("s_boot")
        self.block.gpsimd(lambda e: e.sem_inc(s_boot.h, 1))
        s_boot.inc()
        for eng in ("sync", "tensor", "vector", "scalar"):
            getattr(self.block, eng)(lambda e: e.wait_ge(s_boot.h, 1))
        self.cst_vals = {}
        self.cst_tile = self.sb("csts_sb", [128, 64], F32)
        self.cst_in = self.nc.dram_tensor("csts", [128, 64], F32,
                                          kind="ExternalInput").ap()
        self.s_cst = self.sem("s_cst")
        cst_tile, cst_in, s_cst = self.cst_tile, self.cst_in, self.s_cst
        def _ld(e):
            e.dma_start(out=cst_tile[:], in_=cst_in).then_inc(s_cst.h, 16)
        self.block.sync(_ld)
        self.s_cst.inc(16)
        self.idb = self.sb("ident_sb", [128, 128], BF16)
        self.id_in = self.nc.dram_tensor("ident", [128, 128], F32,
                                         kind="ExternalInput").ap()
        self.idf = self.sb("identf_sb", [128, 128], F32)
        self.cst_ones_b = self.sb("ones_b", [128, 128], BF16)
        self.ones_in = self.nc.dram_tensor("ones64", [128, 128], F32,
                                           kind="ExternalInput").ap()
        self.s_cstg = self.sem("s_cstg")
        idb, idf, id_in = self.idb, self.idf, self.id_in
        s_cst, s_cstg = self.s_cst, self.s_cstg
        ones_b, ones_in = self.cst_ones_b, self.ones_in
        def _ldi(e):
            e.dma_start(out=idb[:], in_=id_in).then_inc(s_cstg.h, 16)
            e.dma_start(out=ones_b[:], in_=ones_in).then_inc(s_cstg.h, 16)
        self.block.gpsimd(_ldi)
        self.s_cstg.inc(32)
        def _ldf(e):
            e.dma_start(out=idf[:], in_=id_in).then_inc(s_cst.h, 16)
        self.block.sync(_ldf)
        self.s_cst.inc(16)
        return self

    def cstcol(self, name, value):
        if name not in self.cst_vals:
            self.cst_vals[name] = (len(self.cst_vals), float(value))
        i, v = self.cst_vals[name]
        assert v == float(value)
        return self.cst_tile[:, i:i + 1]

    def cst_host_array(self):
        a = np.zeros((128, 64), np.float32)
        for i, v in self.cst_vals.values():
            a[:, i] = v
        return a

    def __exit__(self, a, bb, c):
        return False

    def finish(self):
        self._block.__exit__(None, None, None)
        self.nc.all_engine_barrier()
        self.nc.gpsimd.dma_reset(self.sem_range)
        self._cleanup.__exit__(None, None, None)


def emit_load_x(b, block, x_hbm, xs, sem, queue="sync"):
    def f(e):
        for i in range(NT):
            e.dma_start(out=xs[:, i * C:(i + 1) * C],
                        in_=x_hbm[i * 128:(i + 1) * 128, :]).then_inc(sem.h, 16)
            sem.inc(16)
    getattr(block, queue)(f)


def emit_ln(b, block, xs, ns, wait_sems, done_sem, tag, n_tiles=NT,
            sq=None, stats=None, gain_rep=None, out_dtype=None):
    nc = b.nc
    if sq is None:
        sq = b.sb(f"lnsq_{tag}", [128, C], F32)
    if stats is None:
        stats = b.sb(f"lnst_{tag}", [128, 8 * n_tiles], F32)
    b.dbg_stats = stats
    ntl = n_tiles
    s_sq = b.sem(f"ln_sq_{tag}")
    s_dv = b.sem(f"ln_dv_{tag}")
    s_st = b.sem(f"ln_std_{tag}")
    s_rb = b.sem(f"ln_rb_{tag}")

    def act1(e):
        for s, cnt in wait_sems:
            e.wait_ge(s.h, cnt)
        for i in range(ntl):
            ins = e.activation(sq[:], xs[:, i * C:(i + 1) * C], AF.Square,
                               accum_out=stats[:, ntl + i:ntl + i + 1])
        ins.then_inc(s_sq.h, 1)
        s_sq.inc()
    block.scalar(act1)

    mu = b.sem(f"ln_mu_{tag}")

    def dve1(e):
        for s, cnt in wait_sems:
            e.wait_ge(s.h, cnt)
        for i in range(ntl):
            ins = e.reduce_sum(stats[:, i:i + 1], xs[:, i * C:(i + 1) * C], axis=AX.X)
        ins.then_inc(mu.h, mu.inc())
        e.wait_ge(mu.h, mu.n)
        e.wait_ge(s_sq.h, s_sq.n)
        settle(e)
        e.tensor_scalar_mul(stats[:, 2 * ntl:3 * ntl], stats[:, 0:ntl], 1.0 / C)
        e.tensor_scalar_mul(stats[:, 3 * ntl:4 * ntl], stats[:, ntl:2 * ntl],
                            1.0 / C).then_inc(mu.h, mu.inc())
        e.wait_ge(mu.h, mu.n)
        settle(e)
        e.tensor_tensor(stats[:, 0:ntl], stats[:, 2 * ntl:3 * ntl],
                        stats[:, 2 * ntl:3 * ntl], ALU.mult)
        e.tensor_scalar_mul(stats[:, ntl:2 * ntl], stats[:, 2 * ntl:3 * ntl],
                            -1.0).then_inc(mu.h, mu.inc())
        e.wait_ge(mu.h, mu.n)
        settle(e)
        e.tensor_tensor(stats[:, 3 * ntl:4 * ntl], stats[:, 3 * ntl:4 * ntl],
                        stats[:, 0:ntl], ALU.subtract).then_inc(s_dv.h, 1)
        s_dv.inc()
    block.vector(dve1)

    def act2(e):
        e.wait_ge(s_dv.h, s_dv.n)
        e.wait_ge(b.s_cst.h, b.s_cst.n)
        settle(e)
        e.activation(stats[:, 4 * ntl:5 * ntl], stats[:, 3 * ntl:4 * ntl],
                     AF.Sqrt, bias=b.cstcol("ln_eps", LN_EPS)).then_inc(s_st.h, 1)
        s_st.inc()
    block.scalar(act2)

    def dve2(e):
        e.wait_ge(s_st.h, s_st.n)
        settle(e)
        e.reciprocal(stats[:, 5 * ntl:6 * ntl],
                     stats[:, 4 * ntl:5 * ntl]).then_inc(mu.h, mu.inc())
        e.wait_ge(mu.h, mu.n)
        settle(e)
        e.tensor_tensor(stats[:, 6 * ntl:7 * ntl], stats[:, ntl:2 * ntl],
                        stats[:, 5 * ntl:6 * ntl], ALU.mult).then_inc(s_rb.h, 1)
        s_rb.inc()
    block.vector(dve2)

    def act3(e):
        e.wait_ge(s_rb.h, s_rb.n)
        settle(e)
        for i in range(ntl):
            ins = e.activation(ns[:, i * C:(i + 1) * C], xs[:, i * C:(i + 1) * C],
                               AF.Identity, scale=stats[:, 5 * ntl + i:5 * ntl + i + 1],
                               bias=stats[:, 6 * ntl + i:6 * ntl + i + 1])
        ins.then_inc(done_sem.h, 1)
        done_sem.inc()
    block.scalar(act3)


def emit_A2(b, block, x_hbm, xo_hbm, filt_hbm):
    nc = b.nc
    xs = b.sb("a2_xs", [128, NT * C], F32)
    ns = b.sb("a2_ns", [128, NT * C], BF16)
    zrow = b.sb("a2_z", [1, 127], F32)
    hload = b.sb("a2_h", [128, 128], F32)
    bt = b.sb("a2_bt", [128, 16 * 128], BF16)
    ot = b.sb("a2_ot", [128, C], F32)
    fz_dram = b.scratch("a2_fzd", [2175], F32)

    s_ld = b.sem("a2_ld")
    s_fz = b.sem("a2_fz")
    s_fzd = b.sem("a2_fzd")
    s_h = b.sem("a2_h")
    s_b = b.sem("a2_b")
    s_n = b.sem("a2_n")
    s_pe = b.sem("a2_pe")
    s_ev = b.sem("a2_ev")
    s_st = b.sem("a2_st")

    emit_load_x(b, block, x_hbm, xs, s_ld, queue="sync")

    def g1(e):
        e.memset(zrow[:], 0.0).then_inc(s_fz.h, 1)
        s_fz.inc()
        e.wait_ge(s_fz.h, s_fz.n)
        zd = AP(fz_dram.tensor, 0, [[127, 1], [1, 127]])
        e.dma_start(out=zd, in_=zrow[0:1, :]).then_inc(s_fzd.h, 16)
        fd = AP(fz_dram.tensor, 127, [[2048, 1], [1, 2048]])
        filt2d = AP(filt_hbm.tensor, 0, [[2048, 1], [1, 2048]])
        e.dma_start(out=fd, in_=filt2d).then_inc(s_fzd.h, 16)
        s_fzd.inc(32)
        e.wait_ge(s_fzd.h, s_fzd.n)
        for d in range(16):
            src = AP(fz_dram.tensor, d * 128, [[1, 128], [1, 128]])
            e.dma_start(out=hload[:], in_=src).then_inc(s_h.h, 16)
            s_h.inc(16)
            e.wait_ge(s_b.h, (d + 1))
    block.gpsimd(g1)

    def dve_b(e):
        for d in range(16):
            e.wait_ge(s_h.h, 16 * (d + 1))
            e.tensor_copy(bt[:, d * 128:(d + 1) * 128], hload[:, 127::-1]).then_inc(s_b.h, 1)
            s_b.inc()
    block.vector(dve_b)

    emit_ln(b, block, xs, ns, [(s_ld, s_ld.n)], s_n, "a2")

    ps0 = b.ps("a2_ps0", [128, 512], F32)
    ps1 = b.ps("a2_ps1", [128, 512], F32)
    ps2 = b.ps("a2_ps2", [128, 512], F32)
    ps3 = b.ps("a2_ps3", [128, 512], F32)
    psl = [ps0, ps1, ps2, ps3]

    def pe(e):
        e.wait_ge(s_n.h, s_n.n)
        e.wait_ge(s_b.h, s_b.n)
        k = 0
        for s in range(SEQ_PER_CORE):
            for i in range(TPS):
                grp = k % 2
                if k >= 2:
                    e.wait_ge(s_ev.h, (k - 1))
                pa, pb_ = psl[2 * grp], psl[2 * grp + 1]
                nd = TPS - i
                for jj, j in enumerate(range(i, TPS)):
                    d = j - i
                    tile = s * TPS + j
                    rhs0 = ns[:, tile * C: tile * C + 512]
                    rhs1 = ns[:, tile * C + 512: (tile + 1) * C]
                    lt = bt[:, d * 128:(d + 1) * 128]
                    st0 = jj == 0
                    sp = jj == nd - 1
                    e.matmul(pa[:], lt, rhs0, start=st0, stop=sp)
                    ins = e.matmul(pb_[:], lt, rhs1, start=st0, stop=sp)
                    if sp:
                        ins.then_inc(s_pe.h, 1)
                k += 1
                s_pe.inc()
    block.tensor(pe)

    def dve_ev(e):
        for k in range(NT):
            grp = k % 2
            pa, pb_ = psl[2 * grp], psl[2 * grp + 1]
            e.wait_ge(s_pe.h, k + 1)
            og = ot[:, 0:C]
            if k >= 1:
                e.wait_ge(s_st.h, 16 * k)
            e.tensor_tensor(og[:, 0:512], pa[:], xs[:, k * C:k * C + 512], ALU.add)
            e.tensor_tensor(og[:, 512:1024], pb_[:], xs[:, k * C + 512:(k + 1) * C],
                            ALU.add).then_inc(s_ev.h, 1)
            s_ev.inc()
    block.vector(dve_ev)

    def store(e):
        for k in range(NT):
            e.wait_ge(s_ev.h, k + 1)
            og = ot[:, 0:C]
            e.dma_start(out=xo_hbm[k * 128:(k + 1) * 128, :], in_=og[:]).then_inc(s_st.h, 16)
            s_st.inc(16)
        e.wait_ge(s_st.h, s_st.n)
    block.sync(store)

    if getattr(b, "debug", False):
        ns_d = b.out("dbg_ns", (128, NT * C), BF16)
        bt_d = b.out("dbg_bt", (128, 16 * 128), BF16)
        st_d = b.out("dbg_stats", (128, 8 * NT), F32)
        xs_d = b.out("dbg_xs", (128, NT * C), F32)
        stats_t = b.nc.virtual_tensors.get("x") if False else None
        s_dbg = b.sem("a2_dbg")
        def dbg(e):
            e.wait_ge(s_n.h, s_n.n)
            e.wait_ge(s_b.h, s_b.n)
            e.dma_start(out=ns_d, in_=ns[:]).then_inc(s_dbg.h, 16)
            e.dma_start(out=bt_d, in_=bt[:]).then_inc(s_dbg.h, 16)
            e.dma_start(out=st_d, in_=b.dbg_stats[:]).then_inc(s_dbg.h, 16)
            e.dma_start(out=xs_d, in_=xs[:]).then_inc(s_dbg.h, 16)
            s_dbg.inc(64)
            e.wait_ge(s_dbg.h, s_dbg.n)
        block.gpsimd(dbg)


def build_A2_program(core_inputs_example, debug=False):
    b = Bld("a2")
    b.debug = debug
    x_in = b.inp("x_in", np.zeros((NTOK, C), np.float32))
    filt = b.inp("filt", np.zeros((T,), np.float32))
    xo = b.out("x_out", (NTOK, C), F32)
    with b:
        emit_A2(b, b.block, x_in, xo, filt)
        b.finish()
    return b




def emit_ln_fm(b, block, x_hbm, n_fm, tag, ps_a, ps_b, also_tm=None, t0=0, nt=NT, pre_waits=()):
    nc = b.nc
    xr = b.sb(f"lnf_xr_{tag}", [128, 4 * C], F32)
    sq = b.sb(f"lnf_sq_{tag}", [128, C], F32)
    stats = b.sb(f"lnf_st_{tag}", [128, 8 * NT], F32)
    nr = also_tm if also_tm is not None else b.sb(f"lnf_nr_{tag}", [128, 2 * C], BF16)
    nslot = (lambda i: i) if also_tm is not None else (lambda i: i % 2)

    s_xa = [b.sem(f"lnf_xa{k}_{tag}") for k in range(4)]
    s_ca = b.sem(f"lnf_ca_{tag}")
    s_cb = b.sem(f"lnf_cb_{tag}")
    s_xb = [b.sem(f"lnf_xb{k}_{tag}") for k in range(4)]
    s_nr = b.sem(f"lnf_nr_{tag}")
    s_tr = b.sem(f"lnf_tr_{tag}")
    s_nfm = b.sem(f"lnf_nfm_{tag}")
    ntl = nt
    xa0 = [s.n for s in s_xa]
    ca0, cb0 = s_ca.n, s_cb.n
    st_chain = Stepper(b, block, f"lnf_{tag}")

    def loads_a(e):
        for sem, cnt in pre_waits:
            e.wait_ge(sem.h, cnt)
        for i in range(ntl):
            if i >= 4:
                e.wait_ge(s_ca.h, ca0 + i - 3)
                e.wait_ge(s_cb.h, cb0 + i - 3)
            sl = s_xa[i % 4]
            e.dma_start(out=xr[:, (i % 4) * C:(i % 4 + 1) * C],
                        in_=x_hbm[(t0 + i) * 128:(t0 + i + 1) * 128, :]).then_inc(sl.h, 16)
            sl.inc(16)
    block.sync(loads_a)

    def act_a(e):
        for i in range(ntl):
            e.wait_ge(s_xa[i % 4].h, xa0[i % 4] + 16 * (i // 4 + 1))
            if i >= 1:
                e.wait_ge(s_ca.h, ca0 + i)
            e.activation(sq[:],
                         xr[:, (i % 4) * C:(i % 4 + 1) * C], AF.Square,
                         accum_out=stats[:, ntl + i:ntl + i + 1]).then_inc(s_ca.h, 1)
            s_ca.inc()
    block.scalar(act_a)

    def dve_a(e):
        for i in range(ntl):
            e.wait_ge(s_xa[i % 4].h, xa0[i % 4] + 16 * (i // 4 + 1))
            e.reduce_sum(stats[:, i:i + 1], xr[:, (i % 4) * C:(i % 4 + 1) * C],
                         axis=AX.X).then_inc(s_cb.h, 1)
            s_cb.inc()
    block.vector(dve_a)

    st_chain.extra = [(s_ca, s_ca.n), (s_cb, s_cb.n)]

    def f1(e):
        e.tensor_scalar_mul(stats[:, 2 * ntl:3 * ntl], stats[:, 0:ntl], 1.0 / C)
        return e.tensor_scalar_mul(stats[:, 3 * ntl:4 * ntl],
                                   stats[:, ntl:2 * ntl], 1.0 / C)
    st_chain.step("vector", f1)

    def f2(e):
        e.tensor_tensor(stats[:, 0:ntl], stats[:, 2 * ntl:3 * ntl],
                        stats[:, 2 * ntl:3 * ntl], ALU.mult)
        return e.tensor_scalar_mul(stats[:, ntl:2 * ntl],
                                   stats[:, 2 * ntl:3 * ntl], -1.0)
    st_chain.step("vector", f2)

    def f3(e):
        return e.tensor_tensor(stats[:, 3 * ntl:4 * ntl], stats[:, 3 * ntl:4 * ntl],
                               stats[:, 0:ntl], ALU.subtract)
    st_chain.step("vector", f3)

    def f4(e):
        return e.activation(stats[:, 4 * ntl:5 * ntl], stats[:, 3 * ntl:4 * ntl],
                            AF.Sqrt, bias=b.cstcol("ln_eps", LN_EPS))
    st_chain.step("scalar", f4, waits=[(b.s_cst, b.s_cst.n)])

    def f5(e):
        return e.reciprocal(stats[:, 5 * ntl:6 * ntl], stats[:, 4 * ntl:5 * ntl])
    st_chain.step("vector", f5)

    def f6(e):
        return e.tensor_tensor(stats[:, 6 * ntl:7 * ntl], stats[:, ntl:2 * ntl],
                               stats[:, 5 * ntl:6 * ntl], ALU.mult)
    stats_done = st_chain.step("vector", f6)
    s_chain = st_chain.s

    xb0 = [s.n for s in s_xb]
    nr0, tr0, nfm0 = s_nr.n, s_tr.n, s_nfm.n

    def loads_b(e):
        e.wait_ge(s_chain.h, stats_done)
        for i in range(ntl):
            if i >= 4:
                e.wait_ge(s_nr.h, nr0 + i - 3)
            sl = s_xb[i % 4]
            e.dma_start(out=xr[:, (i % 4) * C:(i % 4 + 1) * C],
                        in_=x_hbm[(t0 + i) * 128:(t0 + i + 1) * 128, :]).then_inc(sl.h, 16)
            sl.inc(16)
    block.sync(loads_b)

    def act_b(e):
        e.wait_ge(s_chain.h, stats_done)
        settle(e)
        for i in range(ntl):
            e.wait_ge(s_xb[i % 4].h, xb0[i % 4] + 16 * (i // 4 + 1))
            if also_tm is None and i >= 2:
                e.wait_ge(s_tr.h, tr0 + i - 1)
            e.activation(nr[:, nslot(i) * C:(nslot(i) + 1) * C],
                         xr[:, (i % 4) * C:(i % 4 + 1) * C], AF.Identity,
                         scale=stats[:, 5 * ntl + i:5 * ntl + i + 1],
                         bias=stats[:, 6 * ntl + i:6 * ntl + i + 1]).then_inc(s_nr.h, 1)
            s_nr.inc()
    block.scalar(act_b)

    def pe_tr(e):
        e.wait_ge(b.s_cstg.h, b.s_cstg.n)
        for i in range(ntl):
            ps = ps_a if i % 2 == 0 else ps_b
            e.wait_ge(s_nr.h, nr0 + i + 1)
            if i >= 2:
                e.wait_ge(s_nfm.h, nfm0 + i - 1)
            for ch in range(8):
                ins = e.transpose(ps[:, ch * 128:(ch + 1) * 128],
                                  nr[:, nslot(i) * C + ch * 128: nslot(i) * C + (ch + 1) * 128],
                                  b.idb[:])
            ins.then_inc(s_tr.h, 1)
            s_tr.inc()
    block.tensor(pe_tr)

    def dve_b(e):
        for i in range(ntl):
            ps = ps_a if i % 2 == 0 else ps_b
            e.wait_ge(s_tr.h, tr0 + i + 1)
            dst = AP(n_fm, i * 128, [[8 * nt * 128, 128], [nt * 128, 8], [1, 128]])
            e.tensor_copy(dst, AP(ps.tensor if hasattr(ps, "tensor") else ps,
                                  0, [[1024, 128], [128, 8], [1, 128]])
                          ).then_inc(s_nfm.h, 1)
            s_nfm.inc()
    block.vector(dve_b)

    return s_nfm, xr, s_nr


def emit_F1(b, block, x_hbm, xo_hbm, W, scale=0.5, stop_after=None):
    nc = b.nc
    E = 8
    trA = b.ps("f1_trA", [128, 1024], BF16)
    trB = b.ps("f1_trB", [128, 1024], BF16)
    pc = [b.ps(f"f1_pc{k}", [128, 512], F32) for k in range(6)]
    n_fm = b.sb("f1_nfm", [128, 8 * NTOK], BF16)
    gw_sb = b.sb("f1_gw", [128, 64], BF16)
    cw_sb = b.sb("f1_cw", [128, 8 * 1024], BF16)
    bf_sb = b.sb("f1_bf", [128, 64], F32)
    bc_sb = b.sb("f1_bc", [8, 1024], BF16)
    ew_sb = b.sb("f1_ew", [128, 8 * 1024], BF16)
    glog = b.sb("f1_glog", [128, 256], F32)
    gb = b.sb("f1_gb", [128, 1440], F32)
    wfm = b.sb("f1_wfm", [8, 4096], BF16)
    wrep = b.sb("f1_wrep", [128, 1024], F32)
    silu_sb = b.sb("f1_silu", [128, 4 * 512], F32)
    macc = b.sb("f1_macc", [128, 8 * 1024], F32)
    mcb = b.sb("f1_mcb", [128, 2 * 1024], BF16)
    yt = b.sb("f1_yt", [128, 2 * C], F32)

    s_w = b.sem("f1_w")
    s_ew = b.sem("f1_ew")
    s_gl = b.sem("f1_gl")
    s_gev = b.sem("f1_gev")
    s_wt = b.sem("f1_wt")
    s_wfm = b.sem("f1_wfm")
    s_wrp = b.sem("f1_wrp")
    s_wre = b.sem("f1_wre")
    s_eo = b.sem("f1_eo")
    s_si = b.sem("f1_si")
    s_ma = b.sem("f1_ma")
    s_mcb = b.sem("f1_mcb")
    s_y = b.sem("f1_y")
    s_xr2 = [b.sem(f"f1_xr2{k}") for k in range(4)]
    s_ye = b.sem("f1_ye")
    s_st = [b.sem(f"f1_st{k}") for k in range(2)]

    def wloads(e):
        e.dma_start(out=gw_sb[:], in_=AP(W["gwT"].tensor, 0,
                    [[8, 128], [1024, 8], [1, 8]])).then_inc(s_w.h, 16)
        e.dma_start(out=cw_sb[:], in_=AP(W["cwT"].tensor, 0,
                    [[1024, 128], [131072, 8], [1, 1024]])).then_inc(s_w.h, 16)
        e.dma_start(out=bf_sb[:], in_=AP(W["bfm"].tensor, 0,
                    [[8, 128], [1024, 8], [1, 8]])).then_inc(s_w.h, 16)
        e.dma_start(out=bc_sb[:], in_=W["bc8"]).then_inc(s_w.h, 16)
        s_w.inc(64)
    block.gpsimd(wloads)

    s_nfm, xr2, s_lnnr = emit_ln_fm(b, block, x_hbm, n_fm, "f1", trA, trB)
    nfm_done = s_nfm.n
    lnnr_done = s_lnnr.n

    if stop_after == "nfm":
        dbgo = b.out("dbg", (128, 8 * NTOK), BF16)
        s_dbg = b.sem("f1_dbg")
        def dbg(e):
            e.wait_ge(s_nfm.h, nfm_done)
            e.dma_start(out=dbgo, in_=n_fm[:]).then_inc(s_dbg.h, 16)
            s_dbg.inc(16)
            e.wait_ge(s_dbg.h, s_dbg.n)
        block.sync(dbg)
        return

    gl0 = s_gl.n

    def pe_glog(e):
        e.wait_ge(s_nfm.h, nfm_done)
        e.wait_ge(s_w.h, s_w.n)
        for i in range(NT):
            half = 0 if i < 16 else 1
            for ch in range(8):
                ins = e.matmul(pc[half][:, (i % 16) * 8:(i % 16) * 8 + 8],
                               n_fm[:, ch * NTOK + i * 128: ch * NTOK + (i + 1) * 128],
                               gw_sb[:, ch * 8:(ch + 1) * 8],
                               start=(ch == 0), stop=(ch == 7))
            ins.then_inc(s_gl.h, 1)
            s_gl.inc()
    block.tensor(pe_glog)

    def dve_gev(e):
        e.wait_ge(s_gl.h, s_gl.n)
        settle(e)
        e.tensor_copy(glog[:, 0:128], pc[0][:, 0:128])
        ins = e.tensor_copy(glog[:, 128:256], pc[1][:, 0:128])
        ins.then_inc(s_gev.h, 1)
        s_gev.inc()
    block.vector(dve_gev)

    if stop_after == "glog":
        dbgo = b.out("dbg", (128, 256), F32)
        s_dbg = b.sem("f1_dbg")
        def dbg(e):
            e.wait_ge(s_gev.h, s_gev.n)
            e.dma_start(out=dbgo, in_=glog[:]).then_inc(s_dbg.h, 16)
            s_dbg.inc(16)
            e.wait_ge(s_dbg.h, s_dbg.n)
        block.sync(dbg)
        return

    g = Stepper(b, block, "f1g", extra_waits=[(s_gev, s_gev.n)])
    EXP, PRB, SCR, WW = 0, 256, 512, 768
    ESUM, RSUM, M1, M2, REP = 1024, 1056, 1088, 1120, 1152

    def g1(e):
        for i in range(NT):
            ins = e.activation(gb[:, EXP + i * 8:EXP + (i + 1) * 8],
                               glog[:, i * 8:(i + 1) * 8], AF.Exp,
                               accum_out=gb[:, ESUM + i:ESUM + i + 1])
        return ins
    g.step("scalar", g1)

    def g2(e):
        return e.reciprocal(gb[:, RSUM:RSUM + 32], gb[:, ESUM:ESUM + 32])
    g.step("vector", g2)

    def rep32(e, dstc, srcc):
        src = AP(gb.tensor if hasattr(gb, "tensor") else gb, srcc,
                 [[1440, 128], [1, 32], [0, 8]])
        dst = AP(gb.tensor if hasattr(gb, "tensor") else gb, dstc,
                 [[1440, 128], [8, 32], [1, 8]])
        return e.tensor_copy(dst, src)

    g.step("vector", lambda e: rep32(e, REP, RSUM))
    g.step("vector", lambda e: e.tensor_tensor(gb[:, PRB:PRB + 256],
           gb[:, EXP:EXP + 256], gb[:, REP:REP + 256], ALU.mult))

    def g5(e):
        return e.tensor_reduce(gb[:, M1:M1 + 32],
                               AP(gb.tensor if hasattr(gb, "tensor") else gb,
                                  PRB, [[1440, 128], [8, 32], [1, 8]]),
                               axis=AX.X, op=ALU.max)
    g.step("vector", g5)
    g.step("vector", lambda e: rep32(e, REP, M1))
    g.step("vector", lambda e: e.tensor_tensor(gb[:, SCR:SCR + 256],
           gb[:, PRB:PRB + 256], gb[:, REP:REP + 256], ALU.is_lt))
    g.step("vector", lambda e: e.tensor_tensor(gb[:, SCR:SCR + 256],
           gb[:, SCR:SCR + 256], gb[:, PRB:PRB + 256], ALU.mult))

    def g9(e):
        return e.tensor_reduce(gb[:, M2:M2 + 32],
                               AP(gb.tensor if hasattr(gb, "tensor") else gb,
                                  SCR, [[1440, 128], [8, 32], [1, 8]]),
                               axis=AX.X, op=ALU.max)
    g.step("vector", g9)
    g.step("vector", lambda e: rep32(e, REP, M2))
    g.step("vector", lambda e: e.tensor_tensor(gb[:, SCR:SCR + 256],
           gb[:, PRB:PRB + 256], gb[:, REP:REP + 256], ALU.is_ge))
    gate_done = g.step("vector", lambda e: e.tensor_tensor(gb[:, WW:WW + 256],
           gb[:, PRB:PRB + 256], gb[:, SCR:SCR + 256], ALU.mult))
    s_g = g.s

    if stop_after == "gate":
        dbgo = b.out("dbg", (128, 1440), F32)
        s_dbg = b.sem("f1_dbg")
        def dbg(e):
            e.wait_ge(s_g.h, gate_done)
            e.dma_start(out=dbgo, in_=gb[:]).then_inc(s_dbg.h, 16)
            s_dbg.inc(16)
            e.wait_ge(s_dbg.h, s_dbg.n)
        block.sync(dbg)
        return

    wt0 = s_wt.n
    wfm0 = s_wfm.n

    def pe_wt(e):
        e.wait_ge(s_g.h, gate_done)
        for q in range(8):
            if q >= 1:
                e.wait_ge(s_wfm.h, wfm0 + q)
            for j in range(4):
                i = q * 4 + j
                ins = e.transpose(pc[2][0:8, j * 128:(j + 1) * 128],
                                  gb[:, WW + i * 8:WW + (i + 1) * 8], b.idf[:])
            ins.then_inc(s_wt.h, 1)
            s_wt.inc()
    block.tensor(pe_wt)

    def dve_wfm(e):
        for q in range(8):
            e.wait_ge(s_wt.h, wt0 + q + 1)
            settle(e)
            e.tensor_copy(wfm[:, q * 512:(q + 1) * 512],
                          pc[2][0:8, 0:512]).then_inc(s_wfm.h, 1)
            s_wfm.inc()
    block.vector(dve_wfm)
    wfm_done = s_wfm.n
    ew0, eo0 = s_ew.n, s_eo.n
    si0, ma0, wr0, wre0 = s_si.n, s_ma.n, s_wrp.n, s_wre.n
    mcb0, y0, ye0 = s_mcb.n, s_y.n, s_ye.n
    st0 = [x.n for x in s_st]
    xr20 = [x.n for x in s_xr2]

    if stop_after == "wfm":
        dbgo = b.out("dbg", (8, 4096), BF16)
        s_dbg = b.sem("f1_dbg")
        def dbg(e):
            e.wait_ge(s_wfm.h, wfm_done)
            e.dma_start(out=dbgo, in_=wfm[:]).then_inc(s_dbg.h, 16)
            s_dbg.inc(16)
            e.wait_ge(s_dbg.h, s_dbg.n)
        block.sync(dbg)
        return

    NG = 4
    TPG = NTOK // NG
    SPG = TPG // 512
    TLG = TPG // 128
    NLOAD = NG * 8
    single = stop_after in ("moe1", "e0", "moe")
    combine_on = (not single) or stop_after == "c1"
    if stop_after == "c1":
        single = True
    NGRUN = NG if (not single or stop_after == "moe") else 1
    if stop_after == "c1": NGRUN = 1
    NEXP = 8 if stop_after not in ("e0", "c1") else 1
    NLOAD_EFF = NGRUN * NEXP
    ew0 = s_ew.n
    eo0 = s_eo.n

    def pe_main(e):
        e.wait_ge(s_wfm.h, wfm_done)
        for grp in range(NGRUN):
            for eidx in range(NEXP):
                idx = grp * 8 + eidx
                e.wait_ge(s_ew.h, ew0 + 16 * (idx + 1))
                for ts4 in range(SPG):
                    bank = pc[2 + ts4 % 2]
                    g = idx * SPG + ts4
                    if g >= 2:
                        e.wait_ge(s_wre.h, wre0 + g - 1)
                    e.matmul(bank[:], bc_sb[:, eidx * 128:(eidx + 1) * 128],
                             wfm[0:8, grp * TPG + ts4 * 512: grp * TPG + (ts4 + 1) * 512],
                             start=True, stop=True).then_inc(s_wrp.h, 1)
                    s_wrp.inc()
                for co in range(8):
                    for ts in range(SPG):
                        k = co * SPG + ts
                        bank = pc[4 + k % 2]
                        if idx * 8 * SPG + k >= 2:
                            e.wait_ge(s_si.h, si0 + idx * 8 * SPG + k - 1)
                        for ci in range(8):
                            ins = e.matmul(bank[:],
                                           ew_sb[:, ci * 1024 + co * 128: ci * 1024 + (co + 1) * 128],
                                           n_fm[:, ci * NTOK + grp * TPG + ts * 512:
                                                ci * NTOK + grp * TPG + (ts + 1) * 512],
                                           start=(ci == 0), stop=(ci == 7))
                        ins.then_inc(s_eo.h, 1)
                        s_eo.inc()
            if not combine_on:
                continue
            for il in range(TLG):
                i = grp * TLG + il
                e.wait_ge(s_mcb.h, mcb0 + i + 1)
                settle(e, 256)
                for ch2 in range(2):
                    bank = pc[0 + ch2]
                    if i >= 1:
                        e.wait_ge(s_ye.h, ye0 + i)
                    for k1 in range(8):
                        ins = e.matmul(bank[:],
                                       mcb[:, (i % 2) * 1024 + k1 * 128:(i % 2) * 1024 + (k1 + 1) * 128],
                                       cw_sb[:, k1 * 1024 + ch2 * 512: k1 * 1024 + (ch2 + 1) * 512],
                                       start=(k1 == 0), stop=(k1 == 7))
                    ins.then_inc(s_y.h, 1)
                    s_y.inc()
    block.tensor(pe_main)

    def act_si(e):
        for idx in range(NLOAD_EFF if single else NLOAD):
            eidx = idx % 8
            for co in range(8):
                for ts in range(SPG):
                    k = co * SPG + ts
                    bank = pc[4 + k % 2]
                    e.wait_ge(s_eo.h, eo0 + idx * 8 * SPG + k + 1)
                    if idx * 8 * SPG + k >= 4:
                        e.wait_ge(s_ma.h, ma0 + idx * 8 * SPG + k - 3)
                    e.activation(silu_sb[:, (k % 4) * 512:(k % 4 + 1) * 512],
                                 bank[:], AF.Silu,
                                 bias=bf_sb[:, eidx * 8 + co:eidx * 8 + co + 1]
                                 ).then_inc(s_si.h, 1)
                    s_si.inc()
    block.scalar(act_si)

    mtmp = b.sb("f1_mtmp", [128, 2 * 512], F32)
    s_mm = b.sem("f1_mm")
    mm0 = s_mm.n

    def xr2_loads(e):
        if not combine_on:
            return
        e.wait_ge(s_lnnr.h, lnnr_done)
        for i in range(NT if stop_after != "c1" else 8):
            if i >= 4:
                e.wait_ge(s_ye.h, ye0 + i - 3)
            sl = s_xr2[i % 4]
            e.dma_start(out=xr2[:, (i % 4) * C:(i % 4 + 1) * C],
                        in_=x_hbm[i * 128:(i + 1) * 128, :]).then_inc(sl.h, 16)
            sl.inc(16)
    block.sync(xr2_loads)

    def ew_load(e, idx):
        eidx = idx % 8
        if idx > 0:
            e.wait_ge(s_eo.h, eo0 + 8 * SPG * idx)
        e.dma_start(out=ew_sb[:], in_=AP(W["ewT"].tensor, eidx * 1024 * 1024,
                    [[1024, 128], [131072, 8], [1, 1024]])).then_inc(s_ew.h, 16)
        s_ew.inc(16)

    def pool_main(e):
        ew_load(e, 0)
        kh = 0
        for grp in range(NGRUN):
            for eidx in range(NEXP):
                idx = grp * 8 + eidx
                e.wait_ge(s_wre.h, wre0 + (idx + 1) * SPG)
                settle(e)
                for co in range(8):
                    for ts in range(SPG):
                        k = co * SPG + ts
                        e.wait_ge(s_si.h, si0 + idx * 8 * SPG + k + 1)
                        settle(e, 256)
                        if kh >= 2:
                            e.wait_ge(s_ma.h, ma0 + kh - 1)
                        e.tensor_tensor(mtmp[:, (kh % 2) * 512:(kh % 2 + 1) * 512],
                                        silu_sb[:, (k % 4) * 512:(k % 4 + 1) * 512],
                                        wrep[:, ts * 512:(ts + 1) * 512],
                                        ALU.mult).then_inc(s_mm.h, 1)
                        s_mm.inc()
                        kh += 1
                if idx + 1 < NLOAD_EFF if single else idx + 1 < NLOAD:
                    ew_load(e, idx + 1)
            if not combine_on:
                continue
            for il in range(TLG):
                i = grp * TLG + il
                e.wait_ge(s_ye.h, ye0 + i + 1)
                sl = s_st[i % 2]
                e.dma_start(out=xo_hbm[i * 128:(i + 1) * 128, :],
                            in_=yt[:, (i % 2) * C:(i % 2 + 1) * C]).then_inc(sl.h, 16)
                sl.inc(16)
        if combine_on:
            e.wait_ge(s_st[0].h, s_st[0].n)
            e.wait_ge(s_st[1].h, s_st[1].n)
    block.gpsimd(pool_main)

    def dve_main(e):
        kh = 0
        for grp in range(NGRUN):
            for eidx in range(NEXP):
                idx = grp * 8 + eidx
                e.wait_ge(s_ma.h, ma0 + idx * 8 * SPG)
                for ts4 in range(SPG):
                    g = idx * SPG + ts4
                    e.wait_ge(s_wrp.h, wr0 + g + 1)
                    settle(e)
                    e.tensor_copy(wrep[:, ts4 * 512:(ts4 + 1) * 512],
                                  pc[2 + ts4 % 2][:]).then_inc(s_wre.h, 1)
                    s_wre.inc()
                for co in range(8):
                    for ts in range(SPG):
                        k = co * SPG + ts
                        e.wait_ge(s_mm.h, mm0 + kh + 1)
                        settle(e, 256)
                        dst = macc[:, co * TPG + ts * 512: co * TPG + (ts + 1) * 512]
                        sl = mtmp[:, (kh % 2) * 512:(kh % 2 + 1) * 512]
                        if eidx == 0:
                            e.tensor_copy(dst, sl).then_inc(s_ma.h, 1)
                        else:
                            e.tensor_tensor(dst, dst, sl, ALU.add).then_inc(s_ma.h, 1)
                        s_ma.inc()
                        kh += 1
            if not combine_on:
                continue
            e.wait_ge(s_ma.h, ma0 + (grp + 1) * NEXP * 8 * SPG)
            settle(e)
            def mcb_cast(il):
                i = grp * TLG + il
                if i >= 2:
                    e.wait_ge(s_y.h, y0 + 2 * (i - 1))
                src = AP(macc, il * 128, [[8 * TPG, 128], [TPG, 8], [1, 128]])
                dst = AP(mcb, (i % 2) * 1024, [[2048, 128], [128, 8], [1, 128]])
                e.tensor_copy(dst, src).then_inc(s_mcb.h, 1)
                s_mcb.inc()

            def ye(il):
                i = grp * TLG + il
                e.wait_ge(s_y.h, y0 + 2 * (i + 1))
                e.wait_ge(s_xr2[i % 4].h, xr20[i % 4] + 16 * (i // 4 + 1))
                settle(e, 256)
                if i >= 2:
                    e.wait_ge(s_st[i % 2].h, st0[i % 2] + 16 * (i // 2))
                sl = yt[:, (i % 2) * C:(i % 2 + 1) * C]
                e.scalar_tensor_tensor(sl[:, 0:512], pc[0][:], scale,
                                       xr2[:, (i % 4) * C:(i % 4) * C + 512],
                                       ALU.mult, ALU.add)
                e.scalar_tensor_tensor(sl[:, 512:1024], pc[1][:], scale,
                                       xr2[:, (i % 4) * C + 512:(i % 4 + 1) * C],
                                       ALU.mult, ALU.add).then_inc(s_ye.h, 1)
                s_ye.inc()

            mcb_cast(0)
            mcb_cast(1)
            for il in range(TLG):
                ye(il)
                if il + 2 < TLG:
                    mcb_cast(il + 2)
    block.vector(dve_main)

    if single and not combine_on:
        def dump(e):
            e.wait_ge(s_ma.h, s_ma.n)
            dbgo = b.out("dbg", (128, 8 * 1024), F32)
            dbg2 = b.out("dbg2", (128, 1024), F32)
            dbg3 = b.out("dbg3", (128, 2048), F32)
            s_dbg = b.sem("f1_dbg")
            e.dma_start(out=dbgo, in_=macc[:]).then_inc(s_dbg.h, 16)
            e.dma_start(out=dbg2, in_=wrep[:]).then_inc(s_dbg.h, 16)
            e.dma_start(out=dbg3, in_=silu_sb[:]).then_inc(s_dbg.h, 16)
            s_dbg.inc(48)
            e.wait_ge(s_dbg.h, s_dbg.n)
        block.sync(dump)

def f1_host_weights(z):
    gw = z["L1_ffn_gate_w"]; ew = z["L1_ffn_expert_w"]
    eb = z["L1_ffn_expert_b"]; cw = z["L1_ffn_combine_w"]
    gwT = np.ascontiguousarray(gw.T)
    ewT = np.ascontiguousarray(np.transpose(ew, (0, 2, 1)))
    cwT = np.ascontiguousarray(cw.T)
    bfm = np.ascontiguousarray(eb.reshape(8, 8, 128).transpose(0, 2, 1))
    bc8 = np.zeros((8, 1024), np.float32)
    for e in range(8):
        bc8[e, e * 128:(e + 1) * 128] = 1.0
    return dict(gwT=gwT, ewT=ewT, cwT=cwT, bfm=bfm, bc8=bc8)


def build_F1_program(z, stop_after=None):
    b = Bld("f1")
    hw = f1_host_weights(z)
    x_in = b.inp("x_in", np.zeros((NTOK, C), np.float32))
    W = {k: b.inp("f1_" + k, v) for k, v in hw.items()}
    xo = b.out("x_out", (NTOK, C), F32)
    with b:
        emit_F1(b, b.block, x_in, xo, W, scale=0.5, stop_after=stop_after)
        b.finish()
    return b, hw


def emit_A0(b, block, x_hbm, xo_hbm, W):
    nc = b.nc
    H = 16
    EPSL = 1.1920929e-07
    trA = b.ps("a0_trA", [128, 1024], BF16)
    trB = b.ps("a0_trB", [128, 1024], BF16)
    pc0 = b.ps("a0_pc0", [128, 512], F32)
    pc1 = b.ps("a0_pc1", [128, 512], F32)
    pctx = b.ps("a0_pctx", [128, 1024], F32)
    pz = b.ps("a0_pz", [128, 512], F32)
    n_fm = b.sb("a0_nfm", [128, 8 * T], BF16)
    kv_tm = b.sb("a0_kv", [128, 2 * 2048], BF16)
    qh = b.sb("a0_qh", [128, 2048], BF16)
    qz_sb = b.sb("a0_qz", [128, 2 * 512], BF16)
    wq_sb = b.sb("a0_wq", [128, 8 * 1024], BF16)
    wkv_sb = b.sb("a0_wkv", [128, 8 * 2048], BF16)
    wp_sb = b.sb("a0_wp", [128, 8 * 1024], BF16)
    br_sb = b.sb("a0_br", [128, 1024], F32)
    kc_sb = b.sb("a0_kc", [128, 16 * 64 + 16], BF16)
    cx_acc = b.sb("a0_cxa", [128, 16 * 64 + 16], F32)
    zr_sb = b.sb("a0_zr", [1, 512], BF16)
    zf_sb = b.sb("a0_zf", [1, 512], F32)
    zf2_sb = b.sb("a0_zf2", [1, 512], F32)
    ao_fm = b.sb("a0_ao", [128, 8 * T], BF16)
    yt = b.sb("a0_yt", [128, 2 * C], F32)

    s_w = b.sem("a0_w")
    s_qkv = b.sem("a0_qkv")
    s_qke = b.sem("a0_qke")
    s_cx = b.sem("a0_cx")
    s_cxe = b.sem("a0_cxe")
    s_qp = b.sem("a0_qp")
    s_qpe = b.sem("a0_qpe")
    s_zp = b.sem("a0_zp")
    s_ze = b.sem("a0_ze")
    s_zb = b.sem("a0_zb")
    s_qz = b.sem("a0_qz")
    s_ov = b.sem("a0_ov")
    s_ae = b.sem("a0_ae")
    s_y = b.sem("a0_y")
    s_ye = b.sem("a0_ye")
    s_t1 = b.sem("a0_t1")
    s_st = [b.sem(f"a0_st{k}") for k in range(2)]
    s_xr2 = [b.sem(f"a0_xr2{k}") for k in range(4)]

    def wloads(e):
        e.dma_start(out=wq_sb[:], in_=AP(W["wqkvT"].tensor, 0,
                    [[3072, 128], [393216, 8], [1, 1024]])).then_inc(s_w.h, 16)
        e.dma_start(out=wkv_sb[:], in_=AP(W["wqkvT"].tensor, 1024,
                    [[3072, 128], [393216, 8], [1, 2048]])).then_inc(s_w.h, 16)
        e.dma_start(out=wp_sb[:], in_=AP(W["wprojT"].tensor, 0,
                    [[1024, 128], [131072, 8], [1, 1024]])).then_inc(s_w.h, 16)
        s_w.inc(48)
    block.gpsimd(wloads)

    def brload(e):
        e.dma_start(out=br_sb[:], in_=W["brep"]).then_inc(s_w.h, 16)
        s_w.inc(16)
    block.gpsimd(brload)
    w_done = s_w.n

    y0, ye0, t10 = s_y.n, s_ye.n, s_t1.n
    st0 = [x.n for x in s_st]
    xr20 = [x.n for x in s_xr2]
    qkv0, qke0, cx0, cxe0 = s_qkv.n, s_qke.n, s_cx.n, s_cxe.n
    qp0, qpe0, zp0, ze0, zb0, qz0, ov0, ae0 = (s_zp.n, s_qpe.n, s_zp.n, s_ze.n,
                                               s_zb.n, s_qz.n, s_ov.n, s_ae.n)
    qp0 = s_qp.n

    lnnr_dones = []
    xr_buf = [None]

    for sq in range(SEQ_PER_CORE):
        s_nfm, xr2, s_lnnr = emit_ln_fm(b, block, x_hbm, n_fm, "a0", trA, trB,
                                        t0=sq * TPS, nt=TPS,
                                        pre_waits=([(s_ye, ye0 + sq * TPS)]
                                                   if sq else []))
        xr_buf[0] = xr2
        nfm_done = s_nfm.n
        lnnr_dones.append(s_lnnr.n)

        def pe_seq(e, sq=sq, nfm_done=nfm_done):
            e.wait_ge(s_nfm.h, nfm_done)
            e.wait_ge(s_w.h, w_done)
            e.wait_ge(b.s_cstg.h, b.s_cstg.n)
            if sq >= 1:
                e.wait_ge(s_ye.h, ye0 + sq * TPS)
                e.wait_ge(s_cxe.h, cxe0 + sq * (TPS + 1))
            settle(e)
            for it in range(TPS):
                for sl4 in range(4):
                    kq = (sq * TPS + it) * 4 + sl4
                    bank = pc0 if kq % 2 == 0 else pc1
                    if kq >= 2:
                        e.wait_ge(s_qke.h, qke0 + kq - 1)
                    for ci in range(8):
                        ins = e.matmul(bank[:],
                                       n_fm[:, ci * T + it * 128:ci * T + (it + 1) * 128],
                                       wkv_sb[:, ci * 2048 + sl4 * 512:ci * 2048 + (sl4 + 1) * 512],
                                       start=(ci == 0), stop=(ci == 7))
                    ins.then_inc(s_qkv.h, 1)
                    s_qkv.inc()
                e.wait_ge(s_qke.h, qke0 + (sq * TPS + it + 1) * 4)
                if it >= 1:
                    e.wait_ge(s_cxe.h, cxe0 + sq * (TPS + 1) + it)
                settle(e)
                slot = it % 2
                for h in range(H):
                    kt = kv_tm[:, slot * 2048 + h * 64: slot * 2048 + (h + 1) * 64]
                    vt = kv_tm[:, slot * 2048 + 1024 + h * 64: slot * 2048 + 1024 + (h + 1) * 64]
                    e.matmul(pz[0:64, h:h + 1], kt, b.cst_ones_b[:, 0:1],
                             start=True, stop=True)
                    ins = e.matmul(pctx[0:64, h * 64:(h + 1) * 64], kt, vt,
                                   start=True, stop=True)
                ins.then_inc(s_cx.h, 1)
                s_cx.inc()
            e.wait_ge(s_cxe.h, cxe0 + sq * (TPS + 1) + TPS + 1)
            settle(e)
            for h in range(H):
                hh = sq * H + h
                e.wait_ge(s_ae.h, ae0 + hh * 4)
                for sl4 in range(4):
                    gq = hh * 4 + sl4
                    bank = pc0 if gq % 2 == 0 else pc1
                    if gq >= 2:
                        e.wait_ge(s_qpe.h, qpe0 + gq - 1)
                    for ci in range(8):
                        ins = e.matmul(bank[0:64, :],
                                       wq_sb[:, ci * 1024 + h * 64:ci * 1024 + (h + 1) * 64],
                                       n_fm[:, ci * T + sl4 * 512:ci * T + (sl4 + 1) * 512],
                                       start=(ci == 0), stop=(ci == 7))
                    ins.then_inc(s_qp.h, 1)
                    s_qp.inc()
                e.wait_ge(s_qpe.h, qpe0 + (hh + 1) * 4)
                settle(e)
                for sl4 in range(4):
                    g = hh * 4 + sl4
                    qsl = qh[0:64, sl4 * 512:(sl4 + 1) * 512]
                    e.matmul(pz[0:1, 0:512], kc_sb[0:64, 1024 + h:1024 + h + 1], qsl,
                             start=True, stop=True).then_inc(s_zp.h, 1)
                    s_zp.inc()
                    e.wait_ge(s_ze.h, ze0 + 3 * g + 3)
                    settle(e)
                    e.matmul(pz[0:64, 0:512], b.cst_ones_b[0:1, 0:64],
                             zr_sb[0:1, :], start=True, stop=True).then_inc(s_zb.h, 1)
                    s_zb.inc()
                    e.wait_ge(s_qz.h, qz0 + g + 1)
                    settle(e)
                    bank = pc0 if g % 2 == 0 else pc1
                    if g >= 2:
                        e.wait_ge(s_ae.h, ae0 + g - 1)
                    e.matmul(bank[0:64, :], kc_sb[0:64, h * 64:(h + 1) * 64],
                             qz_sb[0:64, (g % 2) * 512:(g % 2 + 1) * 512],
                             start=True, stop=True).then_inc(s_ov.h, 1)
                    s_ov.inc()
            e.wait_ge(s_ae.h, ae0 + (sq + 1) * H * 4)
            settle(e)
            for it in range(TPS):
                i = sq * TPS + it
                for ch2 in range(2):
                    bank = pc0 if ch2 == 0 else pc1
                    if i >= 1:
                        e.wait_ge(s_ye.h, ye0 + i)
                    for k1 in range(8):
                        ins = e.matmul(bank[:],
                                       ao_fm[:, k1 * T + it * 128:k1 * T + (it + 1) * 128],
                                       wp_sb[:, k1 * 1024 + ch2 * 512:k1 * 1024 + (ch2 + 1) * 512],
                                       start=(k1 == 0), stop=(k1 == 7))
                    ins.then_inc(s_y.h, 1)
                    s_y.inc()
        block.tensor(pe_seq)

        def act_seq(e, sq=sq):
            for it in range(TPS):
                slot = it % 2
                for sl4 in range(4):
                    kq = (sq * TPS + it) * 4 + sl4
                    bank = pc0 if kq % 2 == 0 else pc1
                    e.wait_ge(s_qkv.h, qkv0 + kq + 1)
                    if it >= 2 and sl4 == 0:
                        e.wait_ge(s_cx.h, cx0 + sq * TPS + it - 1)
                    fn = AF.Relu if sl4 < 2 else AF.Identity
                    e.activation(kv_tm[:, slot * 2048 + sl4 * 512:slot * 2048 + (sl4 + 1) * 512],
                                 bank[:], fn).then_inc(s_qke.h, 1)
                    s_qke.inc()
            for h in range(H):
                hh = sq * H + h
                for sl4 in range(4):
                    gq = hh * 4 + sl4
                    bank = pc0 if gq % 2 == 0 else pc1
                    e.wait_ge(s_qp.h, qp0 + gq + 1)
                    if sl4 == 0 and hh >= 1:
                        e.wait_ge(s_qz.h, qz0 + hh * 4)
                    e.activation(qh[0:64, sl4 * 512:(sl4 + 1) * 512], bank[0:64, :],
                                 AF.Relu).then_inc(s_qpe.h, 1)
                    s_qpe.inc()
        block.scalar(act_seq)

        def dve_seq(e, sq=sq):
            if sq == 0:
                e.memset(zr_sb[:], 0.0)
                e.memset(zf_sb[:], 0.0)
                e.memset(zf2_sb[:], 0.0)
            for it in range(TPS):
                e.wait_ge(s_cx.h, cx0 + sq * TPS + it + 1)
                settle(e)
                if it == 0:
                    e.tensor_copy(cx_acc[0:64, 0:1024], pctx[0:64, 0:1024])
                    e.tensor_copy(cx_acc[0:64, 1024:1040],
                                  pz[0:64, 0:16]).then_inc(s_cxe.h, 1)
                else:
                    e.tensor_tensor(cx_acc[0:64, 0:1024], cx_acc[0:64, 0:1024],
                                    pctx[0:64, 0:1024], ALU.add)
                    e.tensor_tensor(cx_acc[0:64, 1024:1040], cx_acc[0:64, 1024:1040],
                                    pz[0:64, 0:16], ALU.add).then_inc(s_cxe.h, 1)
                s_cxe.inc()
            e.wait_ge(s_cxe.h, s_cxe.n)
            settle(e)
            e.tensor_copy(kc_sb[0:64, :], cx_acc[0:64, :]).then_inc(s_cxe.h, 1)
            s_cxe.inc()
            for h in range(H):
                hh = sq * H + h
                for sl4 in range(4):
                    g = hh * 4 + sl4
                    e.wait_ge(s_zp.h, zp0 + g + 1)
                    settle(e)
                    e.tensor_scalar_add(zf_sb[0:1, :], pz[0:1, 0:512],
                                        EPSL).then_inc(s_ze.h, 1)
                    s_ze.inc()
                    e.wait_ge(s_ze.h, s_ze.n)
                    settle(e)
                    e.reciprocal(zf2_sb[0:1, :], zf_sb[0:1, :]).then_inc(s_ze.h, 1)
                    s_ze.inc()
                    e.wait_ge(s_ze.h, s_ze.n)
                    e.wait_ge(s_zb.h, zb0 + g)
                    settle(e)
                    e.tensor_copy(zr_sb[0:1, :], zf2_sb[0:1, :]).then_inc(s_ze.h, 1)
                    s_ze.inc()
                    e.wait_ge(s_zb.h, zb0 + g + 1)
                    settle(e)
                    qsl = qh[0:64, sl4 * 512:(sl4 + 1) * 512]
                    e.tensor_tensor(qz_sb[0:64, (g % 2) * 512:(g % 2 + 1) * 512],
                                    qsl, pz[0:64, 0:512], ALU.mult).then_inc(s_qz.h, 1)
                    s_qz.inc()
                    e.wait_ge(s_ov.h, ov0 + g + 1)
                    settle(e)
                    ch_h, off_h = h // 2, (h % 2) * 64
                    bank = pc0 if g % 2 == 0 else pc1
                    e.tensor_copy(ao_fm[off_h:off_h + 64,
                                        ch_h * T + sl4 * 512:ch_h * T + (sl4 + 1) * 512],
                                  bank[0:64, :]).then_inc(s_ae.h, 1)
                    s_ae.inc()
            for it in range(TPS):
                i = sq * TPS + it
                e.wait_ge(s_y.h, y0 + 2 * (i + 1))
                il = i % TPS
                e.wait_ge(s_xr2[il % 4].h,
                          xr20[il % 4] + sq * 16 * (TPS // 4) + 16 * (il // 4 + 1))
                settle(e, 256)
                if i >= 2:
                    e.wait_ge(s_st[i % 2].h, st0[i % 2] + 16 * (i // 2))
                sl = yt[:, (i % 2) * C:(i % 2 + 1) * C]
                e.tensor_tensor(sl[:, 0:512], pc0[:], xr2[:, (i % 4) * C:(i % 4) * C + 512],
                                ALU.add)
                e.tensor_tensor(sl[:, 512:1024], pc1[:],
                                xr2[:, (i % 4) * C + 512:(i % 4 + 1) * C],
                                ALU.add).then_inc(s_t1.h, 1)
                s_t1.inc()
                e.wait_ge(s_w.h, w_done)
                e.wait_ge(s_t1.h, s_t1.n)
                settle(e, 256)
                e.tensor_tensor(sl[:], sl[:], br_sb[:], ALU.add).then_inc(s_ye.h, 1)
                s_ye.inc()
        block.vector(dve_seq)

        def xr2_loads(e, sq=sq, xr2=xr_buf[0], lnnr_done=lnnr_dones[-1]):
            e.wait_ge(s_lnnr.h, lnnr_done)
            for il in range(TPS):
                i = sq * TPS + il
                if i >= 4:
                    e.wait_ge(s_ye.h, ye0 + i - 3)
                sl = s_xr2[il % 4]
                e.dma_start(out=xr2[:, (il % 4) * C:(il % 4 + 1) * C],
                            in_=x_hbm[i * 128:(i + 1) * 128, :]).then_inc(sl.h, 16)
                sl.inc(16)
        block.sync(xr2_loads)

    def stores(e):
        for i in range(NT):
            e.wait_ge(s_ye.h, ye0 + i + 1)
            sl = s_st[i % 2]
            e.dma_start(out=xo_hbm[i * 128:(i + 1) * 128, :],
                        in_=yt[:, (i % 2) * C:(i % 2 + 1) * C]).then_inc(sl.h, 16)
            sl.inc(16)
        e.wait_ge(s_st[0].h, s_st[0].n)
        e.wait_ge(s_st[1].h, s_st[1].n)
    block.gpsimd(stores)


def a0_host_weights(z):
    wqkvT = np.ascontiguousarray(z["L0_attn_w_qkv"].T)
    wprojT = np.ascontiguousarray(z["L0_attn_w_proj"].T)
    brep = np.ascontiguousarray(np.broadcast_to(z["L0_attn_b_proj"], (128, 1024)))
    return dict(wqkvT=wqkvT, wprojT=wprojT, brep=brep)


def build_A0_program(z):
    b = Bld("a0")
    hw = a0_host_weights(z)
    x_in = b.inp("x_in", np.zeros((NTOK, C), np.float32))
    W = {k: b.inp("a0_" + k, v) for k, v in hw.items()}
    xo = b.out("x_out", (NTOK, C), F32)
    with b:
        emit_A0(b, b.block, x_in, xo, W)
        b.finish()
    return b, hw


def emit_F2(b, block, x_hbm, xo_hbm, W, scale=0.5):
    nc = b.nc
    E = 16
    NG = 4
    TPG = NTOK // NG
    SPG = TPG // 512
    TLG = TPG // 128
    GBW = 768
    trA = b.ps("f2_trA", [128, 1024], BF16)
    trB = b.ps("f2_trB", [128, 1024], BF16)
    pc = [b.ps(f"f2_pc{k}", [128, 512], F32) for k in range(6)]
    n_fm = b.sb("f2_nfm", [128, 8 * TPG], BF16)
    gw_sb = b.sb("f2_gw", [128, 8 * 16], BF16)
    gb = b.sb("f2_gb", [128, GBW], F32)
    SC, A_, BB, WW, R5 = 0, 128, 256, 384, 512
    G_, GS, S8a, S8b = 640, 672, 704, 712
    wfm = b.sb("f2_wfm", [16, TPG], BF16)
    ewu = b.sb("f2_ewu", [128, 24 * 1024], BF16)
    hb_sb = b.sb("f2_hb", [128, 4 * TPG], BF16)
    hs_sb = b.sb("f2_hs", [128, 8 * TPG], BF16)
    sg_sb = b.sb("f2_sg", [128, 2 * 512], BF16)
    su_sb = b.sb("f2_su", [128, 2 * 512], BF16)
    dn_sb = b.sb("f2_dn", [128, 4 * 512], F32)
    wrep = b.sb("f2_wrep", [128, TPG], F32)
    mtmp = b.sb("f2_mtmp", [128, 2 * 512], F32)
    macc = b.sb("f2_macc", [128, 8 * TPG], F32)
    yt = b.sb("f2_yt", [128, 2 * C], F32)
    bc_sb = b.sb("f2_bc", [16, 2048], BF16)

    sems = {}
    for nm in ("w ew gl gev wt wfm wrp wre gu sg su h dn dne mm ma tr ye").split():
        sems[nm] = b.sem("f2_" + nm)
    s_w, s_ew, s_gl, s_gev = sems["w"], sems["ew"], sems["gl"], sems["gev"]
    s_wt, s_wfm, s_wrp, s_wre = sems["wt"], sems["wfm"], sems["wrp"], sems["wre"]
    s_gu, s_sg, s_su, s_h = sems["gu"], sems["sg"], sems["su"], sems["h"]
    s_dn, s_dne, s_mm, s_ma = sems["dn"], sems["dne"], sems["mm"], sems["ma"]
    s_tr, s_ye = sems["tr"], sems["ye"]
    s_st = [b.sem(f"f2_st{k}") for k in range(2)]
    s_xr2 = [b.sem(f"f2_xr2{k}") for k in range(4)]

    def wloads(e):
        e.dma_start(out=gw_sb[:], in_=AP(W["gwT"].tensor, 0,
                    [[16, 128], [2048, 8], [1, 16]])).then_inc(s_w.h, 16)
        e.dma_start(out=bc_sb[:], in_=W["bc16"]).then_inc(s_w.h, 16)
        s_w.inc(32)
    block.gpsimd(wloads)
    w_done = s_w.n

    B0 = {nm: sems[nm].n for nm in sems}
    st0 = [x.n for x in s_st]
    xr20 = [x.n for x in s_xr2]

    def gu_i(grp, e_, k):
        return grp * (16 + E * 8) + (k if e_ < 0 else 16 + e_ * 8 + k)

    def dn_i(grp, e_, k):
        return grp * (16 + E * 16) + (k if e_ < 0 else 16 + e_ * 16 + k)

    def mm_i(grp, e_, k):
        return grp * (E * 16) + e_ * 16 + k

    def wr_i(grp, e_, ts):
        return grp * (E * SPG) + e_ * SPG + ts

    GU_G = 16 + E * 8
    DN_G = 16 + E * 16

    def v16(col):
        return AP(gb, col, [[GBW, 128], [16, 8], [1, 16]])

    def v4g(col):
        return AP(gb, col, [[GBW, 128], [16, 8], [4, 4], [1, 4]])

    def vG(col):
        return AP(gb, col, [[GBW, 128], [4, 8], [1, 4]])

    def rep_ig_to_ige(e, dstc, srcc):
        src = AP(gb, srcc, [[GBW, 128], [4, 8], [1, 4], [0, 4]])
        return e.tensor_copy(v4g(dstc), src)

    def rep_i_to_ig(e, dstc, srcc):
        src = AP(gb, srcc, [[GBW, 128], [1, 8], [0, 4]])
        return e.tensor_copy(vG(dstc), src)

    def rep_i_to_ie(e, dstc, srcc):
        src = AP(gb, srcc, [[GBW, 128], [1, 8], [0, 16]])
        return e.tensor_copy(v16(dstc), src)

    def ew_load(e, idx):
        grp, j = divmod(idx, 1 + E)
        if idx > 0:
            g2, j2 = divmod(idx - 1, 1 + E)
            e.wait_ge(s_dn.h, B0["dn"] + dn_i(g2, j2 - 1, 15) + 1)
            e.wait_ge(s_ew.h, B0["ew"] + 32 * idx)
        if j == 0:
            e.dma_start(out=ewu[:, 0:16 * 1024], in_=AP(W["w13sT"].tensor, 0,
                        [[2048, 128], [262144, 8], [1, 2048]])).then_inc(s_ew.h, 16)
            e.dma_start(out=ewu[:, 16 * 1024:24 * 1024], in_=AP(W["w2sT"].tensor, 0,
                        [[1024, 128], [131072, 8], [1, 1024]])).then_inc(s_ew.h, 16)
        else:
            eidx = j - 1
            e.dma_start(out=ewu[:, 0:8192], in_=AP(W["w13eT"].tensor, eidx * 1048576,
                        [[1024, 128], [131072, 8], [1, 1024]])).then_inc(s_ew.h, 16)
            e.dma_start(out=ewu[:, 8192:12288], in_=AP(W["w2eT"].tensor, eidx * 524288,
                        [[1024, 128], [131072, 4], [1, 1024]])).then_inc(s_ew.h, 16)
        s_ew.inc(32)

    lnnr_prev = [0]
    ye0 = B0["ye"]
    for grp in range(NG):
        pre = [(s_ye, ye0 + grp * TLG)] if grp else []
        s_nfm, xr2, s_lnnr = emit_ln_fm(b, block, x_hbm, n_fm, "f2", trA, trB,
                                        t0=grp * TLG, nt=TLG, pre_waits=pre)
        nfm_done = s_nfm.n
        lnnr_done = s_lnnr.n

        def xr2_loads(e, grp=grp, lnnr_done=lnnr_done):
            e.wait_ge(s_lnnr.h, lnnr_done)
            for il in range(TLG):
                i = grp * TLG + il
                if i >= 4:
                    e.wait_ge(s_ye.h, ye0 + i - 3)
                sl = s_xr2[il % 4]
                e.dma_start(out=xr2[:, (il % 4) * C:(il % 4 + 1) * C],
                            in_=x_hbm[i * 128:(i + 1) * 128, :]).then_inc(sl.h, 16)
                sl.inc(16)
        block.sync(xr2_loads)

        def pool_loads(e, grp=grp):
            if grp == 0:
                e.wait_ge(s_w.h, w_done)
            ew_load(e, grp * (1 + E))
        block.gpsimd(pool_loads)

        def pe_glog(e, grp=grp, nfm_done=nfm_done):
            e.wait_ge(s_nfm.h, nfm_done)
            e.wait_ge(s_w.h, w_done)
            e.wait_ge(b.s_cstg.h, b.s_cstg.n)
            if grp >= 1:
                e.wait_ge(s_sg.h, B0["sg"] + grp * GU_G)
                e.wait_ge(s_su.h, B0["su"] + grp * GU_G)
            for il in range(TLG):
                for ch in range(8):
                    ins = e.matmul(pc[0][:, il * 16:(il + 1) * 16],
                                   n_fm[:, ch * TPG + il * 128:ch * TPG + (il + 1) * 128],
                                   gw_sb[:, ch * 16:(ch + 1) * 16],
                                   start=(ch == 0), stop=(ch == 7))
            ins.then_inc(s_gl.h, 1)
            s_gl.inc()
        block.tensor(pe_glog)

        def dve_gev(e, grp=grp):
            e.wait_ge(s_gl.h, B0["gl"] + grp + 1)
            settle(e)
            e.tensor_copy(gb[:, SC:SC + 128], pc[0][:, 0:128]).then_inc(s_gev.h, 1)
            s_gev.inc()
        block.vector(dve_gev)

        g = Stepper(b, block, "f2g", extra_waits=[(s_gev, s_gev.n)])
        g.step("scalar", lambda e: e.activation(gb[:, SC:SC + 128], gb[:, SC:SC + 128],
               AF.Sigmoid))
        g.step("vector", lambda e: e.tensor_reduce(gb[:, G_:G_ + 32], v4g(SC),
               axis=AX.X, op=ALU.max))
        g.step("vector", lambda e: rep_ig_to_ige(e, R5, G_))
        g.step("vector", lambda e: e.tensor_tensor(gb[:, A_:A_ + 128], gb[:, SC:SC + 128],
               gb[:, R5:R5 + 128], ALU.is_lt))
        g.step("vector", lambda e: e.tensor_tensor(gb[:, A_:A_ + 128], gb[:, A_:A_ + 128],
               gb[:, SC:SC + 128], ALU.mult))
        g.step("vector", lambda e: e.tensor_reduce(gb[:, GS:GS + 32], v4g(A_),
               axis=AX.X, op=ALU.max))
        g.step("vector", lambda e: e.tensor_tensor(gb[:, G_:G_ + 32], gb[:, G_:G_ + 32],
               gb[:, GS:GS + 32], ALU.add))
        g.step("vector", lambda e: e.tensor_reduce(gb[:, S8a:S8a + 8], vG(G_),
               axis=AX.X, op=ALU.max))
        g.step("vector", lambda e: rep_i_to_ig(e, GS, S8a))
        g.step("vector", lambda e: e.tensor_tensor(gb[:, BB:BB + 32], gb[:, G_:G_ + 32],
               gb[:, GS:GS + 32], ALU.is_lt))
        g.step("vector", lambda e: e.tensor_tensor(gb[:, BB:BB + 32], gb[:, BB:BB + 32],
               gb[:, G_:G_ + 32], ALU.mult))
        g.step("vector", lambda e: e.tensor_reduce(gb[:, S8b:S8b + 8], vG(BB),
               axis=AX.X, op=ALU.max))
        g.step("vector", lambda e: rep_i_to_ig(e, GS, S8b))
        g.step("vector", lambda e: e.tensor_tensor(gb[:, GS:GS + 32], gb[:, G_:G_ + 32],
               gb[:, GS:GS + 32], ALU.is_ge))
        g.step("vector", lambda e: rep_ig_to_ige(e, R5, GS))
        g.step("vector", lambda e: e.tensor_tensor(gb[:, WW:WW + 128], gb[:, SC:SC + 128],
               gb[:, R5:R5 + 128], ALU.mult))
        g.step("vector", lambda e: e.tensor_copy(gb[:, A_:A_ + 128], gb[:, WW:WW + 128]))
        for _ in range(3):
            g.step("vector", lambda e: e.tensor_reduce(gb[:, S8a:S8a + 8], v16(A_),
                   axis=AX.X, op=ALU.max))
            g.step("vector", lambda e: rep_i_to_ie(e, BB, S8a))
            g.step("vector", lambda e: e.tensor_tensor(gb[:, BB:BB + 128],
                   gb[:, A_:A_ + 128], gb[:, BB:BB + 128], ALU.is_lt))
            g.step("vector", lambda e: e.tensor_tensor(gb[:, A_:A_ + 128],
                   gb[:, A_:A_ + 128], gb[:, BB:BB + 128], ALU.mult))
        g.step("vector", lambda e: e.tensor_reduce(gb[:, S8a:S8a + 8], v16(A_),
               axis=AX.X, op=ALU.max))
        g.step("vector", lambda e: rep_i_to_ie(e, BB, S8a))
        g.step("vector", lambda e: e.tensor_tensor(gb[:, BB:BB + 128], gb[:, WW:WW + 128],
               gb[:, BB:BB + 128], ALU.is_ge))
        g.step("vector", lambda e: e.tensor_tensor(gb[:, WW:WW + 128], gb[:, WW:WW + 128],
               gb[:, BB:BB + 128], ALU.mult))
        g.step("vector", lambda e: e.tensor_reduce(gb[:, S8a:S8a + 8], v16(WW),
               axis=AX.X, op=ALU.add))
        g.step("vector", lambda e: e.tensor_scalar_add(gb[:, S8a:S8a + 8],
               gb[:, S8a:S8a + 8], 1e-6))
        g.step("vector", lambda e: e.reciprocal(gb[:, S8b:S8b + 8], gb[:, S8a:S8a + 8]))
        g.step("vector", lambda e: rep_i_to_ie(e, BB, S8b))
        gate_done = g.step("vector", lambda e: e.tensor_tensor(gb[:, WW:WW + 128],
                           gb[:, WW:WW + 128], gb[:, BB:BB + 128], ALU.mult))
        s_g = g.s

        def pe_wt(e, grp=grp, gate_done=gate_done):
            e.wait_ge(s_g.h, gate_done)
            if grp >= 1:
                e.wait_ge(s_wfm.h, B0["wfm"] + 2 * grp)
                e.wait_ge(s_dne.h, B0["dne"] + grp * DN_G)
            for q in range(2):
                for j in range(4):
                    i = q * 4 + j
                    ins = e.transpose(pc[2][0:16, j * 128:(j + 1) * 128],
                                      gb[:, WW + i * 16:WW + (i + 1) * 16], b.idf[:])
                ins.then_inc(s_wt.h, 1)
                s_wt.inc()
                if q == 0:
                    e.wait_ge(s_wfm.h, B0["wfm"] + 2 * grp + 1)
        block.tensor(pe_wt)

        def dve_wfm(e, grp=grp):
            for q in range(2):
                e.wait_ge(s_wt.h, B0["wt"] + 2 * grp + q + 1)
                settle(e)
                e.tensor_copy(wfm[:, q * 512:(q + 1) * 512],
                              pc[2][0:16, 0:512]).then_inc(s_wfm.h, 1)
                s_wfm.inc()
        block.vector(dve_wfm)

        def pe_main(e, grp=grp):
            e.wait_ge(s_wfm.h, B0["wfm"] + 2 * (grp + 1))
            settle(e)
            if grp >= 1:
                e.wait_ge(s_ye.h, ye0 + grp * TLG)

            def gu_pair(stride, goff, uoff, gi, ts):
                if gi >= 1:
                    e.wait_ge(s_sg.h, B0["sg"] + gi)
                    e.wait_ge(s_su.h, B0["su"] + gi)
                for ci in range(8):
                    e.matmul(pc[0][:], ewu[:, ci * stride + goff:ci * stride + goff + 128],
                             n_fm[:, ci * TPG + ts * 512:ci * TPG + (ts + 1) * 512],
                             start=(ci == 0), stop=(ci == 7))
                for ci in range(8):
                    ins = e.matmul(pc[1][:],
                                   ewu[:, ci * stride + uoff:ci * stride + uoff + 128],
                                   n_fm[:, ci * TPG + ts * 512:ci * TPG + (ts + 1) * 512],
                                   start=(ci == 0), stop=(ci == 7))
                ins.then_inc(s_gu.h, 1)
                s_gu.inc()

            e.wait_ge(s_ew.h, B0["ew"] + 32 * (grp * (1 + E) + 1))
            for k in range(16):
                fsc, ts = divmod(k, SPG)
                gu_pair(2048, fsc * 128, 1024 + fsc * 128, gu_i(grp, -1, k), ts)
            e.wait_ge(s_h.h, B0["h"] + grp * GU_G + 16)
            settle(e)
            for k in range(16):
                hc, ts = divmod(k, SPG)
                di = dn_i(grp, -1, k)
                bank = pc[2 + k % 2]
                if di >= 2:
                    e.wait_ge(s_dne.h, B0["dne"] + di - 1)
                for ff in range(8):
                    ins = e.matmul(bank[:],
                                   ewu[:, 16384 + ff * 1024 + hc * 128:
                                       16384 + ff * 1024 + (hc + 1) * 128],
                                   hs_sb[:, ff * TPG + ts * 512:ff * TPG + (ts + 1) * 512],
                                   start=(ff == 0), stop=(ff == 7))
                ins.then_inc(s_dn.h, 1)
                s_dn.inc()
            for eidx in range(E):
                e.wait_ge(s_ew.h, B0["ew"] + 32 * (grp * (1 + E) + eidx + 2))
                for ts in range(SPG):
                    wi = wr_i(grp, eidx, ts)
                    bank = pc[4 + ts % 2]
                    if wi >= 2:
                        e.wait_ge(s_wre.h, B0["wre"] + wi - 1)
                    e.matmul(bank[:], bc_sb[:, eidx * 128:(eidx + 1) * 128],
                             wfm[0:16, ts * 512:(ts + 1) * 512],
                             start=True, stop=True).then_inc(s_wrp.h, 1)
                    s_wrp.inc()
                for k in range(8):
                    fc, ts = divmod(k, SPG)
                    gu_pair(1024, fc * 128, 512 + fc * 128, gu_i(grp, eidx, k), ts)
                e.wait_ge(s_h.h, B0["h"] + grp * GU_G + 16 + (eidx + 1) * 8)
                settle(e)
                for k in range(16):
                    hc, ts = divmod(k, SPG)
                    di = dn_i(grp, eidx, k)
                    bank = pc[2 + k % 2]
                    if di >= 2:
                        e.wait_ge(s_dne.h, B0["dne"] + di - 1)
                    for ff in range(4):
                        ins = e.matmul(bank[:],
                                       ewu[:, 8192 + ff * 1024 + hc * 128:
                                           8192 + ff * 1024 + (hc + 1) * 128],
                                       hb_sb[:, ff * TPG + ts * 512:ff * TPG + (ts + 1) * 512],
                                       start=(ff == 0), stop=(ff == 3))
                    ins.then_inc(s_dn.h, 1)
                    s_dn.inc()
            e.wait_ge(s_ma.h, B0["ma"] + (grp + 1) * DN_G)
            settle(e)
            for il in range(TLG):
                i = grp * TLG + il
                if i >= 1:
                    e.wait_ge(s_ye.h, ye0 + i)
                for hc in range(8):
                    bank = pc[4 + hc // 4]
                    ins = e.transpose(bank[:, (hc % 4) * 128:(hc % 4 + 1) * 128],
                                      macc[:, hc * TPG + il * 128:hc * TPG + (il + 1) * 128],
                                      b.idf[:])
                ins.then_inc(s_tr.h, 1)
                s_tr.inc()
        block.tensor(pe_main)

        def act_main(e, grp=grp):
            def sg_block(nk, e_):
                for k in range(nk):
                    gi = gu_i(grp, e_, k)
                    e.wait_ge(s_gu.h, B0["gu"] + gi + 1)
                    if gi >= 2:
                        e.wait_ge(s_h.h, B0["h"] + gi - 1)
                    e.activation(sg_sb[:, (gi % 2) * 512:(gi % 2 + 1) * 512],
                                 pc[0][:], AF.Silu).then_inc(s_sg.h, 1)
                    s_sg.inc()

            def dn_block(e_):
                e.wait_ge(s_ma.h, B0["ma"] + dn_i(grp, e_, 0))
                e.wait_ge(s_mm.h, B0["mm"] + (mm_i(grp, e_, 0) if e_ >= 0
                                              else grp * E * 16))
                for k in range(16):
                    di = dn_i(grp, e_, k)
                    e.wait_ge(s_dn.h, B0["dn"] + di + 1)
                    if k >= 4:
                        if e_ < 0:
                            e.wait_ge(s_ma.h, B0["ma"] + di - 3)
                        else:
                            e.wait_ge(s_mm.h, B0["mm"] + mm_i(grp, e_, k - 4) + 1)
                    e.activation(dn_sb[:, (di % 4) * 512:(di % 4 + 1) * 512],
                                 pc[2 + k % 2][:], AF.Identity).then_inc(s_dne.h, 1)
                    s_dne.inc()

            sg_block(16, -1)
            dn_block(-1)
            for eidx in range(E):
                sg_block(8, eidx)
                dn_block(eidx)
        block.scalar(act_main)

        def pool_main(e, grp=grp):
            km0 = grp * E * 16
            for k in range(16):
                gi = gu_i(grp, -1, k)
                e.wait_ge(s_sg.h, B0["sg"] + gi + 1)
                e.wait_ge(s_su.h, B0["su"] + gi + 1)
                settle(e, 256)
                fsc, ts = divmod(k, SPG)
                e.tensor_tensor(hs_sb[:, fsc * TPG + ts * 512:fsc * TPG + (ts + 1) * 512],
                                sg_sb[:, (gi % 2) * 512:(gi % 2 + 1) * 512],
                                su_sb[:, (gi % 2) * 512:(gi % 2 + 1) * 512],
                                ALU.mult).then_inc(s_h.h, 1)
                s_h.inc()
            ew_load(e, grp * (1 + E) + 1)
            for eidx in range(E):
                for k in range(8):
                    gi = gu_i(grp, eidx, k)
                    e.wait_ge(s_sg.h, B0["sg"] + gi + 1)
                    e.wait_ge(s_su.h, B0["su"] + gi + 1)
                    settle(e, 256)
                    fc, ts = divmod(k, SPG)
                    e.tensor_tensor(hb_sb[:, fc * TPG + ts * 512:fc * TPG + (ts + 1) * 512],
                                    sg_sb[:, (gi % 2) * 512:(gi % 2 + 1) * 512],
                                    su_sb[:, (gi % 2) * 512:(gi % 2 + 1) * 512],
                                    ALU.mult).then_inc(s_h.h, 1)
                    s_h.inc()
                e.wait_ge(s_wre.h, B0["wre"] + wr_i(grp, eidx, SPG - 1) + 1)
                settle(e)
                for k in range(16):
                    di = dn_i(grp, eidx, k)
                    km = km0 + eidx * 16 + k
                    e.wait_ge(s_dne.h, B0["dne"] + di + 1)
                    settle(e, 256)
                    if km >= 2:
                        km2 = km - 2
                        i2, k2 = divmod(km2, 16)
                        g2, e2 = divmod(i2, E)
                        e.wait_ge(s_ma.h, B0["ma"] + dn_i(g2, e2, k2) + 1)
                    hc, ts = divmod(k, SPG)
                    e.tensor_tensor(mtmp[:, (km % 2) * 512:(km % 2 + 1) * 512],
                                    dn_sb[:, (di % 4) * 512:(di % 4 + 1) * 512],
                                    wrep[:, ts * 512:(ts + 1) * 512],
                                    ALU.mult).then_inc(s_mm.h, 1)
                    s_mm.inc()
                if eidx + 2 <= E:
                    ew_load(e, grp * (1 + E) + eidx + 2)
            for il in range(TLG):
                i = grp * TLG + il
                e.wait_ge(s_ye.h, ye0 + i + 1)
                sl = s_st[i % 2]
                e.dma_start(out=xo_hbm[i * 128:(i + 1) * 128, :],
                            in_=yt[:, (i % 2) * C:(i % 2 + 1) * C]).then_inc(sl.h, 16)
                sl.inc(16)
            if grp == NG - 1:
                e.wait_ge(s_st[0].h, s_st[0].n)
                e.wait_ge(s_st[1].h, s_st[1].n)
        block.gpsimd(pool_main)

        def dve_main(e, grp=grp):
            def su_block(nk, e_):
                for k in range(nk):
                    gi = gu_i(grp, e_, k)
                    e.wait_ge(s_gu.h, B0["gu"] + gi + 1)
                    settle(e, 256)
                    if gi >= 2:
                        e.wait_ge(s_h.h, B0["h"] + gi - 1)
                    e.tensor_copy(su_sb[:, (gi % 2) * 512:(gi % 2 + 1) * 512],
                                  pc[1][:]).then_inc(s_su.h, 1)
                    s_su.inc()

            su_block(16, -1)
            if grp >= 1:
                e.wait_ge(s_tr.h, B0["tr"] + grp * TLG)
            for k in range(16):
                di = dn_i(grp, -1, k)
                e.wait_ge(s_dne.h, B0["dne"] + di + 1)
                settle(e, 256)
                hc, ts = divmod(k, SPG)
                e.tensor_copy(macc[:, hc * TPG + ts * 512:hc * TPG + (ts + 1) * 512],
                              dn_sb[:, (di % 4) * 512:(di % 4 + 1) * 512]).then_inc(s_ma.h, 1)
                s_ma.inc()
            for eidx in range(E):
                su_block(8, eidx)
                for ts in range(SPG):
                    wi = wr_i(grp, eidx, ts)
                    e.wait_ge(s_wrp.h, B0["wrp"] + wi + 1)
                    settle(e)
                    e.tensor_copy(wrep[:, ts * 512:(ts + 1) * 512],
                                  pc[4 + ts % 2][:]).then_inc(s_wre.h, 1)
                    s_wre.inc()
                for k in range(16):
                    mi = mm_i(grp, eidx, k)
                    e.wait_ge(s_mm.h, B0["mm"] + mi + 1)
                    settle(e, 256)
                    hc, ts = divmod(k, SPG)
                    dst = macc[:, hc * TPG + ts * 512:hc * TPG + (ts + 1) * 512]
                    e.tensor_tensor(dst, dst, mtmp[:, (mi % 2) * 512:(mi % 2 + 1) * 512],
                                    ALU.add).then_inc(s_ma.h, 1)
                    s_ma.inc()
            for il in range(TLG):
                i = grp * TLG + il
                e.wait_ge(s_tr.h, B0["tr"] + i + 1)
                e.wait_ge(s_xr2[il % 4].h, xr20[il % 4] + grp * 16 * (TLG // 4) +
                          16 * (il // 4 + 1))
                settle(e, 256)
                if i >= 2:
                    e.wait_ge(s_st[i % 2].h, st0[i % 2] + 16 * (i // 2))
                sl = yt[:, (i % 2) * C:(i % 2 + 1) * C]
                e.scalar_tensor_tensor(sl[:, 0:512], pc[4][:], scale,
                                       xr2[:, (il % 4) * C:(il % 4) * C + 512],
                                       ALU.mult, ALU.add)
                e.scalar_tensor_tensor(sl[:, 512:1024], pc[5][:], scale,
                                       xr2[:, (il % 4) * C + 512:(il % 4 + 1) * C],
                                       ALU.mult, ALU.add).then_inc(s_ye.h, 1)
                s_ye.inc()
        block.vector(dve_main)


def f2_host_weights(z):
    gw = z["L2_ffn_w_gate"]
    w1e, w3e, w2e = z["L2_ffn_w1_e"], z["L2_ffn_w3_e"], z["L2_ffn_w2_e"]
    w1s, w3s, w2s = z["L2_ffn_w1_s"], z["L2_ffn_w3_s"], z["L2_ffn_w2_s"]
    gwT = np.ascontiguousarray(gw.T)
    w13eT = np.ascontiguousarray(np.concatenate(
        [np.transpose(w1e, (0, 2, 1)), np.transpose(w3e, (0, 2, 1))], axis=2))
    w2eT = np.ascontiguousarray(np.transpose(w2e, (0, 2, 1)))
    w13sT = np.ascontiguousarray(np.concatenate([w1s.T, w3s.T], axis=1))
    w2sT = np.ascontiguousarray(w2s.T)
    bc16 = np.zeros((16, 2048), np.float32)
    for e in range(16):
        bc16[e, e * 128:(e + 1) * 128] = 1.0
    return dict(gwT=gwT, w13eT=w13eT, w2eT=w2eT, w13sT=w13sT, w2sT=w2sT, bc16=bc16)


def build_F2_program(z):
    b = Bld("f2")
    hw = f2_host_weights(z)
    x_in = b.inp("x_in", np.zeros((NTOK, C), np.float32))
    W = {k: b.inp("f2_" + k, v) for k, v in hw.items()}
    xo = b.out("x_out", (NTOK, C), F32)
    with b:
        emit_F2(b, b.block, x_in, xo, W, scale=0.5)
        b.finish()
    return b, hw


def emit_A1(b, block, x_hbm, xo_hbm, W):
    nc = b.nc
    H = 16
    trA = b.ps("a1_trA", [128, 1024], BF16)
    trB = b.ps("a1_trB", [128, 1024], BF16)
    pc = [b.ps(f"a1_pc{k}", [128, 512], F32) for k in range(6)]
    n_fm = b.sb("a1_nfm", [128, 8 * T], BF16)
    wq_sb = b.sb("a1_wq", [128, 8 * 1024], BF16)
    wk_sb = b.sb("a1_wk", [128, 8 * 1024], BF16)
    wv_sb = b.sb("a1_wv", [128, 8 * 1024], BF16)
    wc_sb = b.sb("a1_wc", [128, 8 * 1024], BF16)
    lt_sb = b.sb("a1_lt", [128, 16 * 64], F32)
    trm_sb = b.sb("a1_trm", [128, 128], F32)
    lm_sb = b.sb("a1_lm", [128, 16 * 64], BF16)
    bm_sb = b.sb("a1_bm", [128, 16 * 64], BF16)
    qf_sb = b.sb("a1_qf", [128, 1024], BF16)
    kf_sb = b.sb("a1_kf", [128, 1024], BF16)
    kt_sb = b.sb("a1_kt", [128, 1024], BF16)
    vt_sb = b.sb("a1_vt", [128, 1024], BF16)
    atm_sb = b.sb("a1_atm", [128, 128], BF16)
    sacc = b.sb("a1_sacc", [128, 16 * 64], F32)
    sb_sb = b.sb("a1_sb", [128, 64], BF16)
    yb_sb = b.sb("a1_yb", [128, 64], BF16)
    yfm_sb = b.sb("a1_yfm", [128, 128], BF16)
    ao_fm = b.sb("a1_ao", [128, 8 * T], BF16)
    yt = b.sb("a1_yt", [128, 2 * C], F32)

    s_w = b.sem("a1_w")
    s_pj = b.sem("a1_pj")
    s_ye = b.sem("a1_ye")
    s_st = [b.sem(f"a1_st{k}") for k in range(2)]
    s_xr2 = [b.sem(f"a1_xr2{k}") for k in range(4)]

    def wloads(e):
        for nm, dst in (("wqT", wq_sb), ("wkT", wk_sb), ("wvT", wv_sb), ("wcT", wc_sb)):
            e.dma_start(out=dst[:], in_=AP(W[nm].tensor, 0,
                        [[1024, 128], [131072, 8], [1, 1024]])).then_inc(s_w.h, 16)
            s_w.inc(16)
        e.dma_start(out=lt_sb[0:64, :], in_=AP(W["LtriT"].tensor, 0,
                    [[64, 64], [4096, 16], [1, 64]])).then_inc(s_w.h, 16)
        s_w.inc(16)
    block.gpsimd(wloads)

    def trmload(e):
        e.dma_start(out=trm_sb[:], in_=W["triu"]).then_inc(s_w.h, 16)
        s_w.inc(16)
    block.gpsimd(trmload)
    w_done = s_w.n

    bmc = Stepper(b, block, "a1bm", extra_waits=[(s_w, w_done)])

    def lmf(e):
        for h in range(H):
            ins = e.tensor_tensor(lm_sb[0:64, h * 64:(h + 1) * 64],
                                  lt_sb[0:64, h * 64:(h + 1) * 64],
                                  trm_sb[0:64, 0:64], ALU.mult)
        return ins
    bmc.step("vector", lmf)
    for q in range(2):
        def bmm(e, q=q):
            for hh in range(8):
                h = q * 8 + hh
                ins = e.matmul(pc[3][0:64, hh * 64:(hh + 1) * 64],
                               lm_sb[0:64, h * 64:(h + 1) * 64],
                               lm_sb[0:64, h * 64:(h + 1) * 64],
                               start=True, stop=True)
            return ins
        bmc.step("tensor", bmm)
        bmc.step("vector", lambda e, q=q: e.tensor_copy(
            bm_sb[0:64, q * 512:(q + 1) * 512], pc[3][0:64, 0:512]))
    bm_done = bmc.s.n
    s_bmc = bmc.s

    ye0 = s_ye.n
    pj0 = s_pj.n
    st0 = [x.n for x in s_st]
    xr20 = [x.n for x in s_xr2]

    for sq in range(SEQ_PER_CORE):
        pre = [(s_ye, ye0 + sq * TPS)] if sq else []
        s_nfm, xr2, s_lnnr = emit_ln_fm(b, block, x_hbm, n_fm, "a1", trA, trB,
                                        t0=sq * TPS, nt=TPS, pre_waits=pre)
        nfm_done = s_nfm.n
        lnnr_done = s_lnnr.n

        def xr2_loads(e, sq=sq, lnnr_done=lnnr_done):
            e.wait_ge(s_lnnr.h, lnnr_done)
            for il in range(TPS):
                i = sq * TPS + il
                if i >= 4:
                    e.wait_ge(s_ye.h, ye0 + i - 3)
                sl = s_xr2[il % 4]
                e.dma_start(out=xr2[:, (il % 4) * C:(il % 4 + 1) * C],
                            in_=x_hbm[i * 128:(i + 1) * 128, :]).then_inc(sl.h, 16)
                sl.inc(16)
        block.sync(xr2_loads)

        ch_ = Stepper(b, block, "a1m",
                      extra_waits=[(s_nfm, nfm_done), (s_w, w_done),
                                   (s_bmc, bm_done),
                                   (b.s_cstg, b.s_cstg.n)] +
                                  ([(s_ye, ye0 + sq * TPS)] if sq else []))

        for it in range(TPS):
            def mm_tm(e, wsb=None, it=it):
                for sl2 in range(2):
                    bank = pc[4 + sl2]
                    for ci in range(8):
                        ins = e.matmul(bank[:],
                                       n_fm[:, ci * T + it * 128:ci * T + (it + 1) * 128],
                                       wsb[:, ci * 1024 + sl2 * 512:ci * 1024 + (sl2 + 1) * 512],
                                       start=(ci == 0), stop=(ci == 7))
                return ins

            def mm_fm(e, wsb=None, it=it):
                for ch in range(8):
                    bank = pc[4 + ch // 4]
                    for ci in range(8):
                        ins = e.matmul(bank[:, (ch % 4) * 128:(ch % 4 + 1) * 128],
                                       wsb[:, ci * 1024 + ch * 128:ci * 1024 + (ch + 1) * 128],
                                       n_fm[:, ci * T + it * 128:ci * T + (it + 1) * 128],
                                       start=(ci == 0), stop=(ci == 7))
                return ins

            def ev2(e, dst=None):
                e.tensor_copy(dst[:, 0:512], pc[4][:])
                return e.tensor_copy(dst[:, 512:1024], pc[5][:])

            ch_.step("tensor", lambda e, f=mm_tm: f(e, wsb=wk_sb))
            ch_.step("vector", lambda e: ev2(e, dst=kt_sb))
            ch_.step("tensor", lambda e, f=mm_tm: f(e, wsb=wv_sb))
            ch_.step("vector", lambda e: ev2(e, dst=vt_sb))
            ch_.step("tensor", lambda e, f=mm_fm: f(e, wsb=wq_sb))
            ch_.step("vector", lambda e: ev2(e, dst=qf_sb))
            ch_.step("tensor", lambda e, f=mm_fm: f(e, wsb=wk_sb))
            ch_.step("vector", lambda e: ev2(e, dst=kf_sb))

            for h in range(H):
                po, co = (h % 2) * 64, (h // 2) * 128

                def at_mm(e, h=h, po=po, co=co):
                    return e.matmul(pc[0][:, 0:128], kf_sb[po:po + 64, co:co + 128],
                                    qf_sb[po:po + 64, co:co + 128],
                                    start=True, stop=True)
                ch_.step("tensor", at_mm)
                ch_.step("vector", lambda e: e.tensor_tensor(
                    atm_sb[:], pc[0][:, 0:128], trm_sb[:], ALU.mult))
                if it > 0:
                    ch_.step("vector", lambda e, h=h, po=po: e.tensor_copy(
                        sb_sb[po:po + 64, :], sacc[po:po + 64, h * 64:(h + 1) * 64]))

                def y_mm(e, h=h, po=po, co=co, it=it):
                    if it == 0:
                        return e.matmul(pc[1][:, 0:64], atm_sb[:],
                                        vt_sb[:, h * 64:(h + 1) * 64],
                                        start=True, stop=True)
                    e.matmul(pc[1][:, 0:64], atm_sb[:],
                             vt_sb[:, h * 64:(h + 1) * 64], start=True, stop=False)
                    return e.matmul(pc[1][:, 0:64], qf_sb[po:po + 64, co:co + 128],
                                    sb_sb[po:po + 64, :], start=False, stop=True)
                ch_.step("tensor", y_mm)
                ch_.step("vector", lambda e: e.tensor_copy(yb_sb[:, 0:64],
                                                           pc[1][:, 0:64]))

                def sd_mm(e, h=h, po=po):
                    return e.matmul(pc[3][po:po + 64, 0:64],
                                    kt_sb[:, h * 64:(h + 1) * 64],
                                    vt_sb[:, h * 64:(h + 1) * 64],
                                    start=True, stop=True,
                                    tile_position=(0, po))
                ch_.step("tensor", sd_mm)
                if it == 0:
                    ch_.step("vector", lambda e, h=h, po=po: e.tensor_copy(
                        sacc[po:po + 64, h * 64:(h + 1) * 64], pc[3][po:po + 64, 0:64]))
                else:
                    ch_.step("vector", lambda e, h=h, po=po: e.tensor_tensor(
                        sacc[po:po + 64, h * 64:(h + 1) * 64],
                        sacc[po:po + 64, h * 64:(h + 1) * 64],
                        pc[3][po:po + 64, 0:64], ALU.add))
                ch_.step("tensor", lambda e: e.transpose(trA[0:64, 0:128],
                         yb_sb[:, 0:64], b.idb[:]))
                ch_.step("vector", lambda e: e.tensor_copy(yfm_sb[0:64, 0:128],
                                                           trA[0:64, 0:128]))

                def y2_mm(e, h=h):
                    return e.matmul(pc[2][0:64, 0:128], bm_sb[0:64, h * 64:(h + 1) * 64],
                                    yfm_sb[0:64, 0:128], start=True, stop=True)
                ch_.step("tensor", y2_mm)
                ch_.step("vector", lambda e, h=h, po=po, it=it: e.tensor_copy(
                    ao_fm[po:po + 64, (h // 2) * T + it * 128:(h // 2) * T + (it + 1) * 128],
                    pc[2][0:64, 0:128]))

        chain_done = ch_.s.n
        s_chm = ch_.s

        def pe_proj(e, sq=sq, chain_done=chain_done):
            e.wait_ge(s_chm.h, chain_done)
            settle(e)
            for it in range(TPS):
                i = sq * TPS + it
                for ch2 in range(2):
                    bank = pc[4 + ch2]
                    if i >= 1:
                        e.wait_ge(s_ye.h, ye0 + i)
                    for k1 in range(8):
                        ins = e.matmul(bank[:],
                                       ao_fm[:, k1 * T + it * 128:k1 * T + (it + 1) * 128],
                                       wc_sb[:, k1 * 1024 + ch2 * 512:k1 * 1024 + (ch2 + 1) * 512],
                                       start=(k1 == 0), stop=(k1 == 7))
                    ins.then_inc(s_pj.h, 1)
                    s_pj.inc()
        block.tensor(pe_proj)

        def dve_ye(e, sq=sq):
            for it in range(TPS):
                i = sq * TPS + it
                e.wait_ge(s_pj.h, pj0 + 2 * (i + 1))
                il = it
                e.wait_ge(s_xr2[il % 4].h, xr20[il % 4] + sq * 16 * (TPS // 4) +
                          16 * (il // 4 + 1))
                settle(e, 256)
                if i >= 2:
                    e.wait_ge(s_st[i % 2].h, st0[i % 2] + 16 * (i // 2))
                sl = yt[:, (i % 2) * C:(i % 2 + 1) * C]
                e.tensor_tensor(sl[:, 0:512], pc[4][:],
                                xr2[:, (il % 4) * C:(il % 4) * C + 512], ALU.add)
                e.tensor_tensor(sl[:, 512:1024], pc[5][:],
                                xr2[:, (il % 4) * C + 512:(il % 4 + 1) * C],
                                ALU.add).then_inc(s_ye.h, 1)
                s_ye.inc()
        block.vector(dve_ye)

    def stores(e):
        for i in range(NT):
            e.wait_ge(s_ye.h, ye0 + i + 1)
            sl = s_st[i % 2]
            e.dma_start(out=xo_hbm[i * 128:(i + 1) * 128, :],
                        in_=yt[:, (i % 2) * C:(i % 2 + 1) * C]).then_inc(sl.h, 16)
            sl.inc(16)
        e.wait_ge(s_st[0].h, s_st[0].n)
        e.wait_ge(s_st[1].h, s_st[1].n)
    block.gpsimd(stores)


def a1_host_weights(z):
    wqT = np.ascontiguousarray(z["L1_attn_Wq"].T)
    wkT = np.ascontiguousarray(z["L1_attn_Wk"].T)
    wvT = np.ascontiguousarray(z["L1_attn_Wv"].T)
    wcT = np.ascontiguousarray(z["L1_attn_Wc"].T)
    LtriT = np.ascontiguousarray(np.transpose(z["L1_attn_Ltri"], (0, 2, 1)))
    triu = np.triu(np.ones((128, 128), np.float32))
    return dict(wqT=wqT, wkT=wkT, wvT=wvT, wcT=wcT, LtriT=LtriT, triu=triu)


def build_A1_program(z):
    b = Bld("a1")
    hw = a1_host_weights(z)
    x_in = b.inp("x_in", np.zeros((NTOK, C), np.float32))
    W = {k: b.inp("a1_" + k, v) for k, v in hw.items()}
    xo = b.out("x_out", (NTOK, C), F32)
    with b:
        emit_A1(b, b.block, x_in, xo, W)
        b.finish()
    return b, hw


def emit_F3(b, block, x_hbm, xo_hbm, W, scale=0.5):
    nc = b.nc
    E = 16
    ATT_SCALE = 1.0 / math.sqrt(128.0)
    trA = b.ps("f3_trA", [128, 1024], BF16)
    trB = b.ps("f3_trB", [128, 1024], BF16)
    pc = [b.ps(f"f3_pc{k}", [128, 512], F32) for k in range(6)]
    n_fm = b.sb("f3_nfm", [128, 8 * NTOK], BF16)
    acc = b.sb("f3_acc", [128, 32 * 128], F32)
    qf = b.sb("f3_qf", [128, NTOK], BF16)
    kf = b.sb("f3_kf", [128, NTOK], BF16)
    vt = b.sb("f3_vt", [128, 32 * 128], BF16)
    ot = b.sb("f3_ot", [128, 32 * 128], BF16)
    gwk = b.sb("f3_gwk", [128, 2112], F32)
    RAW, GA, SCR, WSEL, M32a, M32b = 0, 512, 1024, 1536, 2048, 2080
    gb2 = b.sb("f3_gb2", [128, 512], F32)
    simb = b.sb("f3_simb", [128, 8 * 16], BF16)
    gat_sb = b.sb("f3_gat", [128, 16], F32)
    sq2 = b.sb("f3_sq2", [128, 512], BF16)
    cnb = b.sb("f3_cnb", [1, 16], BF16)
    rn_sb = b.sb("f3_rn", [1, 512], BF16)
    rniv = b.sb("f3_rniv", [128, 32], F32)
    crep = b.sb("f3_crep", [128, 16], F32)
    sig = b.sb("f3_sig", [128, 16], F32)
    pw = b.sb("f3_pw", [128, 2 * 1024], BF16)
    opw = b.sb("f3_opw", [128, 16 * 1024], BF16)
    sc_sb = acc
    pt_sb = b.sb("f3_pt", [128, 2048], BF16)
    ptT = b.sb("f3_ptT", [128, 128], BF16)
    ofm1 = b.sb("f3_ofm1", [128, 128], BF16)
    mcol = b.sb("f3_mcol", [128, 2], F32)
    oacc = b.sb("f3_oacc", [128, 1024], F32)
    yt = b.sb("f3_yt", [128, 2 * C], F32)

    sems = {}
    for nm in ("w qa qae att ate oy oye ye").split():
        sems[nm] = b.sem("f3_" + nm)
    s_pwl = [b.sem(f"f3_pwl{k}") for k in range(2)]
    s_w, s_qa, s_qae = sems["w"], sems["qa"], sems["qae"]
    s_att, s_ate, s_oy, s_oye, s_ye = (sems["att"], sems["ate"], sems["oy"],
                                       sems["oye"], sems["ye"])
    s_st = [b.sem(f"f3_st{k}") for k in range(2)]
    s_xr2 = [b.sem(f"f3_xr2{k}") for k in range(4)]

    def wloads(e):
        e.dma_start(out=simb[:], in_=AP(W["sim"].tensor, 0,
                    [[16, 128], [2048, 8], [1, 16]])).then_inc(s_w.h, 16)
        e.dma_start(out=gat_sb[:], in_=W["gatesb"]).then_inc(s_w.h, 16)
        e.dma_start(out=opw[:], in_=AP(W["opT"].tensor, 0,
                    [[1024, 128], [131072, 16], [1, 1024]])).then_inc(s_w.h, 16)
        s_w.inc(48)
    block.gpsimd(wloads)
    w_done = s_w.n

    s_nfm, xr2, s_lnnr = emit_ln_fm(b, block, x_hbm, n_fm, "f3", trA, trB)
    nfm_done = s_nfm.n
    lnnr_done = s_lnnr.n
    B0 = {nm: sems[nm].n for nm in sems}
    st0 = [x.n for x in s_st]
    xr20 = [x.n for x in s_xr2]
    ye0 = B0["ye"]

    def xr2_loads(e):
        e.wait_ge(s_lnnr.h, lnnr_done)
        for i in range(NT):
            if i >= 4:
                e.wait_ge(s_ye.h, ye0 + i - 3)
            sl = s_xr2[i % 4]
            e.dma_start(out=xr2[:, (i % 4) * C:(i % 4 + 1) * C],
                        in_=x_hbm[i * 128:(i + 1) * 128, :]).then_inc(sl.h, 16)
            sl.inc(16)
    block.sync(xr2_loads)

    g = Stepper(b, block, "f3n", extra_waits=[(s_nfm, nfm_done), (s_w, w_done),
                                              (b.s_cstg, b.s_cstg.n)])
    g.step("vector", lambda e: e.tensor_tensor(sq2[:, 0:128], simb[:], simb[:],
           ALU.mult))

    def cnmm(e):
        for ch in range(8):
            ins = e.matmul(pc[0][0:1, 0:16], b.cst_ones_b[:, 0:1],
                           sq2[:, ch * 16:(ch + 1) * 16],
                           start=(ch == 0), stop=(ch == 7))
        return ins
    g.step("tensor", cnmm)
    g.step("scalar", lambda e: e.activation(gwk[0:1, 0:16], pc[0][0:1, 0:16],
           AF.Sqrt))
    g.step("vector", lambda e: e.reciprocal(gwk[0:1, 16:32], gwk[0:1, 0:16]))
    g.step("vector", lambda e: e.tensor_copy(cnb[0:1, :], gwk[0:1, 16:32]))
    g.step("tensor", lambda e: e.matmul(pc[0][0:128, 0:16], b.cst_ones_b[0:1, 0:128],
           cnb[0:1, :], start=True, stop=True))
    g.step("vector", lambda e: e.tensor_copy(crep[:], pc[0][:, 0:16]))
    g.step("scalar", lambda e: e.activation(sig[:], gat_sb[:], AF.Sigmoid))
    for sl8 in range(8):
        for ch in range(8):
            g.step("vector", lambda e, sl8=sl8, ch=ch: e.tensor_tensor(sq2[:],
                   n_fm[:, ch * NTOK + sl8 * 512:ch * NTOK + (sl8 + 1) * 512],
                   n_fm[:, ch * NTOK + sl8 * 512:ch * NTOK + (sl8 + 1) * 512],
                   ALU.mult), settle_cycles=128)
            g.step("tensor", lambda e, ch=ch: e.matmul(pc[1][0:1, 0:512],
                   b.cst_ones_b[:, 0:1], sq2[:], start=(ch == 0), stop=(ch == 7)),
                   settle_cycles=128)
        g.step("scalar", lambda e: e.activation(gwk[0:1, 32:544], pc[1][0:1, 0:512],
               AF.Sqrt))
        g.step("vector", lambda e: e.reciprocal(gwk[0:1, 32:544], gwk[0:1, 32:544]))
        g.step("vector", lambda e: e.tensor_copy(rn_sb[0:1, :], gwk[0:1, 32:544]))

        def rtr(e, sl8=sl8):
            for j in range(4):
                ins = e.matmul(pc[2][:, j:j + 1], rn_sb[0:1, j * 128:(j + 1) * 128],
                               b.cst_ones_b[0:1, 0:1], start=True, stop=True)
            return ins
        g.step("tensor", rtr)
        g.step("vector", lambda e, sl8=sl8: e.tensor_copy(
            rniv[:, sl8 * 4:(sl8 + 1) * 4], pc[2][:, 0:4]))
    for i in range(NT):
        def rawmm(e, i=i):
            for ch in range(8):
                ins = e.matmul(pc[3][:, (i % 2) * 16:(i % 2) * 16 + 16],
                               n_fm[:, ch * NTOK + i * 128:ch * NTOK + (i + 1) * 128],
                               simb[:, ch * 16:(ch + 1) * 16],
                               start=(ch == 0), stop=(ch == 7))
            return ins
        g.step("tensor", rawmm, settle_cycles=128)
        g.step("vector", lambda e, i=i: e.scalar_tensor_tensor(
            gwk[:, RAW + i * 16:RAW + (i + 1) * 16],
            pc[3][:, (i % 2) * 16:(i % 2) * 16 + 16], rniv[:, i:i + 1],
            crep[:], ALU.mult, ALU.mult), settle_cycles=128)
    g.step("vector", lambda e: e.tensor_tensor(gwk[:, RAW:RAW + 512],
           gwk[:, RAW:RAW + 512],
           AP(sig, 0, [[16, 128], [0, 32], [1, 16]]), ALU.subtract))
    g.step("scalar", lambda e: e.activation(gwk[:, GA:GA + 512], gwk[:, RAW:RAW + 512],
           AF.Relu))
    g.step("vector", lambda e: e.tensor_scalar(gwk[:, SCR:SCR + 512],
           gwk[:, GA:GA + 512], 0.0, None, ALU.is_gt))

    def v16g(col):
        return AP(gwk, col, [[2112, 128], [16, 32], [1, 16]])

    def rep_ie(e, dstc, srcc):
        src = AP(gwk, srcc, [[2112, 128], [1, 32], [0, 16]])
        dst = AP(gb2, dstc, [[512, 128], [16, 32], [1, 16]])
        return e.tensor_copy(dst, src)

    g.step("vector", lambda e: e.tensor_reduce(gwk[:, M32a:M32a + 32], v16g(SCR),
           axis=AX.X, op=ALU.add))
    g.step("vector", lambda e: e.tensor_scalar(gwk[:, M32b:M32b + 32],
           gwk[:, M32a:M32a + 32], 0.5, None, ALU.is_lt))
    g.step("vector", lambda e: e.tensor_reduce(gwk[:, M32a:M32a + 32], v16g(RAW),
           axis=AX.X, op=ALU.max))
    g.step("vector", lambda e: rep_ie(e, 0, M32a))
    g.step("vector", lambda e: e.tensor_tensor(gwk[:, WSEL:WSEL + 512],
           gwk[:, RAW:RAW + 512], gb2[:], ALU.is_lt))
    g.step("vector", lambda e: e.tensor_scalar(gwk[:, WSEL:WSEL + 512],
           gwk[:, WSEL:WSEL + 512], -1.0, 1e32, ALU.add, ALU.mult))
    g.step("vector", lambda e: e.tensor_tensor(gwk[:, WSEL:WSEL + 512],
           gwk[:, RAW:RAW + 512], gwk[:, WSEL:WSEL + 512], ALU.add))
    g.step("vector", lambda e: e.tensor_reduce(gwk[:, M32a:M32a + 32], v16g(WSEL),
           axis=AX.X, op=ALU.max))
    g.step("vector", lambda e: rep_ie(e, 0, M32a))
    g.step("vector", lambda e: e.tensor_tensor(gwk[:, WSEL:WSEL + 512],
           gwk[:, RAW:RAW + 512], gb2[:], ALU.is_ge))
    g.step("vector", lambda e: rep_ie(e, 0, M32b))
    g.step("vector", lambda e: e.tensor_tensor(gwk[:, WSEL:WSEL + 512],
           gwk[:, WSEL:WSEL + 512], gb2[:], ALU.mult))
    g.step("vector", lambda e: e.tensor_tensor(gwk[:, WSEL:WSEL + 512],
           gwk[:, WSEL:WSEL + 512], gwk[:, SCR:SCR + 512], ALU.add))
    g.step("vector", lambda e: e.tensor_scalar(gwk[:, SCR:SCR + 512],
           gwk[:, WSEL:WSEL + 512], -1.0, -1e32, ALU.add, ALU.mult))
    g.step("vector", lambda e: e.tensor_tensor(gwk[:, SCR:SCR + 512],
           gwk[:, GA:GA + 512], gwk[:, SCR:SCR + 512], ALU.add))
    g.step("vector", lambda e: e.tensor_reduce(gwk[:, M32a:M32a + 32], v16g(SCR),
           axis=AX.X, op=ALU.max))
    g.step("vector", lambda e: rep_ie(e, 0, M32a))
    g.step("vector", lambda e: e.tensor_tensor(gwk[:, SCR:SCR + 512],
           gwk[:, SCR:SCR + 512], gb2[:], ALU.subtract))

    def expblk(e):
        for i in range(NT):
            ins = e.activation(gwk[:, SCR + i * 16:SCR + (i + 1) * 16],
                               gwk[:, SCR + i * 16:SCR + (i + 1) * 16], AF.Exp,
                               accum_out=gwk[:, M32a + i:M32a + i + 1])
        return ins
    g.step("scalar", expblk)
    g.step("vector", lambda e: e.reciprocal(gwk[:, M32b:M32b + 32],
           gwk[:, M32a:M32a + 32]))
    g.step("vector", lambda e: rep_ie(e, 0, M32b))
    gate_done = g.step("vector", lambda e: e.tensor_tensor(gwk[:, WSEL:WSEL + 512],
                       gwk[:, SCR:SCR + 512], gb2[:], ALU.mult))
    s_g = g.s

    qa0, qae0 = B0["qa"], B0["qae"]
    pwl0 = [x.n for x in s_pwl]

    def pool_pw(e):
        kk = 0
        for pj in range(3):
            for eidx in range(E):
                if kk >= 2:
                    e.wait_ge(s_qa.h, qa0 + (kk - 1) * NT)
                sl = s_pwl[kk % 2]
                e.dma_start(out=pw[:, (kk % 2) * 1024:(kk % 2 + 1) * 1024],
                            in_=AP(W["qkvpT"].tensor, (pj * E + eidx) * 131072,
                                   [[128, 128], [16384, 8], [1, 128]])).then_inc(sl.h, 16)
                sl.inc(16)
                kk += 1
    block.gpsimd(pool_pw)

    def pe_qkv(e):
        e.wait_ge(s_g.h, gate_done)
        settle(e)
        kk = 0
        for pj in range(3):
            for eidx in range(E):
                e.wait_ge(s_pwl[kk % 2].h, pwl0[kk % 2] + 16 * (kk // 2 + 1))
                for i in range(NT):
                    gi = kk * NT + i
                    bank = pc[4 + gi % 2]
                    if gi >= 2:
                        e.wait_ge(s_qae.h, qae0 + gi - 1 + pj)
                    for ch in range(8):
                        ins = e.matmul(bank[:, 0:128],
                                       n_fm[:, ch * NTOK + i * 128:ch * NTOK + (i + 1) * 128],
                                       pw[:, (kk % 2) * 1024 + ch * 128:
                                          (kk % 2) * 1024 + (ch + 1) * 128],
                                       start=(ch == 0), stop=(ch == 7))
                    ins.then_inc(s_qa.h, 1)
                    s_qa.inc()
                kk += 1
    block.tensor(pe_qkv)

    def dve_qkv(e):
        e.wait_ge(s_g.h, gate_done)
        settle(e)
        kk = 0
        for pj in range(3):
            e.wait_ge(s_qae.h, s_qae.n)
            settle(e)
            for eidx in range(E):
                for i in range(NT):
                    gi = kk * NT + i
                    bank = pc[4 + gi % 2]
                    e.wait_ge(s_qa.h, qa0 + gi + 1)
                    settle(e, 256)
                    dst = acc[:, i * 128:(i + 1) * 128]
                    wc = gwk[:, WSEL + i * 16 + eidx:WSEL + i * 16 + eidx + 1]
                    if eidx == 0:
                        e.tensor_scalar(dst, bank[:, 0:128], wc, None,
                                        ALU.mult).then_inc(s_qae.h, 1)
                    else:
                        e.scalar_tensor_tensor(dst, bank[:, 0:128], wc, dst,
                                               ALU.mult, ALU.add).then_inc(s_qae.h, 1)
                    s_qae.inc()
                kk += 1
            e.wait_ge(s_qae.h, s_qae.n)
            settle(e)
            if pj == 2:
                ins = e.tensor_copy(vt[:], acc[:])
            else:
                ins = e.tensor_copy(ot[:], acc[:])
            ins.then_inc(s_qae.h, 1)
            s_qae.inc()
    block.vector(dve_qkv)

    att0, ate0 = B0["att"], B0["ate"]

    def pe_qtr(e):
        for pj in range(2):
            e.wait_ge(s_qae.h, qae0 + (pj + 1) * (E * NT + 1))
            settle(e)
            for i in range(NT):
                gi = pj * NT + i
                bank = trA if gi % 2 == 0 else trB
                if gi >= 2:
                    e.wait_ge(s_ate.h, ate0 + gi - 1)
                e.transpose(bank[:, 0:128], ot[:, i * 128:(i + 1) * 128],
                            b.idb[:]).then_inc(s_att.h, 1)
                s_att.inc()
    block.tensor(pe_qtr)

    def dve_qtr(e):
        for pj in range(2):
            dstb = qf if pj == 0 else kf
            for i in range(NT):
                gi = pj * NT + i
                bank = trA if gi % 2 == 0 else trB
                e.wait_ge(s_att.h, att0 + gi + 1)
                settle(e, 256)
                e.tensor_copy(dstb[:, i * 128:(i + 1) * 128],
                              bank[:, 0:128]).then_inc(s_ate.h, 1)
                s_ate.inc()
    block.vector(dve_qtr)
    qtr_att = s_att.n
    qtr_ate = s_ate.n

    ac = Stepper(b, block, "f3a", extra_waits=[(s_ate, qtr_ate),
                                               (s_qae, s_qae.n)])
    for sq in range(SEQ_PER_CORE):
        for qt in range(TPS):
            i = sq * TPS + qt

            def smm1(e, sq=sq, i=i):
                e.matmul(pc[0][:], qf[:, i * 128:(i + 1) * 128],
                         kf[:, sq * T:sq * T + 512], start=True, stop=True)
                return e.matmul(pc[1][:], qf[:, i * 128:(i + 1) * 128],
                                kf[:, sq * T + 512:sq * T + 1024],
                                start=True, stop=True)
            ac.step("tensor", smm1)
            ac.step("vector", lambda e: (e.tensor_copy(sc_sb[:, 0:512], pc[0][:]),
                                         e.tensor_copy(sc_sb[:, 512:1024],
                                                       pc[1][:]))[-1])

            def smm2(e, sq=sq, i=i):
                e.matmul(pc[0][:], qf[:, i * 128:(i + 1) * 128],
                         kf[:, sq * T + 1024:sq * T + 1536], start=True, stop=True)
                return e.matmul(pc[1][:], qf[:, i * 128:(i + 1) * 128],
                                kf[:, sq * T + 1536:sq * T + 2048],
                                start=True, stop=True)
            ac.step("tensor", smm2)
            ac.step("vector", lambda e: (e.tensor_copy(sc_sb[:, 1024:1536], pc[0][:]),
                                         e.tensor_copy(sc_sb[:, 1536:2048],
                                                       pc[1][:]))[-1])
            ac.step("vector", lambda e: e.reduce_max(mcol[:, 0:1],
                    sc_sb[:, 0:2048], axis=AX.X))
            ac.step("vector", lambda e: e.tensor_scalar_mul(mcol[:, 1:2],
                    mcol[:, 0:1], -ATT_SCALE))
            ac.step("scalar", lambda e: e.activation(pt_sb[:], sc_sb[:, 0:2048],
                    AF.Exp, bias=mcol[:, 1:2], scale=ATT_SCALE,
                    accum_out=mcol[:, 0:1]))
            ac.step("vector", lambda e: e.reciprocal(mcol[:, 1:2], mcol[:, 0:1]))
            for kt in range(TPS):
                ac.step("tensor", lambda e, kt=kt: e.transpose(trA[:, 0:128],
                        pt_sb[:, kt * 128:(kt + 1) * 128], b.idb[:]),
                        settle_cycles=128)
                ac.step("vector", lambda e: e.tensor_copy(ptT[:], trA[:, 0:128]),
                        settle_cycles=128)
                ac.step("tensor", lambda e, sq=sq, kt=kt: e.matmul(
                        pc[2][:, 0:128], ptT[:],
                        vt[:, (sq * TPS + kt) * 128:(sq * TPS + kt + 1) * 128],
                        start=(kt == 0), stop=(kt == TPS - 1),
                        skip_group_check=True), settle_cycles=128)
            ac.step("vector", lambda e, i=i: e.tensor_scalar(
                ot[:, i * 128:(i + 1) * 128], pc[2][:, 0:128], mcol[:, 1:2],
                None, ALU.mult))
    att_done = ac.s.n
    s_ac = ac.s

    oy0, oye0 = B0["oy"], B0["oye"]

    def pe_out(e):
        e.wait_ge(s_ac.h, att_done)
        e.wait_ge(s_w.h, w_done)
        settle(e)
        for i in range(NT):
            bank = trA if i % 2 == 0 else trB
            if i >= 2:
                e.wait_ge(s_ate.h, qtr_ate + i - 1)
            e.transpose(bank[:, 0:128], ot[:, i * 128:(i + 1) * 128],
                        b.idb[:]).then_inc(s_att.h, 1)
            s_att.inc()
            e.wait_ge(s_ate.h, qtr_ate + i + 1)
            settle(e)
            for eidx in range(E):
                gi = i * E + eidx
                if gi >= 1:
                    e.wait_ge(s_oye.h, oye0 + 2 * gi)
                e.matmul(pc[4][:], ofm1[:],
                         opw[:, eidx * 1024:eidx * 1024 + 512],
                         start=True, stop=True).then_inc(s_oy.h, 1)
                s_oy.inc()
                e.matmul(pc[5][:], ofm1[:],
                         opw[:, eidx * 1024 + 512:(eidx + 1) * 1024],
                         start=True, stop=True).then_inc(s_oy.h, 1)
                s_oy.inc()
    block.tensor(pe_out)

    def dve_out(e):
        for i in range(NT):
            e.wait_ge(s_att.h, qtr_att + i + 1)
            settle(e, 256)
            bank = trA if i % 2 == 0 else trB
            e.tensor_copy(ofm1[:], bank[:, 0:128]).then_inc(s_ate.h, 1)
            s_ate.inc()
            for eidx in range(E):
                gi = i * E + eidx
                e.wait_ge(s_oy.h, oy0 + 2 * gi + 2)
                settle(e, 256)
                wc = gwk[:, WSEL + i * 16 + eidx:WSEL + i * 16 + eidx + 1]
                if eidx == 0:
                    e.tensor_scalar(oacc[:, 0:512], pc[4][:], wc, None, ALU.mult)
                    e.tensor_scalar(oacc[:, 512:1024], pc[5][:], wc, None,
                                    ALU.mult).then_inc(s_oye.h, 2)
                else:
                    e.scalar_tensor_tensor(oacc[:, 0:512], pc[4][:], wc,
                                           oacc[:, 0:512], ALU.mult, ALU.add)
                    e.scalar_tensor_tensor(oacc[:, 512:1024], pc[5][:], wc,
                                           oacc[:, 512:1024], ALU.mult,
                                           ALU.add).then_inc(s_oye.h, 2)
                s_oye.inc(2)
            e.wait_ge(s_oye.h, s_oye.n)
            e.wait_ge(s_xr2[i % 4].h, xr20[i % 4] + 16 * (i // 4 + 1))
            settle(e, 256)
            if i >= 2:
                e.wait_ge(s_st[i % 2].h, st0[i % 2] + 16 * (i // 2))
            sl = yt[:, (i % 2) * C:(i % 2 + 1) * C]
            e.scalar_tensor_tensor(sl[:], oacc[:], scale,
                                   xr2[:, (i % 4) * C:(i % 4 + 1) * C],
                                   ALU.mult, ALU.add).then_inc(s_ye.h, 1)
            s_ye.inc()
    block.vector(dve_out)

    def stores(e):
        for i in range(NT):
            e.wait_ge(s_ye.h, ye0 + i + 1)
            sl = s_st[i % 2]
            e.dma_start(out=xo_hbm[i * 128:(i + 1) * 128, :],
                        in_=yt[:, (i % 2) * C:(i % 2 + 1) * C]).then_inc(sl.h, 16)
            sl.inc(16)
        e.wait_ge(s_st[0].h, s_st[0].n)
        e.wait_ge(s_st[1].h, s_st[1].n)
    block.gpsimd(stores)


def f3_host_weights(z):
    sim = np.ascontiguousarray(z["L3_ffn_sim_matrix"])
    gatesb = np.ascontiguousarray(
        np.broadcast_to(z["L3_ffn_gates"], (128, 16))).astype(np.float32)
    qkvpT = np.ascontiguousarray(np.concatenate(
        [z["L3_ffn_q_proj"], z["L3_ffn_k_proj"], z["L3_ffn_v_proj"]], axis=0))
    opT = np.ascontiguousarray(z["L3_ffn_o_proj"])
    return dict(sim=sim, gatesb=gatesb, qkvpT=qkvpT, opT=opT)


def build_F3_program(z):
    b = Bld("f3")
    hw = f3_host_weights(z)
    x_in = b.inp("x_in", np.zeros((NTOK, C), np.float32))
    W = {k: b.inp("f3_" + k, v) for k, v in hw.items()}
    xo = b.out("x_out", (NTOK, C), F32)
    with b:
        emit_F3(b, b.block, x_in, xo, W, scale=0.5)
        b.finish()
    return b, hw


def emit_A3(b, block, x_hbm, xo_hbm, W):
    nc = b.nc
    H = 16
    SC8 = 0.125
    trA = b.ps("a3_trA", [128, 1024], BF16)
    trB = b.ps("a3_trB", [128, 1024], BF16)
    pc = [b.ps(f"a3_pc{k}", [128, 512], F32) for k in range(6)]
    nb = b.sb("a3_nb", [128, 8 * T], BF16)
    kfb = b.sb("a3_kf", [128, 8 * T], BF16)
    vtb = b.sb("a3_vt", [128, 16 * 1024], BF16)
    wq_sb = b.sb("a3_wq", [128, 8 * 1024], BF16)
    wsh = b.sb("a3_wsh", [128, 16 * 1024], BF16)
    qtb = b.sb("a3_qt", [128, 8 * 128], BF16)
    aob = b.sb("a3_ao", [128, 8 * 128], BF16)
    aof = b.sb("a3_aof", [128, 8 * 128], BF16)
    stg = b.sb("a3_stg", [128, 8 * 128], BF16)
    ohfm = b.sb("a3_oh", [8, NTOK], BF16)
    ohtm = b.sb("a3_ohtm", [128, 8], BF16)
    cntb = b.sb("a3_cnt", [8, 2], BF16)
    i1tmp = b.sb("a3_i1t", [1, 512], BF16)
    i1col = b.sb("a3_i1c", [128, 32], BF16)
    i1rep = b.sb("a3_i1rep", [128, 128], F32)
    i0rep = b.sb("a3_i0rep", [128, 128], F32)
    pt_sb = b.sb("a3_pt", [128, 2048], BF16)
    ptT = b.sb("a3_ptT", [128, 1024], BF16)
    mcol = b.sb("a3_mcol", [128, 2], F32)
    xr_ln = b.sb("lnf_xr_a3", [128, 4 * C], F32)
    scb = xr_ln
    binq = b.sb("a3_binqs", [128, 16], F32)
    bink = b.sb("a3_binks", [128, 16], F32)
    binv = b.sb("a3_binvs", [128, 1024], BF16)
    bout = b.sb("a3_bouts", [128, 16], F32)
    fbr = b.sb("a3_fbr", [128, 1024], BF16)
    prot = b.sb("a3_prot", [128, 8 * 8], BF16)
    yt = b.sb("a3_yt", [128, C], F32)

    n_scr = b.scratch("a3_nscr", [128, 8 * NTOK], BF16)
    l_scr = b.scratch("a3_lscr", [128, 8 * NTOK], BF16)
    s_scr = b.scratch("a3_sscr", [128, 8 * NTOK], BF16)

    s_w = b.sem("a3_w")
    s_wp = [b.sem(f"a3_wp{k}") for k in range(2)]
    s_ns = b.sem("a3_ns")
    s_ls = b.sem("a3_ls")
    s_ll = b.sem("a3_ll")
    s_ye = b.sem("a3_ye")
    s_st = [b.sem(f"a3_st{k}") for k in range(2)]
    s_xr2 = [b.sem(f"a3_xr2{k}") for k in range(4)]

    def wloads(e):
        e.dma_start(out=prot[:], in_=AP(W["protT"].tensor, 0,
                    [[8, 128], [1024, 8], [1, 8]])).then_inc(s_w.h, 16)
        e.dma_start(out=binq[:], in_=W["binq"]).then_inc(s_w.h, 16)
        e.dma_start(out=bink[:], in_=W["bink"]).then_inc(s_w.h, 16)
        e.dma_start(out=bout[:], in_=W["bout"]).then_inc(s_w.h, 16)
        e.dma_start(out=fbr[:], in_=W["fbrep"]).then_inc(s_w.h, 16)
        s_w.inc(80)
    block.gpsimd(wloads)
    w_done = s_w.n

    ye0 = s_ye.n
    st0 = [x.n for x in s_st]
    xr20 = [x.n for x in s_xr2]
    xr_holder = [None]

    seg = Stepper(b, block, "a3s")
    for sq in range(SEQ_PER_CORE):
        pre = [(seg.s, seg.s.n), (s_ns, s_ns.n)] if sq else []
        s_nfm, xr2, s_lnnr = emit_ln_fm(b, block, x_hbm, nb, "a3", trA, trB,
                                        t0=sq * TPS, nt=TPS, pre_waits=pre)
        xr_holder[0] = xr2
        nfm_done = s_nfm.n
        if sq == 0:
            seg.extra = [(s_nfm, nfm_done), (s_w, w_done), (b.s_cstg, b.s_cstg.n)]
        else:
            seg.extra = [(s_nfm, nfm_done)]

        def nstore(e, sq=sq, nfm_done=nfm_done):
            e.wait_ge(s_nfm.h, nfm_done)
            for ch in range(8):
                e.dma_start(out=AP(n_scr.tensor, ch * NTOK + sq * T,
                                   [[8 * NTOK, 128], [1, T]]),
                            in_=nb[:, ch * T:(ch + 1) * T]).then_inc(s_ns.h, 16)
                s_ns.inc(16)
        block.sync(nstore)

        for it in range(TPS):
            def pmm(e, it=it):
                for ch in range(8):
                    ins = e.matmul(pc[0][:, 0:8],
                                   nb[:, ch * T + it * 128:ch * T + (it + 1) * 128],
                                   prot[:, ch * 8:(ch + 1) * 8],
                                   start=(ch == 0), stop=(ch == 7))
                return ins
            seg.step("tensor", pmm, settle_cycles=128)
            seg.step("vector", lambda e: e.tensor_copy(scb[:, 0:8], pc[0][:, 0:8]),
                     settle_cycles=128)
            seg.step("vector", lambda e: e.reduce_max(mcol[:, 0:1], scb[:, 0:8],
                     axis=AX.X), settle_cycles=128)
            seg.step("vector", lambda e: e.tensor_scalar(ohtm[:], scb[:, 0:8],
                     mcol[:, 0:1], None, ALU.is_ge), settle_cycles=128)
            seg.step("tensor", lambda e: e.transpose(trA[0:8, 0:128], ohtm[:],
                     b.idb[:]), settle_cycles=128)
            seg.step("vector", lambda e, sq=sq, it=it: e.tensor_copy(
                ohfm[0:8, sq * T + it * 128:sq * T + (it + 1) * 128],
                trA[0:8, 0:128]), settle_cycles=128)
        seg.step("vector", lambda e, sq=sq: e.reduce_sum(
            scb[0:8, 0:1], ohfm[0:8, sq * T:(sq + 1) * T], axis=AX.X),
            settle_cycles=128)
        seg.step("vector", lambda e, sq=sq: e.tensor_copy(cntb[0:8, sq:sq + 1],
                 scb[0:8, 0:1]), settle_cycles=128)
        for s4 in range(4):
            seg.step("tensor", lambda e, sq=sq, s4=s4: e.matmul(
                pc[1][0:1, 0:512], cntb[0:8, sq:sq + 1],
                ohfm[0:8, sq * T + s4 * 512:sq * T + (s4 + 1) * 512],
                start=True, stop=True), settle_cycles=128)
            seg.step("vector", lambda e: e.tensor_scalar(
                i1tmp[0:1, :], pc[1][0:1, 0:512], 1.5, None, ALU.is_lt),
                settle_cycles=128)

            def i1tr(e, s4=s4):
                for j in range(4):
                    ins = e.matmul(pc[1][:, j:j + 1],
                                   i1tmp[0:1, j * 128:(j + 1) * 128],
                                   b.cst_ones_b[0:1, 0:1], start=True, stop=True)
                return ins
            seg.step("tensor", i1tr, settle_cycles=128)
            seg.step("vector", lambda e, sq=sq, s4=s4: e.tensor_copy(
                i1col[:, sq * 16 + s4 * 4:sq * 16 + (s4 + 1) * 4],
                pc[1][:, 0:4]), settle_cycles=128)
    seg_done = seg.s.n
    s_seg = seg.s

    mha = Stepper(b, block, "a3m", extra_waits=[(s_seg, seg_done),
                                                (s_ns, s_ns.n)])
    wp_cnt = [0, 0]

    for mi, pref in enumerate(("lt", "st")):
        for sq in range(SEQ_PER_CORE):
            def phw(e, mi=mi, nwait=mha.s.n):
                e.wait_ge(mha.s.h, nwait)
                e.dma_start(out=wsh[:], in_=AP(W[("lt" if mi == 0 else "st") + "_wkvT"].tensor,
                            0, [[2048, 128], [262144, 8], [1, 2048]])).then_inc(s_wp[0].h, 16)
                e.dma_start(out=wq_sb[:], in_=AP(W[("lt" if mi == 0 else "st") + "_wqT"].tensor,
                            0, [[1024, 128], [131072, 8], [1, 1024]])).then_inc(s_wp[1].h, 16)
                e.dma_start(out=binv[:], in_=AP(W["binv"].tensor, mi * 1024,
                            [[2048, 128], [1, 1024]])).then_inc(s_wp[1].h, 16)
            block.gpsimd(phw)
            s_wp[0].inc(16)
            s_wp[1].inc(32)
            def nload(e, sq=sq, nwait=mha.s.n, nsw=s_ns.n):
                e.wait_ge(s_seg.h, seg_done)
                e.wait_ge(s_ns.h, nsw)
                e.wait_ge(mha.s.h, nwait)
                e.dma_start(out=nb[:], in_=AP(n_scr.tensor, sq * T,
                            [[8 * NTOK, 128], [NTOK, 8], [1, T]])).then_inc(s_ll.h, 16)
                s_ll.inc(16)
            block.gpsimd(nload)
            mha.extra = [(s_wp[0], s_wp[0].n), (s_wp[1], s_wp[1].n),
                         (s_ll, s_ll.n), (s_ls, s_ls.n)]
            for ch in range(8):
                for sl4 in range(4):
                    def kmm(e, ch=ch, sl4=sl4):
                        for ci in range(8):
                            ins = e.matmul(pc[0][:],
                                           wsh[:, ci * 2048 + ch * 128:ci * 2048 + (ch + 1) * 128],
                                           nb[:, ci * T + sl4 * 512:ci * T + (sl4 + 1) * 512],
                                           start=(ci == 0), stop=(ci == 7))
                        return ins
                    mha.step("tensor", kmm, settle_cycles=128)
                    mha.step("scalar", lambda e, ch=ch, sl4=sl4, mi=mi:
                             e.activation(kfb[:, ch * T + sl4 * 512:ch * T + (sl4 + 1) * 512],
                                          pc[0][:], AF.Identity,
                                          bias=bink[:, mi * 8 + ch:mi * 8 + ch + 1]),
                             settle_cycles=128)
            for it in range(TPS):
                for sl2 in range(2):
                    def vmm(e, it=it, sl2=sl2):
                        for ci in range(8):
                            ins = e.matmul(pc[1][:],
                                           nb[:, ci * T + it * 128:ci * T + (it + 1) * 128],
                                           wsh[:, ci * 2048 + 1024 + sl2 * 512:
                                               ci * 2048 + 1024 + (sl2 + 1) * 512],
                                           start=(ci == 0), stop=(ci == 7))
                        return ins
                    mha.step("tensor", vmm, settle_cycles=128)
                    mha.step("vector", lambda e, it=it, sl2=sl2, mi=mi:
                             e.tensor_tensor(vtb[:, it * 1024 + sl2 * 512:
                                                 it * 1024 + (sl2 + 1) * 512],
                                             pc[1][:],
                                             binv[:, sl2 * 512:(sl2 + 1) * 512],
                                             ALU.add), settle_cycles=128)
            def phw2(e, mi=mi, nwait=mha.s.n):
                e.wait_ge(mha.s.h, nwait)
                e.dma_start(out=wsh[:, 0:8 * 1024],
                            in_=AP(W[("lt" if mi == 0 else "st") + "_woutT"].tensor,
                                   0, [[1024, 128], [131072, 8], [1, 1024]])
                            ).then_inc(s_wp[0].h, 16)
            block.gpsimd(phw2)
            s_wp[0].inc(16)
            mha.extra = [(s_wp[0], s_wp[0].n)]

            for qt in range(TPS):
                def qmm(e, qt=qt):
                    for ch in range(8):
                        bank = pc[4 + ch // 4]
                        for ci in range(8):
                            ins = e.matmul(bank[:, (ch % 4) * 128:(ch % 4 + 1) * 128],
                                           wq_sb[:, ci * 1024 + ch * 128:ci * 1024 + (ch + 1) * 128],
                                           nb[:, ci * T + qt * 128:ci * T + (qt + 1) * 128],
                                           start=(ci == 0), stop=(ci == 7))
                    return ins
                mha.step("tensor", qmm, settle_cycles=128)

                def qev(e, mi=mi):
                    for hb in range(2):
                        for ch4 in range(4):
                            ch = hb * 4 + ch4
                            ins = e.activation(qtb[:, ch * 128:(ch + 1) * 128],
                                               pc[4 + hb][:, ch4 * 128:(ch4 + 1) * 128],
                                               AF.Identity,
                                               bias=binq[:, mi * 8 + ch:mi * 8 + ch + 1])
                    return ins
                mha.step("scalar", qev, settle_cycles=128)
                for h in range(H):
                    po, co = (h % 2) * 64, (h // 2) * 128
                    for w2 in range(2):
                        def smm(e, h=h, po=po, co=co, w2=w2):
                            e.matmul(pc[0][:], qtb[po:po + 64, co:co + 128],
                                     kfb[po:po + 64, (h // 2) * T + (2 * w2) * 512:
                                         (h // 2) * T + (2 * w2 + 1) * 512],
                                     start=True, stop=True)
                            return e.matmul(pc[1][:], qtb[po:po + 64, co:co + 128],
                                            kfb[po:po + 64, (h // 2) * T + (2 * w2 + 1) * 512:
                                                (h // 2) * T + (2 * w2 + 2) * 512],
                                            start=True, stop=True)
                        mha.step("tensor", smm, settle_cycles=128)
                        if pref == "st":
                            def samm(e, qt=qt, w2=w2, sq=sq):
                                e.matmul(pc[2][:],
                                         ohfm[0:8, sq * T + qt * 128:sq * T + (qt + 1) * 128],
                                         ohfm[0:8, sq * T + (2 * w2) * 512:
                                              sq * T + (2 * w2 + 1) * 512],
                                         start=True, stop=True)
                                return e.matmul(pc[3][:],
                                                ohfm[0:8, sq * T + qt * 128:sq * T + (qt + 1) * 128],
                                                ohfm[0:8, sq * T + (2 * w2 + 1) * 512:
                                                     sq * T + (2 * w2 + 2) * 512],
                                                start=True, stop=True)
                            mha.step("tensor", samm, settle_cycles=128)

                            def smsk1(e, w2=w2):
                                e.tensor_copy(scb[:, (2 * w2) * 512:(2 * w2 + 1) * 512],
                                              pc[0][:])
                                return e.tensor_copy(scb[:, (2 * w2 + 1) * 512:
                                                         (2 * w2 + 2) * 512], pc[1][:])
                            mha.step("vector", smsk1, settle_cycles=128)

                            def smsk2(e, w2=w2):
                                for j in range(2):
                                    c0 = (2 * w2 + j) * 512
                                    ins = e.scalar_tensor_tensor(scb[:, c0:c0 + 512],
                                                                 pc[2 + j][:], 1e32,
                                                                 scb[:, c0:c0 + 512],
                                                                 ALU.mult, ALU.add)
                                return ins
                            mha.step("vector", smsk2, settle_cycles=128)

                            def smsk3(e, w2=w2):
                                for j in range(2):
                                    c0 = (2 * w2 + j) * 512
                                    ins = e.tensor_scalar_add(scb[:, c0:c0 + 512],
                                                              scb[:, c0:c0 + 512],
                                                              -1e32)
                                return ins
                            mha.step("vector", smsk3, settle_cycles=128)
                        else:
                            def sev(e, w2=w2):
                                e.tensor_copy(scb[:, (2 * w2) * 512:(2 * w2 + 1) * 512],
                                              pc[0][:])
                                return e.tensor_copy(scb[:, (2 * w2 + 1) * 512:
                                                         (2 * w2 + 2) * 512], pc[1][:])
                            mha.step("vector", sev, settle_cycles=128)
                    mha.step("vector", lambda e: e.reduce_max(mcol[:, 0:1],
                             scb[:, 0:2048], axis=AX.X), settle_cycles=128)
                    mha.step("vector", lambda e: e.tensor_scalar_mul(mcol[:, 1:2],
                             mcol[:, 0:1], -SC8), settle_cycles=128)
                    mha.step("scalar", lambda e: e.activation(pt_sb[:],
                             scb[:, 0:2048], AF.Exp, bias=mcol[:, 1:2], scale=SC8,
                             accum_out=mcol[:, 0:1]), settle_cycles=128)
                    mha.step("vector", lambda e: e.reciprocal(mcol[:, 1:2],
                             mcol[:, 0:1]), settle_cycles=128)
                    for g8 in range(2):
                        def ptr8(e, g8=g8):
                            for j in range(8):
                                ins = e.transpose(trA[:, j * 128:(j + 1) * 128],
                                                  pt_sb[:, (g8 * 8 + j) * 128:
                                                        (g8 * 8 + j + 1) * 128],
                                                  b.idb[:])
                            return ins
                        mha.step("tensor", ptr8, settle_cycles=128)
                        mha.step("vector", lambda e: e.tensor_copy(ptT[:], trA[:]),
                                 settle_cycles=128)

                        def pv8(e, g8=g8, h=h):
                            for j in range(8):
                                kt = g8 * 8 + j
                                ins = e.matmul(pc[2][:, 0:64],
                                               ptT[:, j * 128:(j + 1) * 128],
                                               vtb[:, kt * 1024 + h * 64:kt * 1024 + (h + 1) * 64],
                                               start=(kt == 0), stop=(kt == 15),
                                               skip_group_check=True)
                            return ins
                        mha.step("tensor", pv8, settle_cycles=128)
                    mha.step("vector", lambda e, h=h: e.tensor_scalar(
                        aob[:, h * 64:(h + 1) * 64], pc[2][:, 0:64],
                        mcol[:, 1:2], None, ALU.mult), settle_cycles=128)
                def aotr(e):
                    for j in range(8):
                        ins = e.transpose(trB[:, j * 128:(j + 1) * 128],
                                          aob[:, j * 128:(j + 1) * 128], b.idb[:])
                    return ins
                mha.step("tensor", aotr, settle_cycles=128)
                mha.step("vector", lambda e: e.tensor_copy(aof[:], trB[:]),
                         settle_cycles=128)
                for c2 in range(2):
                    def omm(e, c2=c2):
                        for ch in range(4):
                            cg = c2 * 4 + ch
                            for k1 in range(8):
                                ins = e.matmul(pc[4 + c2][:, ch * 128:(ch + 1) * 128],
                                               wsh[:, k1 * 1024 + cg * 128:
                                                   k1 * 1024 + (cg + 1) * 128],
                                               aof[:, k1 * 128:(k1 + 1) * 128],
                                               start=(k1 == 0), stop=(k1 == 7))
                        return ins
                    mha.step("tensor", omm, settle_cycles=128)

                    def oev(e, c2=c2, mi=mi):
                        for ch in range(4):
                            cg = c2 * 4 + ch
                            ins = e.activation(stg[:, cg * 128:(cg + 1) * 128],
                                               pc[4 + c2][:, ch * 128:(ch + 1) * 128],
                                               AF.Identity,
                                               bias=bout[:, mi * 8 + cg:mi * 8 + cg + 1])
                        return ins
                    mha.step("scalar", oev, settle_cycles=128)

                def stg_store(e, mi=mi, sq=sq, qt=qt, nwait=mha.s.n):
                    e.wait_ge(mha.s.h, nwait)
                    scr = l_scr if mi == 0 else s_scr
                    e.dma_start(out=AP(scr.tensor, sq * T + qt * 128,
                                       [[8 * NTOK, 128], [NTOK, 8], [1, 128]]),
                                in_=AP(stg, 0, [[1024, 128], [128, 8], [1, 128]])
                                ).then_inc(s_ls.h, 16)
                    s_ls.inc(16)
                block.sync(stg_store)
                mha.extra = [(s_ls, s_ls.n)]
    mha_done = mha.s.n
    s_mha = mha.s

    def xr2_loads(e):
        e.wait_ge(s_mha.h, mha_done)
        xr2 = xr_holder[0]
        for i in range(NT):
            if i >= 4:
                e.wait_ge(s_ye.h, ye0 + i - 3)
            sl = s_xr2[i % 4]
            e.dma_start(out=xr2[:, (i % 4) * C:(i % 4 + 1) * C],
                        in_=x_hbm[i * 128:(i + 1) * 128, :]).then_inc(sl.h, 16)
            sl.inc(16)
    block.sync(xr2_loads)

    def fus_wload(e):
        e.wait_ge(s_mha.h, mha_done)
        e.dma_start(out=wsh[:], in_=AP(W["fusT"].tensor, 0,
                    [[1024, 128], [131072, 16], [1, 1024]])).then_inc(s_wp[0].h, 16)
        s_wp[0].inc(16)
    block.gpsimd(fus_wload)
    fusw_done = s_wp[0].n

    fus = Stepper(b, block, "a3f", extra_waits=[(s_mha, mha_done),
                                               (s_wp[0], fusw_done),
                                               (s_ls, s_ls.n)])
    xr2v = xr_holder[0]
    for i in range(NT):
        def tl_load(e, i=i, nwait=fus.s.n):
            e.wait_ge(fus.s.h, nwait)
            e.dma_start(out=qtb[:], in_=AP(l_scr.tensor, i * 128,
                        [[8 * NTOK, 128], [NTOK, 8], [1, 128]])).then_inc(s_ll.h, 16)
            e.dma_start(out=aob[:], in_=AP(s_scr.tensor, i * 128,
                        [[8 * NTOK, 128], [NTOK, 8], [1, 128]])).then_inc(s_ll.h, 16)
            e.dma_start(out=aof[:], in_=AP(n_scr.tensor, i * 128,
                        [[8 * NTOK, 128], [NTOK, 8], [1, 128]])).then_inc(s_ll.h, 16)
            s_ll.inc(48)
        block.gpsimd(tl_load)
        fus.extra = fus.extra + [(s_ll, s_ll.n), (s_st[0], s_st[0].n)]
        sq = i // TPS
        def i1mk(e, i=i):
            src = AP(i1col, i, [[32, 128], [0, 128]])
            return e.tensor_copy(i1rep[:], src)
        fus.step("vector", i1mk, settle_cycles=128)
        fus.step("vector", lambda e: e.tensor_scalar(i0rep[:], i1rep[:], -1.0,
                 -1.0, ALU.mult, ALU.subtract), settle_cycles=128)

        def blend1(e, i=i):
            for ch in range(8):
                e.tensor_tensor(stg[:, ch * 128:(ch + 1) * 128],
                                aob[:, ch * 128:(ch + 1) * 128],
                                i0rep[:], ALU.mult)
            for ch in range(8):
                ins = e.tensor_tensor(yt[:, ch * 128:(ch + 1) * 128],
                                      aof[:, ch * 128:(ch + 1) * 128], i1rep[:],
                                      ALU.mult)
            return ins
        fus.step("vector", blend1)

        def blend2(e, i=i):
            for ch in range(8):
                ins = e.tensor_tensor(stg[:, ch * 128:(ch + 1) * 128],
                                      stg[:, ch * 128:(ch + 1) * 128],
                                      yt[:, ch * 128:(ch + 1) * 128], ALU.add)
            return ins
        fus.step("vector", blend2)

        def fmm(e, i=i):
            for c2 in range(2):
                for k1 in range(8):
                    e.matmul(pc[4 + c2][:], qtb[:, k1 * 128:(k1 + 1) * 128],
                             wsh[:, k1 * 1024 + c2 * 512:k1 * 1024 + (c2 + 1) * 512],
                             start=(k1 == 0), stop=False)
                for k1 in range(8):
                    ins = e.matmul(pc[4 + c2][:], stg[:, k1 * 128:(k1 + 1) * 128],
                                   wsh[:, (8 + k1) * 1024 + c2 * 512:
                                       (8 + k1) * 1024 + (c2 + 1) * 512],
                                   start=False, stop=(k1 == 7))
            return ins
        fus.step("tensor", fmm)

        def yev(e, i=i):
            e.wait_ge(s_xr2[i % 4].h, xr20[i % 4] + 16 * (i // 4 + 1))
            sl = yt[:, 0:C]
            e.tensor_tensor(sl[:, 0:512], pc[4][:],
                            xr2v[:, (i % 4) * C:(i % 4) * C + 512], ALU.add)
            return e.tensor_tensor(sl[:, 512:1024], pc[5][:],
                                   xr2v[:, (i % 4) * C + 512:(i % 4 + 1) * C],
                                   ALU.add)
        fus.step("vector", yev)
        fus.step("vector", lambda e: e.tensor_tensor(yt[:, 0:C], yt[:, 0:C],
                 fbr[:], ALU.add))

        def st_store(e, i=i, nwait=fus.s.n):
            e.wait_ge(fus.s.h, nwait)
            sl = s_st[0]
            e.dma_start(out=xo_hbm[i * 128:(i + 1) * 128, :],
                        in_=yt[:, 0:C]).then_inc(sl.h, 16)
            sl.inc(16)
            s_ye.inc()
            e.sem_inc(s_ye.h, 1)
        block.gpsimd(st_store)
    def fin(e):
        e.wait_ge(s_st[0].h, s_st[0].n)
        e.wait_ge(s_st[1].h, s_st[1].n)
    block.gpsimd(fin)


def a3_host_weights(z):
    def T_(a):
        return np.ascontiguousarray(a.T)
    out = {}
    for p in ("lt", "st"):
        wi = z[f"L3_attn_{p}_w_in"]
        bi = z[f"L3_attn_{p}_b_in"]
        out[p + "_wqT"] = T_(wi[0:1024])
        out[p + "_wkvT"] = T_(wi[1024:3072])
        out[p + "_woutT"] = T_(z[f"L3_attn_{p}_w_out"])
        out[p + "_binq"] = np.ascontiguousarray(bi[0:1024].reshape(8, 128).T)
        out[p + "_bink"] = np.ascontiguousarray(bi[1024:2048].reshape(8, 128).T)
        out[p + "_binv"] = np.ascontiguousarray(
            np.broadcast_to(bi[2048:3072], (128, 1024)))
        out[p + "_bout"] = np.ascontiguousarray(
            z[f"L3_attn_{p}_b_out"].reshape(8, 128).T)
    binq = np.concatenate([out["lt_binq"], out["st_binq"]], axis=1)
    bink = np.concatenate([out["lt_bink"], out["st_bink"]], axis=1)
    binv = np.concatenate([out["lt_binv"], out["st_binv"]], axis=1)
    bout = np.concatenate([out["lt_bout"], out["st_bout"]], axis=1)
    hw = dict(protT=T_(z["L3_attn_protos"]), fusT=T_(z["L3_attn_fus_w"]),
              fbrep=np.ascontiguousarray(
                  np.broadcast_to(z["L3_attn_fus_b"], (128, 1024))),
              binq=binq, bink=bink, binv=binv, bout=bout)
    for p in ("lt", "st"):
        hw[p + "_wqT"] = out[p + "_wqT"]
        hw[p + "_wkvT"] = out[p + "_wkvT"]
        hw[p + "_woutT"] = out[p + "_woutT"]
    return hw


def build_A3_program(z):
    b = Bld("a3")
    hw = a3_host_weights(z)
    x_in = b.inp("x_in", np.zeros((NTOK, C), np.float32))
    W = {k: b.inp("a3_" + k, v) for k, v in hw.items()}
    xo = b.out("x_out", (NTOK, C), F32)
    with b:
        emit_A3(b, b.block, x_in, xo, W)
        b.finish()
    return b, hw


RMS_EPS = 1e-6
NEG30 = -1e30


def emit_F0(b, block, x_hbm, xo_hbm, W, scale=0.5):
    nc = b.nc
    H = 16
    SCA = 0.125
    NG = 4
    TPG = NTOK // NG
    TLG = TPG // 128

    trA = b.ps("f0_trA", [128, 1024], BF16)
    trB = b.ps("f0_trB", [128, 1024], BF16)
    pc = [b.ps(f"f0_pc{k}", [128, 512], F32) for k in range(6)]

    AR = 49152
    arena = b.sb("f0_ar", [128, AR], BF16)
    FAR = 10240
    far = b.sb("f0_far", [128, FAR], F32)
    wbig = b.sb("f0_wb", [128, 8192], BF16)
    wb2 = b.sb("f0_wb2", [128, 8192], BF16)
    hr = b.sb("f0_hr", [128, 2 * C], F32)
    rope_c = b.sb("f0_rc", [128, 1024], F32)
    rope_s = b.sb("f0_rs", [128, 1024], F32)
    rmsw = b.sb("f0_rmsw", [128, 1024], F32)
    pt_sb = b.sb("f0_pt", [128, 2048], BF16)
    ptT = b.sb("f0_ptT", [128, 1024], BF16)
    ao = b.sb("f0_ao", [128, 1024], BF16)
    aof = b.sb("f0_aof", [128, 1024], BF16)
    qt_fm = b.sb("f0_qt", [128, 1024], BF16)
    addm = b.sb("f0_admsk", [128, 128], F32)
    wfm_g = b.sb("f0_wfm", [8, 1024], BF16)
    bc_sb = b.sb("f0_bc", [8, 1024], BF16)
    gb = b.sb("f0_gb", [128, 320], F32)
    glog = b.sb("f0_glog", [128, 64], F32)
    stats = b.sb("f0_st2", [128, 16], F32)
    mcol = b.sb("f0_mc", [128, 4], F32)

    h_scr = b.scratch("f0_hscr", [NTOK, C], F32)

    s_w = b.sem("f0_w")
    s_wl = b.sem("f0_wl")
    s_hld = b.sem("f0_hld")
    s_hst = b.sem("f0_hst")
    s_rpc = b.sem("f0_rpc")

    def boot(e):
        e.dma_start(out=addm[:], in_=W["addm"]).then_inc(s_w.h, 16)
        e.dma_start(out=bc_sb[:], in_=W["bc8"]).then_inc(s_w.h, 16)
        s_w.inc(32)
    block.gpsimd(boot)

    ch = Stepper(b, block, "f0", extra_waits=[
        (b.s_cst, b.s_cst.n), (b.s_cstg, b.s_cstg.n), (s_w, s_w.n)])

    def sync_load(dmas):
        def f(e, nwait=ch.s.n, dmas=dmas):
            e.wait_ge(ch.s.h, nwait)
            for dst, src in dmas:
                e.dma_start(out=dst, in_=src).then_inc(s_hld.h, 16)
                s_hld.inc(16)
        block.sync(f)
        ch.extra = ch.extra + [(s_hld, s_hld.n)]

    def gp_load(dmas, sem):
        def f(e, nwait=ch.s.n, dmas=dmas):
            e.wait_ge(ch.s.h, nwait)
            for dst, src in dmas:
                e.dma_start(out=dst, in_=src).then_inc(sem.h, 16)
                sem.inc(16)
        block.gpsimd(f)
        ch.extra = ch.extra + [(sem, sem.n)]

    def gp_store(dmas):
        def f(e, nwait=ch.s.n, dmas=dmas):
            e.wait_ge(ch.s.h, nwait)
            for dst, src in dmas:
                e.dma_start(out=dst, in_=src).then_inc(s_hst.h, 16)
                s_hst.inc(16)
        block.gpsimd(f)
        ch.extra = ch.extra + [(s_hst, s_hst.n)]

    for i in range(NT):
        sl = hr[:, (i % 2) * C:(i % 2 + 1) * C]
        sync_load([(sl, x_hbm[i * 128:(i + 1) * 128, :])])
        ch.step("scalar", lambda e, sl=sl: e.activation(
            far[:, 8192:9216], sl, AF.Square, accum_out=stats[:, 1:2]))
        ch.step("vector", lambda e, sl=sl: e.reduce_sum(
            stats[:, 0:1], sl, axis=AX.X))
        ch.step("vector", lambda e: e.tensor_scalar_mul(
            stats[:, 2:3], stats[:, 0:1], 1.0 / C))
        ch.step("vector", lambda e: e.tensor_scalar_mul(
            stats[:, 3:4], stats[:, 1:2], 1.0 / C))
        ch.step("vector", lambda e: e.tensor_tensor(
            stats[:, 4:5], stats[:, 2:3], stats[:, 2:3], ALU.mult))
        ch.step("vector", lambda e: e.tensor_tensor(
            stats[:, 3:4], stats[:, 3:4], stats[:, 4:5], ALU.subtract))
        ch.step("scalar", lambda e: e.activation(
            stats[:, 5:6], stats[:, 3:4], AF.Sqrt,
            bias=b.cstcol("ln_eps", LN_EPS)))
        ch.step("vector", lambda e: e.reciprocal(stats[:, 6:7], stats[:, 5:6]))
        ch.step("vector", lambda e: e.tensor_tensor(
            stats[:, 7:8], stats[:, 2:3], stats[:, 6:7], ALU.mult))
        ch.step("vector", lambda e: e.tensor_scalar_mul(
            stats[:, 7:8], stats[:, 7:8], -1.0))
        ch.step("scalar", lambda e, sl=sl: e.activation(
            far[:, 0:1024], sl, AF.Identity,
            scale=stats[:, 6:7], bias=stats[:, 7:8]))
        gp_store([(h_scr[i * 128:(i + 1) * 128, :], far[:, 0:1024])])

    def rms_tile(src_tile, dst_bf, scr0):
        ch.step("scalar", lambda e: e.activation(
            far[:, scr0:scr0 + 1024], src_tile, AF.Square,
            accum_out=stats[:, 0:1]))
        ch.step("vector", lambda e: e.tensor_scalar_mul(
            stats[:, 1:2], stats[:, 0:1], 1.0 / C))
        ch.step("scalar", lambda e: e.activation(
            stats[:, 2:3], stats[:, 1:2], AF.Sqrt,
            bias=b.cstcol("rms_eps", RMS_EPS)))
        ch.step("vector", lambda e: e.reciprocal(stats[:, 3:4], stats[:, 2:3]))
        ch.step("scalar", lambda e: e.activation(
            far[:, scr0:scr0 + 1024], src_tile, AF.Identity,
            scale=stats[:, 3:4]))
        ch.step("vector", lambda e: e.tensor_tensor(
            dst_bf, far[:, scr0:scr0 + 1024], rmsw[:], ALU.mult))

    def tr8_to(dst_ap, src_sb, src0=0):
        def f(e):
            for j in range(8):
                ins = e.transpose(trA[:, j * 128:(j + 1) * 128],
                                  src_sb[:, src0 + j * 128:src0 + (j + 1) * 128],
                                  b.idb[:])
            return ins
        ch.step("tensor", f, settle_cycles=128)
        ch.step("vector", lambda e: e.tensor_copy(
            dst_ap, AP(trA, 0, [[1024, 128], [128, 8], [1, 128]])),
            settle_cycles=128)

    def proj_tile(wbuf, it, evac):
        def f(e, it=it):
            for sl2 in range(2):
                for ci in range(8):
                    ins = e.matmul(
                        pc[4 + sl2][:],
                        arena[:, ci * 2048 + it * 128:ci * 2048 + (it + 1) * 128],
                        wbuf[:, ci * 1024 + sl2 * 512:ci * 1024 + (sl2 + 1) * 512],
                        start=(ci == 0), stop=(ci == 7))
            return ins
        ch.step("tensor", f, settle_cycles=128)
        ch.step("vector", evac)

    def rope_steps():
        ch.step("vector", lambda e: e.tensor_tensor(
            far[:, 3072:4096], far[:, 4096:5120], rope_c[:], ALU.mult))

        def halves(e):
            e.tensor_tensor(
                AP(far, 2048, [[FAR, 128], [64, 16], [1, 32]]),
                AP(far, 4096 + 32, [[FAR, 128], [64, 16], [1, 32]]),
                AP(rope_s, 0, [[1024, 128], [64, 16], [1, 32]]), ALU.mult)
            return e.tensor_tensor(
                AP(far, 2048 + 32, [[FAR, 128], [64, 16], [1, 32]]),
                AP(far, 4096, [[FAR, 128], [64, 16], [1, 32]]),
                AP(rope_s, 32, [[1024, 128], [64, 16], [1, 32]]), ALU.mult)
        ch.step("vector", halves)
        ch.step("vector", lambda e: e.tensor_tensor(
            pt_sb[:, 0:1024], far[:, 3072:4096], far[:, 2048:3072], ALU.add))

    WQ = [[1024, 128], [131072, 8], [1, 1024]]

    gp_load([(rmsw[:], W["n1rep"])], s_wl)
    for sq in range(SEQ_PER_CORE):
        for it in range(TPS):
            gt = sq * TPS + it
            sl = hr[:, (it % 2) * C:(it % 2 + 1) * C]
            sync_load([(sl, h_scr[gt * 128:(gt + 1) * 128, :])])
            rms_tile(sl, pt_sb[:, 0:1024], 4096)
            tr8_to(AP(arena, it * 128, [[AR, 128], [2048, 8], [1, 128]]),
                   pt_sb, 0)
        gp_load([(wb2[:], AP(W["wkT"].tensor, 0, WQ))], s_wl)
        for it in range(TPS):
            sync_load([(rope_c[:], W["cosr"][it * 128:(it + 1) * 128, :]),
                       (rope_s[:], W["sinr"][it * 128:(it + 1) * 128, :])])
            proj_tile(wb2, it, lambda e: (
                e.tensor_copy(far[:, 4096:4608], pc[4][:]),
                e.tensor_copy(far[:, 4608:5120], pc[5][:]))[-1])
            rope_steps()
            tr8_to(AP(arena, 16384 + it * 128, [[AR, 128], [2048, 8], [1, 128]]),
                   pt_sb, 0)
        gp_load([(wb2[:], AP(W["wvT"].tensor, 0, WQ))], s_wl)
        for it in range(TPS):
            proj_tile(wb2, it, lambda e, it=it: (
                e.tensor_copy(arena[:, 32768 + it * 1024:32768 + it * 1024 + 512],
                              pc[4][:]),
                e.tensor_copy(arena[:, 32768 + it * 1024 + 512:
                                    32768 + (it + 1) * 1024], pc[5][:]))[-1])
        gp_load([(wbig[:], AP(W["wqT"].tensor, 0, WQ)),
                 (wb2[:], AP(W["wpT"].tensor, 0, WQ))], s_wl)
        for qt in range(TPS):
            gt = sq * TPS + qt
            nkt = qt + 1
            nsl = (nkt * 128 + 511) // 512
            sync_load([(rope_c[:], W["cosr"][qt * 128:(qt + 1) * 128, :]),
                       (rope_s[:], W["sinr"][qt * 128:(qt + 1) * 128, :])])
            proj_tile(wbig, qt, lambda e: (
                e.tensor_copy(far[:, 4096:4608], pc[4][:]),
                e.tensor_copy(far[:, 4608:5120], pc[5][:]))[-1])
            rope_steps()
            tr8_to(AP(qt_fm, 0, [[1024, 128], [128, 8], [1, 128]]), pt_sb, 0)
            for h in range(H):
                po, co = (h % 2) * 64, (h // 2) * 128
                for s0 in range(0, nsl, 2):
                    npair = min(2, nsl - s0)

                    def smm(e, h=h, po=po, co=co, s0=s0, npair=npair):
                        for k in range(npair):
                            ss = s0 + k
                            ins = e.matmul(
                                pc[k][:], qt_fm[po:po + 64, co:co + 128],
                                arena[po:po + 64,
                                      16384 + (h // 2) * 2048 + ss * 512:
                                      16384 + (h // 2) * 2048 + (ss + 1) * 512],
                                start=True, stop=True)
                        return ins
                    ch.step("tensor", smm, settle_cycles=128)

                    def sev(e, s0=s0, npair=npair):
                        for k in range(npair):
                            ins = e.tensor_copy(
                                far[:, (s0 + k) * 512:(s0 + k + 1) * 512],
                                pc[k][:])
                        return ins
                    ch.step("vector", sev, settle_cycles=128)
                ch.step("vector", lambda e, qt=qt: e.tensor_tensor(
                    far[:, qt * 128:(qt + 1) * 128],
                    far[:, qt * 128:(qt + 1) * 128], addm[:], ALU.add),
                    settle_cycles=128)
                ch.step("vector", lambda e, nkt=nkt: e.reduce_max(
                    mcol[:, 0:1], far[:, 0:nkt * 128], axis=AX.X),
                    settle_cycles=128)
                ch.step("vector", lambda e: e.tensor_scalar_mul(
                    mcol[:, 1:2], mcol[:, 0:1], -SCA))
                ch.step("scalar", lambda e, nkt=nkt: e.activation(
                    pt_sb[:, 0:nkt * 128], far[:, 0:nkt * 128], AF.Exp,
                    bias=mcol[:, 1:2], scale=SCA, accum_out=mcol[:, 0:1]))
                ch.step("vector", lambda e: e.reciprocal(
                    mcol[:, 1:2], mcol[:, 0:1]))
                for g8 in range(0, nkt, 8):
                    j1 = min(nkt, g8 + 8)

                    def ptr(e, g8=g8, j1=j1):
                        for j in range(g8, j1):
                            ins = e.transpose(
                                trA[:, (j - g8) * 128:(j - g8 + 1) * 128],
                                pt_sb[:, j * 128:(j + 1) * 128], b.idb[:])
                        return ins
                    ch.step("tensor", ptr, settle_cycles=128)
                    ch.step("vector", lambda e, g8=g8, j1=j1: e.tensor_copy(
                        ptT[:, 0:(j1 - g8) * 128], trA[:, 0:(j1 - g8) * 128]),
                        settle_cycles=128)

                    def pv(e, g8=g8, j1=j1, h=h, nkt=nkt):
                        for j in range(g8, j1):
                            ins = e.matmul(
                                pc[2][:, 0:64],
                                ptT[:, (j - g8) * 128:(j - g8 + 1) * 128],
                                arena[:, 32768 + j * 1024 + h * 64:
                                      32768 + j * 1024 + (h + 1) * 64],
                                start=(j == 0), stop=(j == nkt - 1),
                                skip_group_check=True)
                        return ins
                    ch.step("tensor", pv, settle_cycles=128)
                ch.step("vector", lambda e, h=h: e.tensor_scalar(
                    ao[:, h * 64:(h + 1) * 64], pc[2][:, 0:64],
                    mcol[:, 1:2], None, ALU.mult))

            def aotr(e):
                for j in range(8):
                    ins = e.transpose(trB[:, j * 128:(j + 1) * 128],
                                      ao[:, j * 128:(j + 1) * 128], b.idb[:])
                return ins
            ch.step("tensor", aotr, settle_cycles=128)
            ch.step("vector", lambda e: e.tensor_copy(aof[:], trB[:]),
                    settle_cycles=128)

            def pmm(e):
                for c2 in range(2):
                    for k1 in range(8):
                        ins = e.matmul(
                            pc[4 + c2][:], aof[:, k1 * 128:(k1 + 1) * 128],
                            wb2[:, k1 * 1024 + c2 * 512:k1 * 1024 + (c2 + 1) * 512],
                            start=(k1 == 0), stop=(k1 == 7))
                return ins
            ch.step("tensor", pmm, settle_cycles=128)
            sync_load([(hr[:, 0:C], h_scr[gt * 128:(gt + 1) * 128, :])])

            def h1ev(e):
                e.tensor_tensor(hr[:, C:C + 512], pc[4][:], hr[:, 0:512],
                                ALU.add)
                return e.tensor_tensor(hr[:, C + 512:2 * C], pc[5][:],
                                       hr[:, 512:1024], ALU.add)
            ch.step("vector", h1ev)
            gp_store([(h_scr[gt * 128:(gt + 1) * 128, :], hr[:, C:2 * C])])

    gp_load([(rmsw[:], W["n2rep"]),
             (arena[:, 24576:24640],
              AP(W["rT"].tensor, 0, [[8, 128], [1024, 8], [1, 8]]))], s_wl)
    for grp in range(NG):
        for il in range(TLG):
            gt = grp * TLG + il
            sl = hr[:, (il % 2) * C:(il % 2 + 1) * C]
            sync_load([(sl, h_scr[gt * 128:(gt + 1) * 128, :])])
            rms_tile(sl, pt_sb[:, 0:1024], 8192)
            tr8_to(AP(arena, il * 128, [[AR, 128], [1024, 8], [1, 128]]),
                   pt_sb, 0)
        for il in range(TLG):
            def glmm(e, il=il):
                for ci in range(8):
                    ins = e.matmul(
                        pc[il % 2][:, 0:8],
                        arena[:, ci * 1024 + il * 128:ci * 1024 + (il + 1) * 128],
                        arena[:, 24576 + ci * 8:24576 + (ci + 1) * 8],
                        start=(ci == 0), stop=(ci == 7))
                return ins
            ch.step("tensor", glmm, settle_cycles=128)
            ch.step("vector", lambda e, il=il: e.tensor_copy(
                glog[:, il * 8:(il + 1) * 8], pc[il % 2][:, 0:8]),
                settle_cycles=128)
        EXPc, PRBc, SCRc, WWc = 0, 64, 128, 192
        ESUMc, RSUMc, M1c, M2c = 256, 264, 272, 280

        def v8(col):
            return AP(gb, col, [[320, 128], [8, 8], [1, 8]])

        def rep8(e, dstc, srcc):
            return e.tensor_copy(AP(gb, dstc, [[320, 128], [8, 8], [1, 8]]),
                                 AP(gb, srcc, [[320, 128], [1, 8], [0, 8]]))

        def expst(e):
            for il in range(8):
                ins = e.activation(gb[:, EXPc + il * 8:EXPc + (il + 1) * 8],
                                   glog[:, il * 8:(il + 1) * 8], AF.Exp,
                                   accum_out=gb[:, ESUMc + il:ESUMc + il + 1])
            return ins
        ch.step("scalar", expst)
        ch.step("vector", lambda e: e.reciprocal(gb[:, RSUMc:RSUMc + 8],
                                                 gb[:, ESUMc:ESUMc + 8]))
        ch.step("vector", lambda e: rep8(e, SCRc, RSUMc))
        ch.step("vector", lambda e: e.tensor_tensor(
            gb[:, PRBc:PRBc + 64], gb[:, EXPc:EXPc + 64],
            gb[:, SCRc:SCRc + 64], ALU.mult))
        ch.step("vector", lambda e: e.tensor_reduce(
            gb[:, M1c:M1c + 8], v8(PRBc), axis=AX.X, op=ALU.max))
        ch.step("vector", lambda e: rep8(e, SCRc, M1c))
        ch.step("vector", lambda e: e.tensor_tensor(
            gb[:, WWc:WWc + 64], gb[:, PRBc:PRBc + 64],
            gb[:, SCRc:SCRc + 64], ALU.is_lt))
        ch.step("vector", lambda e: e.tensor_tensor(
            gb[:, WWc:WWc + 64], gb[:, WWc:WWc + 64],
            gb[:, PRBc:PRBc + 64], ALU.mult))
        ch.step("vector", lambda e: e.tensor_reduce(
            gb[:, M2c:M2c + 8], v8(WWc), axis=AX.X, op=ALU.max))
        ch.step("vector", lambda e: rep8(e, SCRc, M2c))
        ch.step("vector", lambda e: e.tensor_tensor(
            gb[:, WWc:WWc + 64], gb[:, PRBc:PRBc + 64],
            gb[:, SCRc:SCRc + 64], ALU.is_ge))
        ch.step("vector", lambda e: e.tensor_tensor(
            gb[:, WWc:WWc + 64], gb[:, WWc:WWc + 64],
            gb[:, PRBc:PRBc + 64], ALU.mult))
        ch.step("vector", lambda e: e.tensor_reduce(
            gb[:, M1c:M1c + 8], v8(WWc), axis=AX.X, op=ALU.add))
        ch.step("vector", lambda e: e.reciprocal(gb[:, M2c:M2c + 8],
                                                 gb[:, M1c:M1c + 8]))
        ch.step("vector", lambda e: rep8(e, SCRc, M2c))
        ch.step("vector", lambda e: e.tensor_tensor(
            gb[:, WWc:WWc + 64], gb[:, WWc:WWc + 64],
            gb[:, SCRc:SCRc + 64], ALU.mult))
        for q in range(2):
            def wtx(e, q=q):
                for j in range(4):
                    ins = e.transpose(
                        pc[1][0:8, j * 128:(j + 1) * 128],
                        gb[:, WWc + (q * 4 + j) * 8:WWc + (q * 4 + j + 1) * 8],
                        b.idf[:])
                return ins
            ch.step("tensor", wtx, settle_cycles=128)
            ch.step("vector", lambda e, q=q: e.tensor_copy(
                wfm_g[0:8, q * 512:(q + 1) * 512], pc[1][0:8, 0:512]),
                settle_cycles=128)
        for eidx in range(8):
            first_e = (eidx == 0)
            for ts in range(2):
                ch.step("tensor", lambda e, eidx=eidx, ts=ts: e.matmul(
                    pc[3][:], bc_sb[0:8, eidx * 128:(eidx + 1) * 128],
                    wfm_g[0:8, ts * 512:(ts + 1) * 512],
                    start=True, stop=True), settle_cycles=128)
                ch.step("vector", lambda e, ts=ts: e.tensor_copy(
                    far[:, 9216 + ts * 512:9216 + (ts + 1) * 512], pc[3][:]),
                    settle_cycles=128)
            for fb4 in range(4):
                gp_load([
                    (wbig[:], AP(W["wgT"].tensor,
                                 eidx * 4194304 + fb4 * 1024,
                                 [[4096, 128], [524288, 8], [1, 1024]])),
                    (wb2[:], AP(W["wuT"].tensor,
                                eidx * 4194304 + fb4 * 1024,
                                [[4096, 128], [524288, 8], [1, 1024]])),
                    (arena[:, 16384:24576],
                     AP(W["wdT"].tensor, eidx * 4194304 + fb4 * 1048576,
                        [[1024, 128], [131072, 8], [1, 1024]]))], s_wl)
                for fc8 in range(8):
                    def gumm(e, fc8=fc8):
                        for dst, wbuf in ((0, wbig), (2, wb2)):
                            for ts in range(2):
                                for ci in range(8):
                                    ins = e.matmul(
                                        pc[dst + ts][:],
                                        wbuf[:, ci * 1024 + fc8 * 128:
                                             ci * 1024 + (fc8 + 1) * 128],
                                        arena[:, ci * 1024 + ts * 512:
                                              ci * 1024 + (ts + 1) * 512],
                                        start=(ci == 0), stop=(ci == 7))
                        return ins
                    ch.step("tensor", gumm, settle_cycles=128)
                    ch.step("scalar", lambda e: (
                        e.activation(far[:, 8192:8704], pc[0][:], AF.Silu),
                        e.activation(far[:, 8704:9216], pc[1][:], AF.Silu))[-1],
                        settle_cycles=128)
                    ch.step("vector", lambda e, fc8=fc8: (
                        e.tensor_tensor(
                            arena[:, 8192 + fc8 * 1024:8192 + fc8 * 1024 + 512],
                            far[:, 8192:8704], pc[2][:], ALU.mult),
                        e.tensor_tensor(
                            arena[:, 8192 + fc8 * 1024 + 512:
                                  8192 + (fc8 + 1) * 1024],
                            far[:, 8704:9216], pc[3][:], ALU.mult))[-1],
                        settle_cycles=128)
                for m0 in range(0, 16, 2):
                    def dnm(e, m0=m0):
                        for k in range(2):
                            cc, ts = divmod(m0 + k, 2)
                            for fc8 in range(8):
                                ins = e.matmul(
                                    pc[4 + k][:],
                                    arena[:, 16384 + fc8 * 1024 + cc * 128:
                                          16384 + fc8 * 1024 + (cc + 1) * 128],
                                    arena[:, 8192 + fc8 * 1024 + ts * 512:
                                          8192 + fc8 * 1024 + (ts + 1) * 512],
                                    start=(fc8 == 0), stop=(fc8 == 7))
                        return ins
                    ch.step("tensor", dnm, settle_cycles=128)

                    init = (first_e and fb4 == 0)

                    def dnmul(e, m0=m0, init=init):
                        for k in range(2):
                            cc, ts = divmod(m0 + k, 2)
                            wsl = far[:, 9216 + ts * 512:9216 + (ts + 1) * 512]
                            dst = (far[:, cc * 1024 + ts * 512:
                                       cc * 1024 + (ts + 1) * 512] if init
                                   else far[:, 8192 + k * 512:
                                            8192 + (k + 1) * 512])
                            ins = e.tensor_tensor(dst, pc[4 + k][:], wsl,
                                                  ALU.mult)
                        return ins
                    ch.step("vector", dnmul, settle_cycles=128)
                    if not init:
                        def dnadd(e, m0=m0):
                            for k in range(2):
                                cc, ts = divmod(m0 + k, 2)
                                dst = far[:, cc * 1024 + ts * 512:
                                          cc * 1024 + (ts + 1) * 512]
                                ins = e.tensor_tensor(
                                    dst, dst,
                                    far[:, 8192 + k * 512:8192 + (k + 1) * 512],
                                    ALU.add)
                            return ins
                        ch.step("vector", dnadd, settle_cycles=128)
        for il in range(TLG):
            gt = grp * TLG + il

            def mtr(e, il=il):
                for cc in range(8):
                    ins = e.transpose(
                        pc[4 + cc // 4][:, (cc % 4) * 128:(cc % 4 + 1) * 128],
                        far[:, cc * 1024 + il * 128:cc * 1024 + (il + 1) * 128],
                        b.idf[:])
                return ins
            ch.step("tensor", mtr, settle_cycles=128)
            sync_load([(hr[:, 0:C], h_scr[gt * 128:(gt + 1) * 128, :]),
                       (far[:, 8192:9216], x_hbm[gt * 128:(gt + 1) * 128, :])])
            ch.step("vector", lambda e: (
                e.tensor_tensor(far[:, 9216:9728], hr[:, 0:512], pc[4][:],
                                ALU.add),
                e.tensor_tensor(far[:, 9728:10240], hr[:, 512:1024], pc[5][:],
                                ALU.add))[-1])
            ch.step("vector", lambda e: (
                e.scalar_tensor_tensor(hr[:, C:C + 512], far[:, 9216:9728],
                                       scale, far[:, 8192:8704],
                                       ALU.mult, ALU.add),
                e.scalar_tensor_tensor(hr[:, C + 512:2 * C],
                                       far[:, 9728:10240], scale,
                                       far[:, 8704:9216],
                                       ALU.mult, ALU.add))[-1])
            gp_store([(xo_hbm[gt * 128:(gt + 1) * 128, :], hr[:, C:2 * C])])
    def drain(e, n=None):
        e.wait_ge(s_hst.h, s_hst.n)
        e.wait_ge(ch.s.h, ch.s.n)
    block.gpsimd(drain)


def f0_host_weights(z):
    qkv = z["L0_ffn_qkv_w"]
    hw = dict(
        wqT=np.ascontiguousarray(qkv[0:1024].T),
        wkT=np.ascontiguousarray(qkv[1024:2048].T),
        wvT=np.ascontiguousarray(qkv[2048:3072].T),
        wpT=np.ascontiguousarray(z["L0_ffn_proj_w"].T),
        rT=np.ascontiguousarray(z["L0_ffn_router_w"].T),
        wgT=np.ascontiguousarray(np.transpose(z["L0_ffn_wg"], (0, 2, 1))),
        wuT=np.ascontiguousarray(np.transpose(z["L0_ffn_wu"], (0, 2, 1))),
        wdT=np.ascontiguousarray(np.transpose(z["L0_ffn_wd"], (0, 2, 1))),
        n1rep=np.ascontiguousarray(
            np.broadcast_to(z["L0_ffn_norm1_w"], (128, 1024))),
        n2rep=np.ascontiguousarray(
            np.broadcast_to(z["L0_ffn_norm2_w"], (128, 1024))),
    )
    inv = 1.0 / (10000.0 ** (np.arange(0, 64, 2, dtype=np.float64) / 64.0))
    t = np.arange(T, dtype=np.float64)
    fr = np.outer(t, inv)
    emb = np.concatenate([fr, fr], axis=1)
    cos, sin = np.cos(emb), np.sin(emb)
    sin_signed = np.concatenate([-sin[:, :32], sin[:, 32:]], axis=1)
    hw["cosr"] = np.ascontiguousarray(np.tile(cos, (1, 16)).astype(np.float32))
    hw["sinr"] = np.ascontiguousarray(
        np.tile(sin_signed, (1, 16)).astype(np.float32))
    q = np.arange(128)
    hw["addm"] = np.where(q[None, :] > q[:, None], NEG30, 0.0).astype(np.float32)
    hw["bc8"] = np.repeat(np.eye(8, dtype=np.float32), 128, axis=1)
    return hw


def build_F0_program(z):
    b = Bld("f0")
    hw = f0_host_weights(z)
    x_in = b.inp("x_in", np.zeros((NTOK, C), np.float32))
    W = {k: b.inp("f0_" + k, v) for k, v in hw.items()}
    xo = b.out("x_out", (NTOK, C), F32)
    with b:
        emit_F0(b, b.block, x_in, xo, W, scale=0.5)
        b.finish()
    return b, hw


def kernel(**inputs):
    z = {k: np.asarray(v) for k, v in inputs.items()}
    x = np.ascontiguousarray(z["L0_attn_x"])
    eye = np.eye(128, dtype=np.float32)
    ones = np.ones((128, 128), np.float32)
    builders = {
        "f0": build_F0_program, "a0": build_A0_program,
        "f1": build_F1_program, "a1": build_A1_program,
        "f2": build_F2_program, "a2": None,
        "f3": build_F3_program, "a3": build_A3_program,
    }
    progs = {}

    def get(name):
        if name not in progs:
            if name == "a2":
                progs[name] = (build_A2_program(None), {})
            else:
                progs[name] = builders[name](z)
        return progs[name]

    def launch(name, xcur):
        b, hw = get(name)
        in_maps = []
        for c in range(NCORES):
            m = {"x_in": np.ascontiguousarray(
                    xcur[2 * c:2 * c + 2].reshape(NTOK, C)),
                 "csts": b.cst_host_array(), "ident": eye, "ones64": ones}
            if name == "a2":
                m["filt"] = np.ascontiguousarray(z["L2_attn_filt"])
            else:
                m.update({name + "_" + k: v for k, v in hw.items()})
            in_maps.append(m)
        res = run_bass_kernel_spmd(b.nc, in_maps, core_ids=list(range(NCORES)))
        return np.stack([r["x_out"].reshape(SEQ_PER_CORE, T, C)
                         for r in res.results]).reshape(B_FULL, T, C)

    for name in ("f0", "a0", "f0", "f1", "a1", "f1",
                 "f2", "a2", "f2", "f3", "a3", "f3"):
        x = launch(name, x)
    return x


# version 03
# speedup vs baseline: 1.0009x; 1.0009x over previous
import math
import numpy as np
import concourse.bass as bass
import concourse.mybir as mybir
from concourse.ap import AP
from concourse.bass_utils import run_bass_kernel_spmd

F32 = mybir.dt.float32
BF16 = mybir.dt.bfloat16
I32 = mybir.dt.int32
AF = mybir.ActivationFunctionType
ALU = mybir.AluOpType
AX = mybir.AxisListType

NCORES = 8
B_FULL, T, C = 16, 2048, 1024
SEQ_PER_CORE = B_FULL // NCORES
NTOK = SEQ_PER_CORE * T
NT = NTOK // 128
TPS = T // 128
LN_EPS = 1e-5
SETTLE = 512


SETTLE_NOP = True


def settle(e, cycles=SETTLE):
    if SETTLE_NOP:
        e.nop(cycle_cnt=cycles)


class Stepper:

    def __init__(self, b, block, tag, extra_waits=()):
        self.b, self.block, self.tag = b, block, tag
        self.s = b.sem(f"st_{tag}")
        self.extra = list(extra_waits)

    def step(self, engine, fn, waits=(), settle_cycles=SETTLE):
        s = self.s
        prev = s.n
        tgt = s.inc()
        extra = self.extra
        self.extra = []

        def emit(e):
            if prev:
                e.wait_ge(s.h, prev)
            for sem, cnt in (list(extra) + list(waits)):
                e.wait_ge(sem.h, cnt)
            settle(e, settle_cycles)
            ins = fn(e)
            assert ins is not None, "step fn must return the last instruction"
            ins.then_inc(s.h, 1)
        getattr(self.block, engine)(emit)
        return tgt


class Sem:

    def __init__(self, nc, name):
        self.h = nc.alloc_semaphore(name=name)
        self.n = 0

    def inc(self, k=1):
        self.n += k
        return self.n


class Bld:

    NSEM = 100

    def __init__(self, name="mk"):
        self.nc = bass.Bass("TRN2", target_bir_lowering=False, name=name)
        self._sems = {}
        self._block = None
        self.inputs = {}

    def sem(self, name) -> Sem:
        if name not in self._sems:
            assert self._pool, "sem pool exhausted"
            s = Sem.__new__(Sem)
            s.h = self._pool.pop(0)
            s.n = 0
            self._sems[name] = s
        return self._sems[name]

    def inp(self, name, arr, dtype=F32):
        h = self.nc.dram_tensor(name, list(arr.shape), dtype, kind="ExternalInput")
        self.inputs[name] = arr
        return h.ap()

    def out(self, name, shape, dtype=F32):
        return self.nc.dram_tensor(name, list(shape), dtype, kind="ExternalOutput").ap()

    def scratch(self, name, shape, dtype=F32):
        if not hasattr(self, "_scr_cache"):
            self._scr_cache = {}
        if name not in self._scr_cache:
            self._scr_cache[name] = self.nc.dram_tensor(
                name, list(shape), dtype, kind="Internal").ap()
        return self._scr_cache[name]

    def sb(self, name, shape, dtype=F32):
        if not hasattr(self, "_sb_cache"):
            self._sb_cache = {}
        if name in self._sb_cache:
            return self._sb_cache[name]
        if getattr(self, "fused", False):
            isz = 2 if dtype == BF16 else 4
            nby = int(np.prod(shape[1:])) * isz
            off = (self.ov_cur + 31) & ~31
            assert off + nby <= self.nc.sbuf_top, \
                f"overlay OOM: {name} needs {nby}B at {off}, top={self.nc.sbuf_top}"
            t = self.nc.alloc_sbuf_tensor_at(name, list(shape), dtype, offset=off)
            self.ov_cur = off + nby
        else:
            t = self.nc.alloc_sbuf_tensor(name, list(shape), dtype)
        self._sb_cache[name] = t
        return t

    def ps(self, name, shape, dtype=F32):
        if not getattr(self, "fused", False):
            return self.nc.alloc_psum_tensor(name, list(shape), dtype)
        sk = (tuple(shape), str(dtype))
        o = self._ps_ord.get(sk, 0)
        self._ps_ord[sk] = o + 1
        key = (sk, o)
        if key not in self._ps_cache:
            self._ps_cache[key] = self.nc.alloc_psum_tensor(
                f"psr{len(self._ps_cache)}", list(shape), dtype)
        return self._ps_cache[key]

    PERSISTENT = ("s_boot", "s_cst", "s_cstg", "s_bar", "s_epoch")

    def begin_sublayer(self):
        if not getattr(self, "fused", False):
            self.fused = True
            self.sem("s_bar")
            self.sem("s_epoch")
            self.ov_base = self.nc.sbuf_base
            self._ps_cache = {}
            nums = sorted(h.num for h in self._all_handles)
            used = {self._sems[k].h.num for k in self._sems}
            free0 = min(n for n in nums if n not in used)
            self.clr_range = range(free0, nums[-1] + 1)
        else:
            self._emit_barrier()
            keep = {k: v for k, v in self._sems.items() if k in self.PERSISTENT}
            self._sems = keep
            self._pool = [h for h in self._all_handles
                          if h.num in self.clr_range]
        self.ov_cur = self.ov_base
        self._ps_ord = {}

    def _emit_barrier(self):
        bar, ep = self.sem("s_bar"), self.sem("s_epoch")
        clr = self.clr_range
        finals = [(s.h, s.n) for k, s in self._sems.items()
                  if k not in self.PERSISTENT and s.n > 0]
        for eng in ("sync", "tensor", "vector", "scalar"):
            def wf(e):
                for h, n in finals:
                    e.wait_ge(h, n)
                e.sem_inc(bar.h, 1)
            getattr(self.block, eng)(wf)
        self.block.gpsimd(lambda e: e.sem_inc(bar.h, 1))
        tgt = bar.inc(5)

        def gp(e):
            e.wait_ge(bar.h, tgt)
            for h, n in finals:
                e.wait_ge(h, n)
            settle(e, 1024)
            e.dma_reset(clr)
            e.sem_clear(clr)
            e.sem_inc(ep.h, 1)
        self.block.gpsimd(gp)
        tgt2 = ep.inc()
        for eng in ("sync", "tensor", "vector", "scalar", "gpsimd"):
            def w(e, tgt2=tgt2):
                e.wait_ge(ep.h, tgt2)
                settle(e)
            getattr(self.block, eng)(w)

    def __enter__(self):
        self._cleanup = self.nc.cleanup_on_exit()
        self._cleanup.__enter__()
        self._pool = [self.nc.alloc_semaphore(name=f"s{i}") for i in range(self.NSEM)]
        self._all_handles = list(self._pool)
        nums = [h.num for h in self._pool]
        self.sem_range = range(min(nums), max(nums) + 1)
        assert len(nums) == len(set(nums)) and max(nums) - min(nums) + 1 == len(nums)
        self.nc.gpsimd.dma_reset(self.sem_range)
        self.nc.gpsimd.sem_clear(self.sem_range)
        self._block = self.nc.Block()
        self.block = self._block.__enter__()
        s_boot = self.sem("s_boot")
        self.block.gpsimd(lambda e: e.sem_inc(s_boot.h, 1))
        s_boot.inc()
        for eng in ("sync", "tensor", "vector", "scalar"):
            getattr(self.block, eng)(lambda e: e.wait_ge(s_boot.h, 1))
        self.cst_vals = {}
        self.cst_tile = self.sb("csts_sb", [128, 64], F32)
        self.cst_in = self.nc.dram_tensor("csts", [128, 64], F32,
                                          kind="ExternalInput").ap()
        self.s_cst = self.sem("s_cst")
        cst_tile, cst_in, s_cst = self.cst_tile, self.cst_in, self.s_cst
        def _ld(e):
            e.dma_start(out=cst_tile[:], in_=cst_in).then_inc(s_cst.h, 16)
        self.block.sync(_ld)
        self.s_cst.inc(16)
        self.idb = self.sb("ident_sb", [128, 128], BF16)
        self.id_in = self.nc.dram_tensor("ident", [128, 128], F32,
                                         kind="ExternalInput").ap()
        self.idf = self.sb("identf_sb", [128, 128], F32)
        self.cst_ones_b = self.sb("ones_b", [128, 128], BF16)
        self.ones_in = self.nc.dram_tensor("ones64", [128, 128], F32,
                                           kind="ExternalInput").ap()
        self.s_cstg = self.sem("s_cstg")
        idb, idf, id_in = self.idb, self.idf, self.id_in
        s_cst, s_cstg = self.s_cst, self.s_cstg
        ones_b, ones_in = self.cst_ones_b, self.ones_in
        def _ldi(e):
            e.dma_start(out=idb[:], in_=id_in).then_inc(s_cstg.h, 16)
            e.dma_start(out=ones_b[:], in_=ones_in).then_inc(s_cstg.h, 16)
        self.block.gpsimd(_ldi)
        self.s_cstg.inc(32)
        def _ldf(e):
            e.dma_start(out=idf[:], in_=id_in).then_inc(s_cst.h, 16)
        self.block.sync(_ldf)
        self.s_cst.inc(16)
        return self

    def cstcol(self, name, value):
        if name not in self.cst_vals:
            self.cst_vals[name] = (len(self.cst_vals), float(value))
        i, v = self.cst_vals[name]
        assert v == float(value)
        return self.cst_tile[:, i:i + 1]

    def cst_host_array(self):
        a = np.zeros((128, 64), np.float32)
        for i, v in self.cst_vals.values():
            a[:, i] = v
        return a

    def __exit__(self, a, bb, c):
        return False

    def finish(self):
        self._block.__exit__(None, None, None)
        self.nc.all_engine_barrier()
        self.nc.gpsimd.dma_reset(self.sem_range)
        self._cleanup.__exit__(None, None, None)


def emit_load_x(b, block, x_hbm, xs, sem, queue="sync"):
    def f(e):
        for i in range(NT):
            e.dma_start(out=xs[:, i * C:(i + 1) * C],
                        in_=x_hbm[i * 128:(i + 1) * 128, :]).then_inc(sem.h, 16)
            sem.inc(16)
    getattr(block, queue)(f)


def emit_ln(b, block, xs, ns, wait_sems, done_sem, tag, n_tiles=NT,
            sq=None, stats=None, gain_rep=None, out_dtype=None):
    nc = b.nc
    if sq is None:
        sq = b.sb(f"lnsq_{tag}", [128, C], F32)
    if stats is None:
        stats = b.sb(f"lnst_{tag}", [128, 8 * n_tiles], F32)
    b.dbg_stats = stats
    ntl = n_tiles
    s_sq = b.sem(f"ln_sq_{tag}")
    s_dv = b.sem(f"ln_dv_{tag}")
    s_st = b.sem(f"ln_std_{tag}")
    s_rb = b.sem(f"ln_rb_{tag}")

    def act1(e):
        for s, cnt in wait_sems:
            e.wait_ge(s.h, cnt)
        for i in range(ntl):
            ins = e.activation(sq[:], xs[:, i * C:(i + 1) * C], AF.Square,
                               accum_out=stats[:, ntl + i:ntl + i + 1])
            ins.then_inc(s_sq.h, 1)
            s_sq.inc()
            if i < ntl - 1:
                e.wait_ge(s_sq.h, s_sq.n)
    block.scalar(act1)

    mu = b.sem(f"ln_mu_{tag}")

    def dve1(e):
        for s, cnt in wait_sems:
            e.wait_ge(s.h, cnt)
        for i in range(ntl):
            e.reduce_sum(stats[:, i:i + 1], xs[:, i * C:(i + 1) * C],
                         axis=AX.X).then_inc(mu.h, 1)
            mu.inc()
    block.vector(dve1)

    lnch = Stepper(b, block, f"lnch_{tag}",
                   extra_waits=[(mu, mu.n), (s_sq, s_sq.n),
                                (b.s_cst, b.s_cst.n)])

    def f1(e):
        e.tensor_scalar_mul(stats[:, 2 * ntl:3 * ntl], stats[:, 0:ntl], 1.0 / C)
        return e.tensor_scalar_mul(stats[:, 3 * ntl:4 * ntl],
                                   stats[:, ntl:2 * ntl], 1.0 / C)
    lnch.step("vector", f1)

    def f2(e):
        e.tensor_tensor(stats[:, 0:ntl], stats[:, 2 * ntl:3 * ntl],
                        stats[:, 2 * ntl:3 * ntl], ALU.mult)
        return e.tensor_scalar_mul(stats[:, ntl:2 * ntl],
                                   stats[:, 2 * ntl:3 * ntl], -1.0)
    lnch.step("vector", f2)
    lnch.step("vector", lambda e: e.tensor_tensor(
        stats[:, 3 * ntl:4 * ntl], stats[:, 3 * ntl:4 * ntl],
        stats[:, 0:ntl], ALU.subtract))
    lnch.step("scalar", lambda e: e.activation(
        stats[:, 4 * ntl:5 * ntl], stats[:, 3 * ntl:4 * ntl], AF.Sqrt,
        bias=b.cstcol("ln_eps", LN_EPS)))
    lnch.step("vector", lambda e: e.reciprocal(
        stats[:, 5 * ntl:6 * ntl], stats[:, 4 * ntl:5 * ntl]))
    lnch.step("vector", lambda e: e.tensor_tensor(
        stats[:, 6 * ntl:7 * ntl], stats[:, ntl:2 * ntl],
        stats[:, 5 * ntl:6 * ntl], ALU.mult))

    def act3(e, nwait=lnch.s.n):
        e.wait_ge(lnch.s.h, nwait)
        settle(e)
        for i in range(ntl):
            ins = e.activation(ns[:, i * C:(i + 1) * C], xs[:, i * C:(i + 1) * C],
                               AF.Identity, scale=stats[:, 5 * ntl + i:5 * ntl + i + 1],
                               bias=stats[:, 6 * ntl + i:6 * ntl + i + 1])
        ins.then_inc(done_sem.h, 1)
        done_sem.inc()
    block.scalar(act3)


def emit_A2(b, block, x_hbm, xo_hbm, filt_hbm):
    nc = b.nc
    xs = b.sb("a2_xs", [128, NT * C], F32)
    ns = b.sb("a2_ns", [128, NT * C], BF16)
    zrow = b.sb("a2_z", [1, 127], F32)
    hload = b.sb("a2_h", [128, 128], F32)
    bt = b.sb("a2_bt", [128, 16 * 128], BF16)
    ot = b.sb("a2_ot", [128, C], F32)
    fz_dram = b.scratch("a2_fzd", [2175], F32)

    s_ld = b.sem("a2_ld")
    s_fz = b.sem("a2_fz")
    s_fzd = b.sem("a2_fzd")
    s_h = b.sem("a2_h")
    s_b = b.sem("a2_b")
    s_n = b.sem("a2_n")
    s_pe = b.sem("a2_pe")
    s_ev = b.sem("a2_ev")
    s_st = b.sem("a2_st")

    emit_load_x(b, block, x_hbm, xs, s_ld, queue="sync")

    def g1(e):
        e.memset(zrow[:], 0.0).then_inc(s_fz.h, 1)
        s_fz.inc()
        e.wait_ge(s_fz.h, s_fz.n)
        zd = AP(fz_dram.tensor, 0, [[127, 1], [1, 127]])
        e.dma_start(out=zd, in_=zrow[0:1, :]).then_inc(s_fzd.h, 16)
        fd = AP(fz_dram.tensor, 127, [[2048, 1], [1, 2048]])
        filt2d = AP(filt_hbm.tensor, 0, [[2048, 1], [1, 2048]])
        e.dma_start(out=fd, in_=filt2d).then_inc(s_fzd.h, 16)
        s_fzd.inc(32)
        e.wait_ge(s_fzd.h, s_fzd.n)
        for d in range(16):
            src = AP(fz_dram.tensor, d * 128, [[1, 128], [1, 128]])
            e.dma_start(out=hload[:], in_=src).then_inc(s_h.h, 16)
            s_h.inc(16)
            e.wait_ge(s_b.h, (d + 1))
    block.gpsimd(g1)

    def dve_b(e):
        for d in range(16):
            e.wait_ge(s_h.h, 16 * (d + 1))
            e.tensor_copy(bt[:, d * 128:(d + 1) * 128], hload[:, 127::-1]).then_inc(s_b.h, 1)
            s_b.inc()
    block.vector(dve_b)

    emit_ln(b, block, xs, ns, [(s_ld, s_ld.n)], s_n, "a2")

    ps0 = b.ps("a2_ps0", [128, 512], F32)
    ps1 = b.ps("a2_ps1", [128, 512], F32)
    ps2 = b.ps("a2_ps2", [128, 512], F32)
    ps3 = b.ps("a2_ps3", [128, 512], F32)
    psl = [ps0, ps1, ps2, ps3]

    def pe(e):
        e.wait_ge(s_n.h, s_n.n)
        e.wait_ge(s_b.h, s_b.n)
        k = 0
        for s in range(SEQ_PER_CORE):
            for i in range(TPS):
                grp = k % 2
                if k >= 2:
                    e.wait_ge(s_ev.h, (k - 1))
                pa, pb_ = psl[2 * grp], psl[2 * grp + 1]
                nd = TPS - i
                for jj, j in enumerate(range(i, TPS)):
                    d = j - i
                    tile = s * TPS + j
                    rhs0 = ns[:, tile * C: tile * C + 512]
                    rhs1 = ns[:, tile * C + 512: (tile + 1) * C]
                    lt = bt[:, d * 128:(d + 1) * 128]
                    st0 = jj == 0
                    sp = jj == nd - 1
                    e.matmul(pa[:], lt, rhs0, start=st0, stop=sp)
                    ins = e.matmul(pb_[:], lt, rhs1, start=st0, stop=sp)
                    if sp:
                        ins.then_inc(s_pe.h, 1)
                k += 1
                s_pe.inc()
    block.tensor(pe)

    def dve_ev(e):
        for k in range(NT):
            grp = k % 2
            pa, pb_ = psl[2 * grp], psl[2 * grp + 1]
            e.wait_ge(s_pe.h, k + 1)
            og = ot[:, 0:C]
            if k >= 1:
                e.wait_ge(s_st.h, 16 * k)
            e.tensor_tensor(og[:, 0:512], pa[:], xs[:, k * C:k * C + 512], ALU.add)
            e.tensor_tensor(og[:, 512:1024], pb_[:], xs[:, k * C + 512:(k + 1) * C],
                            ALU.add).then_inc(s_ev.h, 1)
            s_ev.inc()
    block.vector(dve_ev)

    def store(e):
        for k in range(NT):
            e.wait_ge(s_ev.h, k + 1)
            og = ot[:, 0:C]
            e.dma_start(out=xo_hbm[k * 128:(k + 1) * 128, :], in_=og[:]).then_inc(s_st.h, 16)
            s_st.inc(16)
        e.wait_ge(s_st.h, s_st.n)
    block.sync(store)

    if getattr(b, "debug", False):
        ns_d = b.out("dbg_ns", (128, NT * C), BF16)
        bt_d = b.out("dbg_bt", (128, 16 * 128), BF16)
        st_d = b.out("dbg_stats", (128, 8 * NT), F32)
        xs_d = b.out("dbg_xs", (128, NT * C), F32)
        stats_t = b.nc.virtual_tensors.get("x") if False else None
        s_dbg = b.sem("a2_dbg")
        def dbg(e):
            e.wait_ge(s_n.h, s_n.n)
            e.wait_ge(s_b.h, s_b.n)
            e.dma_start(out=ns_d, in_=ns[:]).then_inc(s_dbg.h, 16)
            e.dma_start(out=bt_d, in_=bt[:]).then_inc(s_dbg.h, 16)
            e.dma_start(out=st_d, in_=b.dbg_stats[:]).then_inc(s_dbg.h, 16)
            e.dma_start(out=xs_d, in_=xs[:]).then_inc(s_dbg.h, 16)
            s_dbg.inc(64)
            e.wait_ge(s_dbg.h, s_dbg.n)
        block.gpsimd(dbg)


def build_A2_program(core_inputs_example, debug=False):
    b = Bld("a2")
    b.debug = debug
    x_in = b.inp("x_in", np.zeros((NTOK, C), np.float32))
    filt = b.inp("filt", np.zeros((T,), np.float32))
    xo = b.out("x_out", (NTOK, C), F32)
    with b:
        emit_A2(b, b.block, x_in, xo, filt)
        b.finish()
    return b




def emit_ln_fm(b, block, x_hbm, n_fm, tag, ps_a, ps_b, also_tm=None, t0=0, nt=NT, pre_waits=()):
    nc = b.nc
    xr = b.sb(f"lnf_xr_{tag}", [128, 4 * C], F32)
    sq = b.sb(f"lnf_sq_{tag}", [128, C], F32)
    stats = b.sb(f"lnf_st_{tag}", [128, 8 * NT], F32)
    nr = also_tm if also_tm is not None else b.sb(f"lnf_nr_{tag}", [128, 2 * C], BF16)
    nslot = (lambda i: i) if also_tm is not None else (lambda i: i % 2)

    s_xa = [b.sem(f"lnf_xa{k}_{tag}") for k in range(4)]
    s_ca = b.sem(f"lnf_ca_{tag}")
    s_cb = b.sem(f"lnf_cb_{tag}")
    s_xb = [b.sem(f"lnf_xb{k}_{tag}") for k in range(4)]
    s_nr = b.sem(f"lnf_nr_{tag}")
    s_tr = b.sem(f"lnf_tr_{tag}")
    s_nfm = b.sem(f"lnf_nfm_{tag}")
    ntl = nt
    xa0 = [s.n for s in s_xa]
    ca0, cb0 = s_ca.n, s_cb.n
    st_chain = Stepper(b, block, f"lnf_{tag}")

    def loads_a(e):
        for sem, cnt in pre_waits:
            e.wait_ge(sem.h, cnt)
        for i in range(ntl):
            if i >= 4:
                e.wait_ge(s_ca.h, ca0 + i - 3)
                e.wait_ge(s_cb.h, cb0 + i - 3)
            sl = s_xa[i % 4]
            e.dma_start(out=xr[:, (i % 4) * C:(i % 4 + 1) * C],
                        in_=x_hbm[(t0 + i) * 128:(t0 + i + 1) * 128, :]).then_inc(sl.h, 16)
            sl.inc(16)
    block.sync(loads_a)

    def act_a(e):
        for i in range(ntl):
            e.wait_ge(s_xa[i % 4].h, xa0[i % 4] + 16 * (i // 4 + 1))
            if i >= 1:
                e.wait_ge(s_ca.h, ca0 + i)
            e.activation(sq[:],
                         xr[:, (i % 4) * C:(i % 4 + 1) * C], AF.Square,
                         accum_out=stats[:, ntl + i:ntl + i + 1]).then_inc(s_ca.h, 1)
            s_ca.inc()
    block.scalar(act_a)

    def dve_a(e):
        for i in range(ntl):
            e.wait_ge(s_xa[i % 4].h, xa0[i % 4] + 16 * (i // 4 + 1))
            e.reduce_sum(stats[:, i:i + 1], xr[:, (i % 4) * C:(i % 4 + 1) * C],
                         axis=AX.X).then_inc(s_cb.h, 1)
            s_cb.inc()
    block.vector(dve_a)

    st_chain.extra = [(s_ca, s_ca.n), (s_cb, s_cb.n)]

    def f1(e):
        e.tensor_scalar_mul(stats[:, 2 * ntl:3 * ntl], stats[:, 0:ntl], 1.0 / C)
        return e.tensor_scalar_mul(stats[:, 3 * ntl:4 * ntl],
                                   stats[:, ntl:2 * ntl], 1.0 / C)
    st_chain.step("vector", f1)

    def f2(e):
        e.tensor_tensor(stats[:, 0:ntl], stats[:, 2 * ntl:3 * ntl],
                        stats[:, 2 * ntl:3 * ntl], ALU.mult)
        return e.tensor_scalar_mul(stats[:, ntl:2 * ntl],
                                   stats[:, 2 * ntl:3 * ntl], -1.0)
    st_chain.step("vector", f2)

    def f3(e):
        return e.tensor_tensor(stats[:, 3 * ntl:4 * ntl], stats[:, 3 * ntl:4 * ntl],
                               stats[:, 0:ntl], ALU.subtract)
    st_chain.step("vector", f3)

    def f4(e):
        return e.activation(stats[:, 4 * ntl:5 * ntl], stats[:, 3 * ntl:4 * ntl],
                            AF.Sqrt, bias=b.cstcol("ln_eps", LN_EPS))
    st_chain.step("scalar", f4, waits=[(b.s_cst, b.s_cst.n)])

    def f5(e):
        return e.reciprocal(stats[:, 5 * ntl:6 * ntl], stats[:, 4 * ntl:5 * ntl])
    st_chain.step("vector", f5)

    def f6(e):
        return e.tensor_tensor(stats[:, 6 * ntl:7 * ntl], stats[:, ntl:2 * ntl],
                               stats[:, 5 * ntl:6 * ntl], ALU.mult)
    stats_done = st_chain.step("vector", f6)
    s_chain = st_chain.s

    xb0 = [s.n for s in s_xb]
    nr0, tr0, nfm0 = s_nr.n, s_tr.n, s_nfm.n

    def loads_b(e):
        e.wait_ge(s_chain.h, stats_done)
        for i in range(ntl):
            if i >= 4:
                e.wait_ge(s_nr.h, nr0 + i - 3)
            sl = s_xb[i % 4]
            e.dma_start(out=xr[:, (i % 4) * C:(i % 4 + 1) * C],
                        in_=x_hbm[(t0 + i) * 128:(t0 + i + 1) * 128, :]).then_inc(sl.h, 16)
            sl.inc(16)
    block.sync(loads_b)

    def act_b(e):
        e.wait_ge(s_chain.h, stats_done)
        settle(e)
        for i in range(ntl):
            e.wait_ge(s_xb[i % 4].h, xb0[i % 4] + 16 * (i // 4 + 1))
            if also_tm is None and i >= 2:
                e.wait_ge(s_tr.h, tr0 + i - 1)
            e.activation(nr[:, nslot(i) * C:(nslot(i) + 1) * C],
                         xr[:, (i % 4) * C:(i % 4 + 1) * C], AF.Identity,
                         scale=stats[:, 5 * ntl + i:5 * ntl + i + 1],
                         bias=stats[:, 6 * ntl + i:6 * ntl + i + 1]).then_inc(s_nr.h, 1)
            s_nr.inc()
    block.scalar(act_b)

    def pe_tr(e):
        e.wait_ge(b.s_cstg.h, b.s_cstg.n)
        for i in range(ntl):
            ps = ps_a if i % 2 == 0 else ps_b
            e.wait_ge(s_nr.h, nr0 + i + 1)
            if i >= 2:
                e.wait_ge(s_nfm.h, nfm0 + i - 1)
            for ch in range(8):
                ins = e.transpose(ps[:, ch * 128:(ch + 1) * 128],
                                  nr[:, nslot(i) * C + ch * 128: nslot(i) * C + (ch + 1) * 128],
                                  b.idb[:])
            ins.then_inc(s_tr.h, 1)
            s_tr.inc()
    block.tensor(pe_tr)

    def dve_b(e):
        for i in range(ntl):
            ps = ps_a if i % 2 == 0 else ps_b
            e.wait_ge(s_tr.h, tr0 + i + 1)
            dst = AP(n_fm, i * 128, [[8 * nt * 128, 128], [nt * 128, 8], [1, 128]])
            e.tensor_copy(dst, AP(ps.tensor if hasattr(ps, "tensor") else ps,
                                  0, [[1024, 128], [128, 8], [1, 128]])
                          ).then_inc(s_nfm.h, 1)
            s_nfm.inc()
    block.vector(dve_b)

    return s_nfm, xr, s_nr


def emit_F1(b, block, x_hbm, xo_hbm, W, scale=0.5, stop_after=None):
    nc = b.nc
    E = 8
    trA = b.ps("f1_trA", [128, 1024], BF16)
    trB = b.ps("f1_trB", [128, 1024], BF16)
    pc = [b.ps(f"f1_pc{k}", [128, 512], F32) for k in range(6)]
    n_fm = b.sb("f1_nfm", [128, 8 * NTOK], BF16)
    gw_sb = b.sb("f1_gw", [128, 64], BF16)
    cw_sb = b.sb("f1_cw", [128, 8 * 1024], BF16)
    bf_sb = b.sb("f1_bf", [128, 64], F32)
    bc_sb = b.sb("f1_bc", [8, 1024], BF16)
    ew_sb = b.sb("f1_ew", [128, 8 * 1024], BF16)
    glog = b.sb("f1_glog", [128, 256], F32)
    gb = b.sb("f1_gb", [128, 1440], F32)
    wfm = b.sb("f1_wfm", [8, 4096], BF16)
    wrep = b.sb("f1_wrep", [128, 1024], F32)
    silu_sb = b.sb("f1_silu", [128, 4 * 512], F32)
    macc = b.sb("f1_macc", [128, 8 * 1024], F32)
    mcb = b.sb("f1_mcb", [128, 2 * 1024], BF16)
    yt = b.sb("f1_yt", [128, 2 * C], F32)

    s_w = b.sem("f1_w")
    s_ew = b.sem("f1_ew")
    s_gl = b.sem("f1_gl")
    s_gev = b.sem("f1_gev")
    s_wt = b.sem("f1_wt")
    s_wfm = b.sem("f1_wfm")
    s_wrp = b.sem("f1_wrp")
    s_wre = b.sem("f1_wre")
    s_eo = b.sem("f1_eo")
    s_si = b.sem("f1_si")
    s_ma = b.sem("f1_ma")
    s_mcb = b.sem("f1_mcb")
    s_y = b.sem("f1_y")
    s_xr2 = [b.sem(f"f1_xr2{k}") for k in range(4)]
    s_ye = b.sem("f1_ye")
    s_st = [b.sem(f"f1_st{k}") for k in range(2)]

    def wloads(e):
        e.dma_start(out=gw_sb[:], in_=AP(W["gwT"].tensor, 0,
                    [[8, 128], [1024, 8], [1, 8]])).then_inc(s_w.h, 16)
        e.dma_start(out=cw_sb[:], in_=AP(W["cwT"].tensor, 0,
                    [[1024, 128], [131072, 8], [1, 1024]])).then_inc(s_w.h, 16)
        e.dma_start(out=bf_sb[:], in_=AP(W["bfm"].tensor, 0,
                    [[8, 128], [1024, 8], [1, 8]])).then_inc(s_w.h, 16)
        e.dma_start(out=bc_sb[:], in_=W["bc8"]).then_inc(s_w.h, 16)
        s_w.inc(64)
    block.gpsimd(wloads)

    s_nfm, xr2, s_lnnr = emit_ln_fm(b, block, x_hbm, n_fm, "f1", trA, trB)
    nfm_done = s_nfm.n
    lnnr_done = s_lnnr.n

    if stop_after == "nfm":
        dbgo = b.out("dbg", (128, 8 * NTOK), BF16)
        s_dbg = b.sem("f1_dbg")
        def dbg(e):
            e.wait_ge(s_nfm.h, nfm_done)
            e.dma_start(out=dbgo, in_=n_fm[:]).then_inc(s_dbg.h, 16)
            s_dbg.inc(16)
            e.wait_ge(s_dbg.h, s_dbg.n)
        block.sync(dbg)
        return

    gl0 = s_gl.n

    def pe_glog(e):
        e.wait_ge(s_nfm.h, nfm_done)
        e.wait_ge(s_w.h, s_w.n)
        for i in range(NT):
            half = 0 if i < 16 else 1
            for ch in range(8):
                ins = e.matmul(pc[half][:, (i % 16) * 8:(i % 16) * 8 + 8],
                               n_fm[:, ch * NTOK + i * 128: ch * NTOK + (i + 1) * 128],
                               gw_sb[:, ch * 8:(ch + 1) * 8],
                               start=(ch == 0), stop=(ch == 7))
            ins.then_inc(s_gl.h, 1)
            s_gl.inc()
    block.tensor(pe_glog)

    def dve_gev(e):
        e.wait_ge(s_gl.h, s_gl.n)
        settle(e)
        e.tensor_copy(glog[:, 0:128], pc[0][:, 0:128])
        ins = e.tensor_copy(glog[:, 128:256], pc[1][:, 0:128])
        ins.then_inc(s_gev.h, 1)
        s_gev.inc()
    block.vector(dve_gev)

    if stop_after == "glog":
        dbgo = b.out("dbg", (128, 256), F32)
        s_dbg = b.sem("f1_dbg")
        def dbg(e):
            e.wait_ge(s_gev.h, s_gev.n)
            e.dma_start(out=dbgo, in_=glog[:]).then_inc(s_dbg.h, 16)
            s_dbg.inc(16)
            e.wait_ge(s_dbg.h, s_dbg.n)
        block.sync(dbg)
        return

    g = Stepper(b, block, "f1g", extra_waits=[(s_gev, s_gev.n)])
    EXP, PRB, SCR, WW = 0, 256, 512, 768
    ESUM, RSUM, M1, M2, REP = 1024, 1056, 1088, 1120, 1152

    def g1(e):
        for i in range(NT):
            ins = e.activation(gb[:, EXP + i * 8:EXP + (i + 1) * 8],
                               glog[:, i * 8:(i + 1) * 8], AF.Exp,
                               accum_out=gb[:, ESUM + i:ESUM + i + 1])
        return ins
    g.step("scalar", g1)

    def g2(e):
        return e.reciprocal(gb[:, RSUM:RSUM + 32], gb[:, ESUM:ESUM + 32])
    g.step("vector", g2)

    def rep32(e, dstc, srcc):
        src = AP(gb.tensor if hasattr(gb, "tensor") else gb, srcc,
                 [[1440, 128], [1, 32], [0, 8]])
        dst = AP(gb.tensor if hasattr(gb, "tensor") else gb, dstc,
                 [[1440, 128], [8, 32], [1, 8]])
        return e.tensor_copy(dst, src)

    g.step("vector", lambda e: rep32(e, REP, RSUM))
    g.step("vector", lambda e: e.tensor_tensor(gb[:, PRB:PRB + 256],
           gb[:, EXP:EXP + 256], gb[:, REP:REP + 256], ALU.mult))

    def g5(e):
        return e.tensor_reduce(gb[:, M1:M1 + 32],
                               AP(gb.tensor if hasattr(gb, "tensor") else gb,
                                  PRB, [[1440, 128], [8, 32], [1, 8]]),
                               axis=AX.X, op=ALU.max)
    g.step("vector", g5)
    g.step("vector", lambda e: rep32(e, REP, M1))
    g.step("vector", lambda e: e.tensor_tensor(gb[:, SCR:SCR + 256],
           gb[:, PRB:PRB + 256], gb[:, REP:REP + 256], ALU.is_lt))
    g.step("vector", lambda e: e.tensor_tensor(gb[:, SCR:SCR + 256],
           gb[:, SCR:SCR + 256], gb[:, PRB:PRB + 256], ALU.mult))

    def g9(e):
        return e.tensor_reduce(gb[:, M2:M2 + 32],
                               AP(gb.tensor if hasattr(gb, "tensor") else gb,
                                  SCR, [[1440, 128], [8, 32], [1, 8]]),
                               axis=AX.X, op=ALU.max)
    g.step("vector", g9)
    g.step("vector", lambda e: rep32(e, REP, M2))
    g.step("vector", lambda e: e.tensor_tensor(gb[:, SCR:SCR + 256],
           gb[:, PRB:PRB + 256], gb[:, REP:REP + 256], ALU.is_ge))
    gate_done = g.step("vector", lambda e: e.tensor_tensor(gb[:, WW:WW + 256],
           gb[:, PRB:PRB + 256], gb[:, SCR:SCR + 256], ALU.mult))
    s_g = g.s

    if stop_after == "gate":
        dbgo = b.out("dbg", (128, 1440), F32)
        s_dbg = b.sem("f1_dbg")
        def dbg(e):
            e.wait_ge(s_g.h, gate_done)
            e.dma_start(out=dbgo, in_=gb[:]).then_inc(s_dbg.h, 16)
            s_dbg.inc(16)
            e.wait_ge(s_dbg.h, s_dbg.n)
        block.sync(dbg)
        return

    wt0 = s_wt.n
    wfm0 = s_wfm.n

    def pe_wt(e):
        e.wait_ge(s_g.h, gate_done)
        for q in range(8):
            if q >= 1:
                e.wait_ge(s_wfm.h, wfm0 + q)
            for j in range(4):
                i = q * 4 + j
                ins = e.transpose(pc[2][0:8, j * 128:(j + 1) * 128],
                                  gb[:, WW + i * 8:WW + (i + 1) * 8], b.idf[:])
            ins.then_inc(s_wt.h, 1)
            s_wt.inc()
    block.tensor(pe_wt)

    def dve_wfm(e):
        for q in range(8):
            e.wait_ge(s_wt.h, wt0 + q + 1)
            settle(e)
            e.tensor_copy(wfm[:, q * 512:(q + 1) * 512],
                          pc[2][0:8, 0:512]).then_inc(s_wfm.h, 1)
            s_wfm.inc()
    block.vector(dve_wfm)
    wfm_done = s_wfm.n
    ew0, eo0 = s_ew.n, s_eo.n
    si0, ma0, wr0, wre0 = s_si.n, s_ma.n, s_wrp.n, s_wre.n
    mcb0, y0, ye0 = s_mcb.n, s_y.n, s_ye.n
    st0 = [x.n for x in s_st]
    xr20 = [x.n for x in s_xr2]

    if stop_after == "wfm":
        dbgo = b.out("dbg", (8, 4096), BF16)
        s_dbg = b.sem("f1_dbg")
        def dbg(e):
            e.wait_ge(s_wfm.h, wfm_done)
            e.dma_start(out=dbgo, in_=wfm[:]).then_inc(s_dbg.h, 16)
            s_dbg.inc(16)
            e.wait_ge(s_dbg.h, s_dbg.n)
        block.sync(dbg)
        return

    NG = 4
    TPG = NTOK // NG
    SPG = TPG // 512
    TLG = TPG // 128
    NLOAD = NG * 8
    single = stop_after in ("moe1", "e0", "moe")
    combine_on = (not single) or stop_after == "c1"
    if stop_after == "c1":
        single = True
    NGRUN = NG if (not single or stop_after == "moe") else 1
    if stop_after == "c1": NGRUN = 1
    NEXP = 8 if stop_after not in ("e0", "c1") else 1
    NLOAD_EFF = NGRUN * NEXP
    ew0 = s_ew.n
    eo0 = s_eo.n

    def pe_main(e):
        e.wait_ge(s_wfm.h, wfm_done)
        for grp in range(NGRUN):
            for eidx in range(NEXP):
                idx = grp * 8 + eidx
                e.wait_ge(s_ew.h, ew0 + 16 * (idx + 1))
                for ts4 in range(SPG):
                    bank = pc[2 + ts4 % 2]
                    g = idx * SPG + ts4
                    if g >= 2:
                        e.wait_ge(s_wre.h, wre0 + g - 1)
                    e.matmul(bank[:], bc_sb[:, eidx * 128:(eidx + 1) * 128],
                             wfm[0:8, grp * TPG + ts4 * 512: grp * TPG + (ts4 + 1) * 512],
                             start=True, stop=True).then_inc(s_wrp.h, 1)
                    s_wrp.inc()
                for co in range(8):
                    for ts in range(SPG):
                        k = co * SPG + ts
                        bank = pc[4 + k % 2]
                        if idx * 8 * SPG + k >= 2:
                            e.wait_ge(s_si.h, si0 + idx * 8 * SPG + k - 1)
                        for ci in range(8):
                            ins = e.matmul(bank[:],
                                           ew_sb[:, ci * 1024 + co * 128: ci * 1024 + (co + 1) * 128],
                                           n_fm[:, ci * NTOK + grp * TPG + ts * 512:
                                                ci * NTOK + grp * TPG + (ts + 1) * 512],
                                           start=(ci == 0), stop=(ci == 7))
                        ins.then_inc(s_eo.h, 1)
                        s_eo.inc()
            if not combine_on:
                continue
            for il in range(TLG):
                i = grp * TLG + il
                e.wait_ge(s_mcb.h, mcb0 + i + 1)
                settle(e, 256)
                for ch2 in range(2):
                    bank = pc[0 + ch2]
                    if i >= 1:
                        e.wait_ge(s_ye.h, ye0 + i)
                    for k1 in range(8):
                        ins = e.matmul(bank[:],
                                       mcb[:, (i % 2) * 1024 + k1 * 128:(i % 2) * 1024 + (k1 + 1) * 128],
                                       cw_sb[:, k1 * 1024 + ch2 * 512: k1 * 1024 + (ch2 + 1) * 512],
                                       start=(k1 == 0), stop=(k1 == 7))
                    ins.then_inc(s_y.h, 1)
                    s_y.inc()
    block.tensor(pe_main)

    def act_si(e):
        for idx in range(NLOAD_EFF if single else NLOAD):
            eidx = idx % 8
            for co in range(8):
                for ts in range(SPG):
                    k = co * SPG + ts
                    bank = pc[4 + k % 2]
                    e.wait_ge(s_eo.h, eo0 + idx * 8 * SPG + k + 1)
                    if idx * 8 * SPG + k >= 4:
                        e.wait_ge(s_ma.h, ma0 + idx * 8 * SPG + k - 3)
                    e.activation(silu_sb[:, (k % 4) * 512:(k % 4 + 1) * 512],
                                 bank[:], AF.Silu,
                                 bias=bf_sb[:, eidx * 8 + co:eidx * 8 + co + 1]
                                 ).then_inc(s_si.h, 1)
                    s_si.inc()
    block.scalar(act_si)

    mtmp = b.sb("f1_mtmp", [128, 2 * 512], F32)
    s_mm = b.sem("f1_mm")
    mm0 = s_mm.n

    def xr2_loads(e):
        if not combine_on:
            return
        e.wait_ge(s_lnnr.h, lnnr_done)
        for i in range(NT if stop_after != "c1" else 8):
            if i >= 4:
                e.wait_ge(s_ye.h, ye0 + i - 3)
            sl = s_xr2[i % 4]
            e.dma_start(out=xr2[:, (i % 4) * C:(i % 4 + 1) * C],
                        in_=x_hbm[i * 128:(i + 1) * 128, :]).then_inc(sl.h, 16)
            sl.inc(16)
    block.sync(xr2_loads)

    def ew_load(e, idx):
        eidx = idx % 8
        if idx > 0:
            e.wait_ge(s_eo.h, eo0 + 8 * SPG * idx)
        e.dma_start(out=ew_sb[:], in_=AP(W["ewT"].tensor, eidx * 1024 * 1024,
                    [[1024, 128], [131072, 8], [1, 1024]])).then_inc(s_ew.h, 16)
        s_ew.inc(16)

    def pool_main(e):
        ew_load(e, 0)
        kh = 0
        for grp in range(NGRUN):
            for eidx in range(NEXP):
                idx = grp * 8 + eidx
                e.wait_ge(s_wre.h, wre0 + (idx + 1) * SPG)
                settle(e)
                for co in range(8):
                    for ts in range(SPG):
                        k = co * SPG + ts
                        e.wait_ge(s_si.h, si0 + idx * 8 * SPG + k + 1)
                        settle(e, 256)
                        if kh >= 2:
                            e.wait_ge(s_ma.h, ma0 + kh - 1)
                        e.tensor_tensor(mtmp[:, (kh % 2) * 512:(kh % 2 + 1) * 512],
                                        silu_sb[:, (k % 4) * 512:(k % 4 + 1) * 512],
                                        wrep[:, ts * 512:(ts + 1) * 512],
                                        ALU.mult).then_inc(s_mm.h, 1)
                        s_mm.inc()
                        kh += 1
                if idx + 1 < NLOAD_EFF if single else idx + 1 < NLOAD:
                    ew_load(e, idx + 1)
            if not combine_on:
                continue
            for il in range(TLG):
                i = grp * TLG + il
                e.wait_ge(s_ye.h, ye0 + i + 1)
                sl = s_st[i % 2]
                e.dma_start(out=xo_hbm[i * 128:(i + 1) * 128, :],
                            in_=yt[:, (i % 2) * C:(i % 2 + 1) * C]).then_inc(sl.h, 16)
                sl.inc(16)
        if combine_on:
            e.wait_ge(s_st[0].h, s_st[0].n)
            e.wait_ge(s_st[1].h, s_st[1].n)
    block.gpsimd(pool_main)

    def dve_main(e):
        kh = 0
        for grp in range(NGRUN):
            for eidx in range(NEXP):
                idx = grp * 8 + eidx
                e.wait_ge(s_ma.h, ma0 + idx * 8 * SPG)
                for ts4 in range(SPG):
                    g = idx * SPG + ts4
                    e.wait_ge(s_wrp.h, wr0 + g + 1)
                    settle(e)
                    e.tensor_copy(wrep[:, ts4 * 512:(ts4 + 1) * 512],
                                  pc[2 + ts4 % 2][:]).then_inc(s_wre.h, 1)
                    s_wre.inc()
                for co in range(8):
                    for ts in range(SPG):
                        k = co * SPG + ts
                        e.wait_ge(s_mm.h, mm0 + kh + 1)
                        settle(e, 256)
                        dst = macc[:, co * TPG + ts * 512: co * TPG + (ts + 1) * 512]
                        sl = mtmp[:, (kh % 2) * 512:(kh % 2 + 1) * 512]
                        if eidx == 0:
                            e.tensor_copy(dst, sl).then_inc(s_ma.h, 1)
                        else:
                            e.tensor_tensor(dst, dst, sl, ALU.add).then_inc(s_ma.h, 1)
                        s_ma.inc()
                        kh += 1
            if not combine_on:
                continue
            e.wait_ge(s_ma.h, ma0 + (grp + 1) * NEXP * 8 * SPG)
            settle(e)
            def mcb_cast(il):
                i = grp * TLG + il
                if i >= 2:
                    e.wait_ge(s_y.h, y0 + 2 * (i - 1))
                src = AP(macc, il * 128, [[8 * TPG, 128], [TPG, 8], [1, 128]])
                dst = AP(mcb, (i % 2) * 1024, [[2048, 128], [128, 8], [1, 128]])
                e.tensor_copy(dst, src).then_inc(s_mcb.h, 1)
                s_mcb.inc()

            def ye(il):
                i = grp * TLG + il
                e.wait_ge(s_y.h, y0 + 2 * (i + 1))
                e.wait_ge(s_xr2[i % 4].h, xr20[i % 4] + 16 * (i // 4 + 1))
                settle(e, 256)
                if i >= 2:
                    e.wait_ge(s_st[i % 2].h, st0[i % 2] + 16 * (i // 2))
                sl = yt[:, (i % 2) * C:(i % 2 + 1) * C]
                e.scalar_tensor_tensor(sl[:, 0:512], pc[0][:], scale,
                                       xr2[:, (i % 4) * C:(i % 4) * C + 512],
                                       ALU.mult, ALU.add)
                e.scalar_tensor_tensor(sl[:, 512:1024], pc[1][:], scale,
                                       xr2[:, (i % 4) * C + 512:(i % 4 + 1) * C],
                                       ALU.mult, ALU.add).then_inc(s_ye.h, 1)
                s_ye.inc()

            mcb_cast(0)
            mcb_cast(1)
            for il in range(TLG):
                ye(il)
                if il + 2 < TLG:
                    mcb_cast(il + 2)
    block.vector(dve_main)

    if single and not combine_on:
        def dump(e):
            e.wait_ge(s_ma.h, s_ma.n)
            dbgo = b.out("dbg", (128, 8 * 1024), F32)
            dbg2 = b.out("dbg2", (128, 1024), F32)
            dbg3 = b.out("dbg3", (128, 2048), F32)
            s_dbg = b.sem("f1_dbg")
            e.dma_start(out=dbgo, in_=macc[:]).then_inc(s_dbg.h, 16)
            e.dma_start(out=dbg2, in_=wrep[:]).then_inc(s_dbg.h, 16)
            e.dma_start(out=dbg3, in_=silu_sb[:]).then_inc(s_dbg.h, 16)
            s_dbg.inc(48)
            e.wait_ge(s_dbg.h, s_dbg.n)
        block.sync(dump)

def f1_host_weights(z):
    gw = z["L1_ffn_gate_w"]; ew = z["L1_ffn_expert_w"]
    eb = z["L1_ffn_expert_b"]; cw = z["L1_ffn_combine_w"]
    gwT = np.ascontiguousarray(gw.T)
    ewT = np.ascontiguousarray(np.transpose(ew, (0, 2, 1)))
    cwT = np.ascontiguousarray(cw.T)
    bfm = np.ascontiguousarray(eb.reshape(8, 8, 128).transpose(0, 2, 1))
    bc8 = np.zeros((8, 1024), np.float32)
    for e in range(8):
        bc8[e, e * 128:(e + 1) * 128] = 1.0
    return dict(gwT=gwT, ewT=ewT, cwT=cwT, bfm=bfm, bc8=bc8)


def build_F1_program(z, stop_after=None):
    b = Bld("f1")
    hw = f1_host_weights(z)
    x_in = b.inp("x_in", np.zeros((NTOK, C), np.float32))
    W = {k: b.inp("f1_" + k, v) for k, v in hw.items()}
    xo = b.out("x_out", (NTOK, C), F32)
    with b:
        emit_F1(b, b.block, x_in, xo, W, scale=0.5, stop_after=stop_after)
        b.finish()
    return b, hw


def emit_A0(b, block, x_hbm, xo_hbm, W):
    nc = b.nc
    H = 16
    EPSL = 1.1920929e-07
    trA = b.ps("a0_trA", [128, 1024], BF16)
    trB = b.ps("a0_trB", [128, 1024], BF16)
    pc0 = b.ps("a0_pc0", [128, 512], F32)
    pc1 = b.ps("a0_pc1", [128, 512], F32)
    pctxA = b.ps("a0_pctxA", [128, 512], F32)
    pctxB = b.ps("a0_pctxB", [128, 512], F32)
    pz = b.ps("a0_pz", [128, 512], F32)
    n_fm = b.sb("a0_nfm", [128, 8 * T], BF16)
    kv_tm = b.sb("a0_kv", [128, 2 * 2048], BF16)
    qh = b.sb("a0_qh", [128, 2048], BF16)
    qz_sb = b.sb("a0_qz", [128, 2 * 512], BF16)
    wq_sb = b.sb("a0_wq", [128, 8 * 1024], BF16)
    wkv_sb = b.sb("a0_wkv", [128, 8 * 2048], BF16)
    wp_sb = b.sb("a0_wp", [128, 8 * 1024], BF16)
    br_sb = b.sb("a0_br", [128, 1024], F32)
    kc_sb = b.sb("a0_kc", [128, 16 * 64 + 16], BF16)
    cx_acc = b.sb("a0_cxa", [128, 16 * 64 + 16], F32)
    zr_sb = b.sb("a0_zr", [1, 512], BF16)
    zf_sb = b.sb("a0_zf", [1, 512], F32)
    zf2_sb = b.sb("a0_zf2", [1, 512], F32)
    ao_fm = b.sb("a0_ao", [128, 8 * T], BF16)
    yt = b.sb("a0_yt", [128, 2 * C], F32)

    s_w = b.sem("a0_w")
    s_qkv = b.sem("a0_qkv")
    s_qke = b.sem("a0_qke")
    s_cx = b.sem("a0_cx")
    s_cxe = b.sem("a0_cxe")
    s_qp = b.sem("a0_qp")
    s_qpe = b.sem("a0_qpe")
    s_zp = b.sem("a0_zp")
    s_ze = b.sem("a0_ze")
    s_zb = b.sem("a0_zb")
    s_qz = b.sem("a0_qz")
    s_ov = b.sem("a0_ov")
    s_ae = b.sem("a0_ae")
    s_y = b.sem("a0_y")
    s_ye = b.sem("a0_ye")
    s_t1 = b.sem("a0_t1")
    s_st = [b.sem(f"a0_st{k}") for k in range(2)]
    s_xr2 = [b.sem(f"a0_xr2{k}") for k in range(4)]

    def wloads(e):
        e.dma_start(out=wq_sb[:], in_=AP(W["wqkvT"].tensor, 0,
                    [[3072, 128], [393216, 8], [1, 1024]])).then_inc(s_w.h, 16)
        e.dma_start(out=wkv_sb[:], in_=AP(W["wqkvT"].tensor, 1024,
                    [[3072, 128], [393216, 8], [1, 2048]])).then_inc(s_w.h, 16)
        e.dma_start(out=wp_sb[:], in_=AP(W["wprojT"].tensor, 0,
                    [[1024, 128], [131072, 8], [1, 1024]])).then_inc(s_w.h, 16)
        s_w.inc(48)
    block.gpsimd(wloads)

    def brload(e):
        e.dma_start(out=br_sb[:], in_=W["brep"]).then_inc(s_w.h, 16)
        s_w.inc(16)
    block.gpsimd(brload)
    w_done = s_w.n

    y0, ye0, t10 = s_y.n, s_ye.n, s_t1.n
    st0 = [x.n for x in s_st]
    xr20 = [x.n for x in s_xr2]
    qkv0, qke0, cx0, cxe0 = s_qkv.n, s_qke.n, s_cx.n, s_cxe.n
    qp0, qpe0, zp0, ze0, zb0, qz0, ov0, ae0 = (s_zp.n, s_qpe.n, s_zp.n, s_ze.n,
                                               s_zb.n, s_qz.n, s_ov.n, s_ae.n)
    qp0 = s_qp.n

    lnnr_dones = []
    xr_buf = [None]

    for sq in range(SEQ_PER_CORE):
        s_nfm, xr2, s_lnnr = emit_ln_fm(b, block, x_hbm, n_fm, "a0", trA, trB,
                                        t0=sq * TPS, nt=TPS,
                                        pre_waits=([(s_ye, ye0 + sq * TPS)]
                                                   if sq else []))
        xr_buf[0] = xr2
        nfm_done = s_nfm.n
        lnnr_dones.append(s_lnnr.n)

        def pe_seq(e, sq=sq, nfm_done=nfm_done):
            e.wait_ge(s_nfm.h, nfm_done)
            e.wait_ge(s_w.h, w_done)
            e.wait_ge(b.s_cstg.h, b.s_cstg.n)
            if sq >= 1:
                e.wait_ge(s_ye.h, ye0 + sq * TPS)
                e.wait_ge(s_cxe.h, cxe0 + sq * (TPS + 1))
            settle(e)
            for it in range(TPS):
                for sl4 in range(4):
                    kq = (sq * TPS + it) * 4 + sl4
                    bank = pc0 if kq % 2 == 0 else pc1
                    if kq >= 2:
                        e.wait_ge(s_qke.h, qke0 + kq - 1)
                    for ci in range(8):
                        ins = e.matmul(bank[:],
                                       n_fm[:, ci * T + it * 128:ci * T + (it + 1) * 128],
                                       wkv_sb[:, ci * 2048 + sl4 * 512:ci * 2048 + (sl4 + 1) * 512],
                                       start=(ci == 0), stop=(ci == 7))
                    ins.then_inc(s_qkv.h, 1)
                    s_qkv.inc()
                e.wait_ge(s_qke.h, qke0 + (sq * TPS + it + 1) * 4)
                if it >= 1:
                    e.wait_ge(s_cxe.h, cxe0 + sq * (TPS + 1) + it)
                settle(e)
                slot = it % 2
                for h in range(H):
                    kt = kv_tm[:, slot * 2048 + h * 64: slot * 2048 + (h + 1) * 64]
                    vt = kv_tm[:, slot * 2048 + 1024 + h * 64: slot * 2048 + 1024 + (h + 1) * 64]
                    e.matmul(pz[0:64, h:h + 1], kt, b.cst_ones_b[:, 0:1],
                             start=True, stop=True)
                    pcx = pctxA if h < 8 else pctxB
                    ins = e.matmul(pcx[0:64, (h % 8) * 64:(h % 8 + 1) * 64],
                                   kt, vt, start=True, stop=True)
                ins.then_inc(s_cx.h, 1)
                s_cx.inc()
            e.wait_ge(s_cxe.h, cxe0 + sq * (TPS + 1) + TPS + 1)
            settle(e)
            for h in range(H):
                hh = sq * H + h
                e.wait_ge(s_ae.h, ae0 + hh * 4)
                for sl4 in range(4):
                    gq = hh * 4 + sl4
                    bank = pc0 if gq % 2 == 0 else pc1
                    if gq >= 2:
                        e.wait_ge(s_qpe.h, qpe0 + gq - 1)
                    for ci in range(8):
                        ins = e.matmul(bank[0:64, :],
                                       wq_sb[:, ci * 1024 + h * 64:ci * 1024 + (h + 1) * 64],
                                       n_fm[:, ci * T + sl4 * 512:ci * T + (sl4 + 1) * 512],
                                       start=(ci == 0), stop=(ci == 7))
                    ins.then_inc(s_qp.h, 1)
                    s_qp.inc()
                e.wait_ge(s_qpe.h, qpe0 + (hh + 1) * 4)
                settle(e)
                for sl4 in range(4):
                    g = hh * 4 + sl4
                    qsl = qh[0:64, sl4 * 512:(sl4 + 1) * 512]
                    e.matmul(pz[0:1, 0:512], kc_sb[0:64, 1024 + h:1024 + h + 1], qsl,
                             start=True, stop=True).then_inc(s_zp.h, 1)
                    s_zp.inc()
                    e.wait_ge(s_ze.h, ze0 + 3 * g + 3)
                    settle(e)
                    e.matmul(pz[0:64, 0:512], b.cst_ones_b[0:1, 0:64],
                             zr_sb[0:1, :], start=True, stop=True).then_inc(s_zb.h, 1)
                    s_zb.inc()
                    e.wait_ge(s_qz.h, qz0 + g + 1)
                    settle(e)
                    bank = pc0 if g % 2 == 0 else pc1
                    if g >= 2:
                        e.wait_ge(s_ae.h, ae0 + g - 1)
                    e.matmul(bank[0:64, :], kc_sb[0:64, h * 64:(h + 1) * 64],
                             qz_sb[0:64, (g % 2) * 512:(g % 2 + 1) * 512],
                             start=True, stop=True).then_inc(s_ov.h, 1)
                    s_ov.inc()
            e.wait_ge(s_ae.h, ae0 + (sq + 1) * H * 4)
            settle(e)
            for it in range(TPS):
                i = sq * TPS + it
                for ch2 in range(2):
                    bank = pc0 if ch2 == 0 else pc1
                    if i >= 1:
                        e.wait_ge(s_ye.h, ye0 + i)
                    for k1 in range(8):
                        ins = e.matmul(bank[:],
                                       ao_fm[:, k1 * T + it * 128:k1 * T + (it + 1) * 128],
                                       wp_sb[:, k1 * 1024 + ch2 * 512:k1 * 1024 + (ch2 + 1) * 512],
                                       start=(k1 == 0), stop=(k1 == 7))
                    ins.then_inc(s_y.h, 1)
                    s_y.inc()
        block.tensor(pe_seq)

        def act_seq(e, sq=sq):
            for it in range(TPS):
                slot = it % 2
                for sl4 in range(4):
                    kq = (sq * TPS + it) * 4 + sl4
                    bank = pc0 if kq % 2 == 0 else pc1
                    e.wait_ge(s_qkv.h, qkv0 + kq + 1)
                    if it >= 2 and sl4 == 0:
                        e.wait_ge(s_cx.h, cx0 + sq * TPS + it - 1)
                    fn = AF.Relu if sl4 < 2 else AF.Identity
                    e.activation(kv_tm[:, slot * 2048 + sl4 * 512:slot * 2048 + (sl4 + 1) * 512],
                                 bank[:], fn).then_inc(s_qke.h, 1)
                    s_qke.inc()
            for h in range(H):
                hh = sq * H + h
                for sl4 in range(4):
                    gq = hh * 4 + sl4
                    bank = pc0 if gq % 2 == 0 else pc1
                    e.wait_ge(s_qp.h, qp0 + gq + 1)
                    if sl4 == 0 and hh >= 1:
                        e.wait_ge(s_qz.h, qz0 + hh * 4)
                    e.activation(qh[0:64, sl4 * 512:(sl4 + 1) * 512], bank[0:64, :],
                                 AF.Relu).then_inc(s_qpe.h, 1)
                    s_qpe.inc()
        block.scalar(act_seq)

        def dve_seq(e, sq=sq):
            if sq == 0:
                e.memset(zr_sb[:], 0.0)
                e.memset(zf_sb[:], 0.0)
                e.memset(zf2_sb[:], 0.0)
            for it in range(TPS):
                e.wait_ge(s_cx.h, cx0 + sq * TPS + it + 1)
                settle(e)
                if it == 0:
                    e.tensor_copy(cx_acc[0:64, 0:512], pctxA[0:64, :])
                    e.tensor_copy(cx_acc[0:64, 512:1024], pctxB[0:64, :])
                    e.tensor_copy(cx_acc[0:64, 1024:1040],
                                  pz[0:64, 0:16]).then_inc(s_cxe.h, 1)
                else:
                    e.tensor_tensor(cx_acc[0:64, 0:512], cx_acc[0:64, 0:512],
                                    pctxA[0:64, :], ALU.add)
                    e.tensor_tensor(cx_acc[0:64, 512:1024],
                                    cx_acc[0:64, 512:1024],
                                    pctxB[0:64, :], ALU.add)
                    e.tensor_tensor(cx_acc[0:64, 1024:1040], cx_acc[0:64, 1024:1040],
                                    pz[0:64, 0:16], ALU.add).then_inc(s_cxe.h, 1)
                s_cxe.inc()
            e.wait_ge(s_cxe.h, s_cxe.n)
            settle(e)
            e.tensor_copy(kc_sb[0:64, :], cx_acc[0:64, :]).then_inc(s_cxe.h, 1)
            s_cxe.inc()
            for h in range(H):
                hh = sq * H + h
                for sl4 in range(4):
                    g = hh * 4 + sl4
                    e.wait_ge(s_zp.h, zp0 + g + 1)
                    settle(e)
                    e.tensor_scalar_add(zf_sb[0:1, :], pz[0:1, 0:512],
                                        EPSL).then_inc(s_ze.h, 1)
                    s_ze.inc()
                    e.wait_ge(s_ze.h, s_ze.n)
                    settle(e)
                    e.reciprocal(zf2_sb[0:1, :], zf_sb[0:1, :]).then_inc(s_ze.h, 1)
                    s_ze.inc()
                    e.wait_ge(s_ze.h, s_ze.n)
                    e.wait_ge(s_zb.h, zb0 + g)
                    settle(e)
                    e.tensor_copy(zr_sb[0:1, :], zf2_sb[0:1, :]).then_inc(s_ze.h, 1)
                    s_ze.inc()
                    e.wait_ge(s_zb.h, zb0 + g + 1)
                    settle(e)
                    qsl = qh[0:64, sl4 * 512:(sl4 + 1) * 512]
                    e.tensor_tensor(qz_sb[0:64, (g % 2) * 512:(g % 2 + 1) * 512],
                                    qsl, pz[0:64, 0:512], ALU.mult).then_inc(s_qz.h, 1)
                    s_qz.inc()
                    e.wait_ge(s_ov.h, ov0 + g + 1)
                    settle(e)
                    ch_h, off_h = h // 2, (h % 2) * 64
                    bank = pc0 if g % 2 == 0 else pc1
                    e.tensor_copy(ao_fm[off_h:off_h + 64,
                                        ch_h * T + sl4 * 512:ch_h * T + (sl4 + 1) * 512],
                                  bank[0:64, :]).then_inc(s_ae.h, 1)
                    s_ae.inc()
            for it in range(TPS):
                i = sq * TPS + it
                e.wait_ge(s_y.h, y0 + 2 * (i + 1))
                il = i % TPS
                e.wait_ge(s_xr2[il % 4].h,
                          xr20[il % 4] + sq * 16 * (TPS // 4) + 16 * (il // 4 + 1))
                settle(e, 256)
                if i >= 2:
                    e.wait_ge(s_st[i % 2].h, st0[i % 2] + 16 * (i // 2))
                sl = yt[:, (i % 2) * C:(i % 2 + 1) * C]
                e.tensor_tensor(sl[:, 0:512], pc0[:], xr2[:, (i % 4) * C:(i % 4) * C + 512],
                                ALU.add)
                e.tensor_tensor(sl[:, 512:1024], pc1[:],
                                xr2[:, (i % 4) * C + 512:(i % 4 + 1) * C],
                                ALU.add).then_inc(s_t1.h, 1)
                s_t1.inc()
                e.wait_ge(s_w.h, w_done)
                e.wait_ge(s_t1.h, s_t1.n)
                settle(e, 256)
                e.tensor_tensor(sl[:], sl[:], br_sb[:], ALU.add).then_inc(s_ye.h, 1)
                s_ye.inc()
        block.vector(dve_seq)

        def xr2_loads(e, sq=sq, xr2=xr_buf[0], lnnr_done=lnnr_dones[-1]):
            e.wait_ge(s_lnnr.h, lnnr_done)
            for il in range(TPS):
                i = sq * TPS + il
                if i >= 4:
                    e.wait_ge(s_ye.h, ye0 + i - 3)
                sl = s_xr2[il % 4]
                e.dma_start(out=xr2[:, (il % 4) * C:(il % 4 + 1) * C],
                            in_=x_hbm[i * 128:(i + 1) * 128, :]).then_inc(sl.h, 16)
                sl.inc(16)
        block.sync(xr2_loads)

    def stores(e):
        for i in range(NT):
            e.wait_ge(s_ye.h, ye0 + i + 1)
            sl = s_st[i % 2]
            e.dma_start(out=xo_hbm[i * 128:(i + 1) * 128, :],
                        in_=yt[:, (i % 2) * C:(i % 2 + 1) * C]).then_inc(sl.h, 16)
            sl.inc(16)
        e.wait_ge(s_st[0].h, s_st[0].n)
        e.wait_ge(s_st[1].h, s_st[1].n)
    block.gpsimd(stores)


def a0_host_weights(z):
    wqkvT = np.ascontiguousarray(z["L0_attn_w_qkv"].T)
    wprojT = np.ascontiguousarray(z["L0_attn_w_proj"].T)
    brep = np.ascontiguousarray(np.broadcast_to(z["L0_attn_b_proj"], (128, 1024)))
    return dict(wqkvT=wqkvT, wprojT=wprojT, brep=brep)


def build_A0_program(z):
    b = Bld("a0")
    hw = a0_host_weights(z)
    x_in = b.inp("x_in", np.zeros((NTOK, C), np.float32))
    W = {k: b.inp("a0_" + k, v) for k, v in hw.items()}
    xo = b.out("x_out", (NTOK, C), F32)
    with b:
        emit_A0(b, b.block, x_in, xo, W)
        b.finish()
    return b, hw


def emit_F2(b, block, x_hbm, xo_hbm, W, scale=0.5):
    nc = b.nc
    E = 16
    NG = 4
    TPG = NTOK // NG
    SPG = TPG // 512
    TLG = TPG // 128
    GBW = 768
    trA = b.ps("f2_trA", [128, 1024], BF16)
    trB = b.ps("f2_trB", [128, 1024], BF16)
    pc = [b.ps(f"f2_pc{k}", [128, 512], F32) for k in range(6)]
    n_fm = b.sb("f2_nfm", [128, 8 * TPG], BF16)
    gw_sb = b.sb("f2_gw", [128, 8 * 16], BF16)
    gb = b.sb("f2_gb", [128, GBW], F32)
    SC, A_, BB, WW, R5 = 0, 128, 256, 384, 512
    G_, GS, S8a, S8b = 640, 672, 704, 712
    wfm = b.sb("f2_wfm", [16, TPG], BF16)
    ewu = b.sb("f2_ewu", [128, 24 * 1024], BF16)
    hb_sb = b.sb("f2_hb", [128, 4 * TPG], BF16)
    hs_sb = b.sb("f2_hs", [128, 8 * TPG], BF16)
    sg_sb = b.sb("f2_sg", [128, 2 * 512], BF16)
    su_sb = b.sb("f2_su", [128, 2 * 512], BF16)
    dn_sb = b.sb("f2_dn", [128, 4 * 512], F32)
    wrep = b.sb("f2_wrep", [128, TPG], F32)
    mtmp = b.sb("f2_mtmp", [128, 2 * 512], F32)
    macc = b.sb("f2_macc", [128, 8 * TPG], F32)
    yt = b.sb("f2_yt", [128, 2 * C], F32)
    bc_sb = b.sb("f2_bc", [16, 2048], BF16)

    sems = {}
    for nm in ("w ew gl gev wt wfm wrp wre gu sg su h dn dne mm ma tr ye").split():
        sems[nm] = b.sem("f2_" + nm)
    s_w, s_ew, s_gl, s_gev = sems["w"], sems["ew"], sems["gl"], sems["gev"]
    s_wt, s_wfm, s_wrp, s_wre = sems["wt"], sems["wfm"], sems["wrp"], sems["wre"]
    s_gu, s_sg, s_su, s_h = sems["gu"], sems["sg"], sems["su"], sems["h"]
    s_dn, s_dne, s_mm, s_ma = sems["dn"], sems["dne"], sems["mm"], sems["ma"]
    s_tr, s_ye = sems["tr"], sems["ye"]
    s_st = [b.sem(f"f2_st{k}") for k in range(2)]
    s_xr2 = [b.sem(f"f2_xr2{k}") for k in range(4)]

    def wloads(e):
        e.dma_start(out=gw_sb[:], in_=AP(W["gwT"].tensor, 0,
                    [[16, 128], [2048, 8], [1, 16]])).then_inc(s_w.h, 16)
        e.dma_start(out=bc_sb[:], in_=W["bc16"]).then_inc(s_w.h, 16)
        s_w.inc(32)
    block.gpsimd(wloads)
    w_done = s_w.n

    B0 = {nm: sems[nm].n for nm in sems}
    st0 = [x.n for x in s_st]
    xr20 = [x.n for x in s_xr2]

    def gu_i(grp, e_, k):
        return grp * (16 + E * 8) + (k if e_ < 0 else 16 + e_ * 8 + k)

    def dn_i(grp, e_, k):
        return grp * (16 + E * 16) + (k if e_ < 0 else 16 + e_ * 16 + k)

    def mm_i(grp, e_, k):
        return grp * (E * 16) + e_ * 16 + k

    def wr_i(grp, e_, ts):
        return grp * (E * SPG) + e_ * SPG + ts

    GU_G = 16 + E * 8
    DN_G = 16 + E * 16

    def v16(col):
        return AP(gb, col, [[GBW, 128], [16, 8], [1, 16]])

    def v4g(col):
        return AP(gb, col, [[GBW, 128], [16, 8], [4, 4], [1, 4]])

    def vG(col):
        return AP(gb, col, [[GBW, 128], [4, 8], [1, 4]])

    def rep_ig_to_ige(e, dstc, srcc):
        src = AP(gb, srcc, [[GBW, 128], [4, 8], [1, 4], [0, 4]])
        return e.tensor_copy(v4g(dstc), src)

    def rep_i_to_ig(e, dstc, srcc):
        src = AP(gb, srcc, [[GBW, 128], [1, 8], [0, 4]])
        return e.tensor_copy(vG(dstc), src)

    def rep_i_to_ie(e, dstc, srcc):
        src = AP(gb, srcc, [[GBW, 128], [1, 8], [0, 16]])
        return e.tensor_copy(v16(dstc), src)

    def ew_load(e, idx):
        grp, j = divmod(idx, 1 + E)
        if idx > 0:
            g2, j2 = divmod(idx - 1, 1 + E)
            e.wait_ge(s_dn.h, B0["dn"] + dn_i(g2, j2 - 1, 15) + 1)
            e.wait_ge(s_ew.h, B0["ew"] + 32 * idx)
        if j == 0:
            e.dma_start(out=ewu[:, 0:16 * 1024], in_=AP(W["w13sT"].tensor, 0,
                        [[2048, 128], [262144, 8], [1, 2048]])).then_inc(s_ew.h, 16)
            e.dma_start(out=ewu[:, 16 * 1024:24 * 1024], in_=AP(W["w2sT"].tensor, 0,
                        [[1024, 128], [131072, 8], [1, 1024]])).then_inc(s_ew.h, 16)
        else:
            eidx = j - 1
            e.dma_start(out=ewu[:, 0:8192], in_=AP(W["w13eT"].tensor, eidx * 1048576,
                        [[1024, 128], [131072, 8], [1, 1024]])).then_inc(s_ew.h, 16)
            e.dma_start(out=ewu[:, 8192:12288], in_=AP(W["w2eT"].tensor, eidx * 524288,
                        [[1024, 128], [131072, 4], [1, 1024]])).then_inc(s_ew.h, 16)
        s_ew.inc(32)

    lnnr_prev = [0]
    ye0 = B0["ye"]
    for grp in range(NG):
        pre = [(s_ye, ye0 + grp * TLG)] if grp else []
        s_nfm, xr2, s_lnnr = emit_ln_fm(b, block, x_hbm, n_fm, "f2", trA, trB,
                                        t0=grp * TLG, nt=TLG, pre_waits=pre)
        nfm_done = s_nfm.n
        lnnr_done = s_lnnr.n

        def xr2_loads(e, grp=grp, lnnr_done=lnnr_done):
            e.wait_ge(s_lnnr.h, lnnr_done)
            for il in range(TLG):
                i = grp * TLG + il
                if i >= 4:
                    e.wait_ge(s_ye.h, ye0 + i - 3)
                sl = s_xr2[il % 4]
                e.dma_start(out=xr2[:, (il % 4) * C:(il % 4 + 1) * C],
                            in_=x_hbm[i * 128:(i + 1) * 128, :]).then_inc(sl.h, 16)
                sl.inc(16)
        block.sync(xr2_loads)

        def pool_loads(e, grp=grp):
            if grp == 0:
                e.wait_ge(s_w.h, w_done)
            ew_load(e, grp * (1 + E))
        block.gpsimd(pool_loads)

        def pe_glog(e, grp=grp, nfm_done=nfm_done):
            e.wait_ge(s_nfm.h, nfm_done)
            e.wait_ge(s_w.h, w_done)
            e.wait_ge(b.s_cstg.h, b.s_cstg.n)
            if grp >= 1:
                e.wait_ge(s_sg.h, B0["sg"] + grp * GU_G)
                e.wait_ge(s_su.h, B0["su"] + grp * GU_G)
            for il in range(TLG):
                for ch in range(8):
                    ins = e.matmul(pc[0][:, il * 16:(il + 1) * 16],
                                   n_fm[:, ch * TPG + il * 128:ch * TPG + (il + 1) * 128],
                                   gw_sb[:, ch * 16:(ch + 1) * 16],
                                   start=(ch == 0), stop=(ch == 7))
            ins.then_inc(s_gl.h, 1)
            s_gl.inc()
        block.tensor(pe_glog)

        def dve_gev(e, grp=grp):
            e.wait_ge(s_gl.h, B0["gl"] + grp + 1)
            settle(e)
            e.tensor_copy(gb[:, SC:SC + 128], pc[0][:, 0:128]).then_inc(s_gev.h, 1)
            s_gev.inc()
        block.vector(dve_gev)

        g = Stepper(b, block, "f2g", extra_waits=[(s_gev, s_gev.n)])
        g.step("scalar", lambda e: e.activation(gb[:, SC:SC + 128], gb[:, SC:SC + 128],
               AF.Sigmoid))
        g.step("vector", lambda e: e.tensor_reduce(gb[:, G_:G_ + 32], v4g(SC),
               axis=AX.X, op=ALU.max))
        g.step("vector", lambda e: rep_ig_to_ige(e, R5, G_))
        g.step("vector", lambda e: e.tensor_tensor(gb[:, A_:A_ + 128], gb[:, SC:SC + 128],
               gb[:, R5:R5 + 128], ALU.is_lt))
        g.step("vector", lambda e: e.tensor_tensor(gb[:, A_:A_ + 128], gb[:, A_:A_ + 128],
               gb[:, SC:SC + 128], ALU.mult))
        g.step("vector", lambda e: e.tensor_reduce(gb[:, GS:GS + 32], v4g(A_),
               axis=AX.X, op=ALU.max))
        g.step("vector", lambda e: e.tensor_tensor(gb[:, G_:G_ + 32], gb[:, G_:G_ + 32],
               gb[:, GS:GS + 32], ALU.add))
        g.step("vector", lambda e: e.tensor_reduce(gb[:, S8a:S8a + 8], vG(G_),
               axis=AX.X, op=ALU.max))
        g.step("vector", lambda e: rep_i_to_ig(e, GS, S8a))
        g.step("vector", lambda e: e.tensor_tensor(gb[:, BB:BB + 32], gb[:, G_:G_ + 32],
               gb[:, GS:GS + 32], ALU.is_lt))
        g.step("vector", lambda e: e.tensor_tensor(gb[:, BB:BB + 32], gb[:, BB:BB + 32],
               gb[:, G_:G_ + 32], ALU.mult))
        g.step("vector", lambda e: e.tensor_reduce(gb[:, S8b:S8b + 8], vG(BB),
               axis=AX.X, op=ALU.max))
        g.step("vector", lambda e: rep_i_to_ig(e, GS, S8b))
        g.step("vector", lambda e: e.tensor_tensor(gb[:, GS:GS + 32], gb[:, G_:G_ + 32],
               gb[:, GS:GS + 32], ALU.is_ge))
        g.step("vector", lambda e: rep_ig_to_ige(e, R5, GS))
        g.step("vector", lambda e: e.tensor_tensor(gb[:, WW:WW + 128], gb[:, SC:SC + 128],
               gb[:, R5:R5 + 128], ALU.mult))
        g.step("vector", lambda e: e.tensor_copy(gb[:, A_:A_ + 128], gb[:, WW:WW + 128]))
        for _ in range(3):
            g.step("vector", lambda e: e.tensor_reduce(gb[:, S8a:S8a + 8], v16(A_),
                   axis=AX.X, op=ALU.max))
            g.step("vector", lambda e: rep_i_to_ie(e, BB, S8a))
            g.step("vector", lambda e: e.tensor_tensor(gb[:, BB:BB + 128],
                   gb[:, A_:A_ + 128], gb[:, BB:BB + 128], ALU.is_lt))
            g.step("vector", lambda e: e.tensor_tensor(gb[:, A_:A_ + 128],
                   gb[:, A_:A_ + 128], gb[:, BB:BB + 128], ALU.mult))
        g.step("vector", lambda e: e.tensor_reduce(gb[:, S8a:S8a + 8], v16(A_),
               axis=AX.X, op=ALU.max))
        g.step("vector", lambda e: rep_i_to_ie(e, BB, S8a))
        g.step("vector", lambda e: e.tensor_tensor(gb[:, BB:BB + 128], gb[:, WW:WW + 128],
               gb[:, BB:BB + 128], ALU.is_ge))
        g.step("vector", lambda e: e.tensor_tensor(gb[:, WW:WW + 128], gb[:, WW:WW + 128],
               gb[:, BB:BB + 128], ALU.mult))
        g.step("vector", lambda e: e.tensor_reduce(gb[:, S8a:S8a + 8], v16(WW),
               axis=AX.X, op=ALU.add))
        g.step("vector", lambda e: e.tensor_scalar_add(gb[:, S8a:S8a + 8],
               gb[:, S8a:S8a + 8], 1e-6))
        g.step("vector", lambda e: e.reciprocal(gb[:, S8b:S8b + 8], gb[:, S8a:S8a + 8]))
        g.step("vector", lambda e: rep_i_to_ie(e, BB, S8b))
        gate_done = g.step("vector", lambda e: e.tensor_tensor(gb[:, WW:WW + 128],
                           gb[:, WW:WW + 128], gb[:, BB:BB + 128], ALU.mult))
        s_g = g.s

        def pe_wt(e, grp=grp, gate_done=gate_done):
            e.wait_ge(s_g.h, gate_done)
            if grp >= 1:
                e.wait_ge(s_wfm.h, B0["wfm"] + 2 * grp)
                e.wait_ge(s_dne.h, B0["dne"] + grp * DN_G)
            for q in range(2):
                for j in range(4):
                    i = q * 4 + j
                    ins = e.transpose(pc[2][0:16, j * 128:(j + 1) * 128],
                                      gb[:, WW + i * 16:WW + (i + 1) * 16], b.idf[:])
                ins.then_inc(s_wt.h, 1)
                s_wt.inc()
                if q == 0:
                    e.wait_ge(s_wfm.h, B0["wfm"] + 2 * grp + 1)
        block.tensor(pe_wt)

        def dve_wfm(e, grp=grp):
            for q in range(2):
                e.wait_ge(s_wt.h, B0["wt"] + 2 * grp + q + 1)
                settle(e)
                e.tensor_copy(wfm[:, q * 512:(q + 1) * 512],
                              pc[2][0:16, 0:512]).then_inc(s_wfm.h, 1)
                s_wfm.inc()
        block.vector(dve_wfm)

        def pe_main(e, grp=grp):
            e.wait_ge(s_wfm.h, B0["wfm"] + 2 * (grp + 1))
            settle(e)
            if grp >= 1:
                e.wait_ge(s_ye.h, ye0 + grp * TLG)

            def gu_pair(stride, goff, uoff, gi, ts):
                if gi >= 1:
                    e.wait_ge(s_sg.h, B0["sg"] + gi)
                    e.wait_ge(s_su.h, B0["su"] + gi)
                for ci in range(8):
                    e.matmul(pc[0][:], ewu[:, ci * stride + goff:ci * stride + goff + 128],
                             n_fm[:, ci * TPG + ts * 512:ci * TPG + (ts + 1) * 512],
                             start=(ci == 0), stop=(ci == 7))
                for ci in range(8):
                    ins = e.matmul(pc[1][:],
                                   ewu[:, ci * stride + uoff:ci * stride + uoff + 128],
                                   n_fm[:, ci * TPG + ts * 512:ci * TPG + (ts + 1) * 512],
                                   start=(ci == 0), stop=(ci == 7))
                ins.then_inc(s_gu.h, 1)
                s_gu.inc()

            e.wait_ge(s_ew.h, B0["ew"] + 32 * (grp * (1 + E) + 1))
            for k in range(16):
                fsc, ts = divmod(k, SPG)
                gu_pair(2048, fsc * 128, 1024 + fsc * 128, gu_i(grp, -1, k), ts)
            e.wait_ge(s_h.h, B0["h"] + grp * GU_G + 16)
            settle(e)
            for k in range(16):
                hc, ts = divmod(k, SPG)
                di = dn_i(grp, -1, k)
                bank = pc[2 + k % 2]
                if di >= 2:
                    e.wait_ge(s_dne.h, B0["dne"] + di - 1)
                for ff in range(8):
                    ins = e.matmul(bank[:],
                                   ewu[:, 16384 + ff * 1024 + hc * 128:
                                       16384 + ff * 1024 + (hc + 1) * 128],
                                   hs_sb[:, ff * TPG + ts * 512:ff * TPG + (ts + 1) * 512],
                                   start=(ff == 0), stop=(ff == 7))
                ins.then_inc(s_dn.h, 1)
                s_dn.inc()
            for eidx in range(E):
                e.wait_ge(s_ew.h, B0["ew"] + 32 * (grp * (1 + E) + eidx + 2))
                for ts in range(SPG):
                    wi = wr_i(grp, eidx, ts)
                    bank = pc[4 + ts % 2]
                    if wi >= 2:
                        e.wait_ge(s_wre.h, B0["wre"] + wi - 1)
                    e.matmul(bank[:], bc_sb[:, eidx * 128:(eidx + 1) * 128],
                             wfm[0:16, ts * 512:(ts + 1) * 512],
                             start=True, stop=True).then_inc(s_wrp.h, 1)
                    s_wrp.inc()
                for k in range(8):
                    fc, ts = divmod(k, SPG)
                    gu_pair(1024, fc * 128, 512 + fc * 128, gu_i(grp, eidx, k), ts)
                e.wait_ge(s_h.h, B0["h"] + grp * GU_G + 16 + (eidx + 1) * 8)
                settle(e)
                for k in range(16):
                    hc, ts = divmod(k, SPG)
                    di = dn_i(grp, eidx, k)
                    bank = pc[2 + k % 2]
                    if di >= 2:
                        e.wait_ge(s_dne.h, B0["dne"] + di - 1)
                    for ff in range(4):
                        ins = e.matmul(bank[:],
                                       ewu[:, 8192 + ff * 1024 + hc * 128:
                                           8192 + ff * 1024 + (hc + 1) * 128],
                                       hb_sb[:, ff * TPG + ts * 512:ff * TPG + (ts + 1) * 512],
                                       start=(ff == 0), stop=(ff == 3))
                    ins.then_inc(s_dn.h, 1)
                    s_dn.inc()
            e.wait_ge(s_ma.h, B0["ma"] + (grp + 1) * DN_G)
            settle(e)
            for il in range(TLG):
                i = grp * TLG + il
                if i >= 1:
                    e.wait_ge(s_ye.h, ye0 + i)
                for hc in range(8):
                    bank = pc[4 + hc // 4]
                    ins = e.transpose(bank[:, (hc % 4) * 128:(hc % 4 + 1) * 128],
                                      macc[:, hc * TPG + il * 128:hc * TPG + (il + 1) * 128],
                                      b.idf[:])
                ins.then_inc(s_tr.h, 1)
                s_tr.inc()
        block.tensor(pe_main)

        def act_main(e, grp=grp):
            def sg_block(nk, e_):
                for k in range(nk):
                    gi = gu_i(grp, e_, k)
                    e.wait_ge(s_gu.h, B0["gu"] + gi + 1)
                    if gi >= 2:
                        e.wait_ge(s_h.h, B0["h"] + gi - 1)
                    e.activation(sg_sb[:, (gi % 2) * 512:(gi % 2 + 1) * 512],
                                 pc[0][:], AF.Silu).then_inc(s_sg.h, 1)
                    s_sg.inc()

            def dn_block(e_):
                e.wait_ge(s_ma.h, B0["ma"] + dn_i(grp, e_, 0))
                e.wait_ge(s_mm.h, B0["mm"] + (mm_i(grp, e_, 0) if e_ >= 0
                                              else grp * E * 16))
                for k in range(16):
                    di = dn_i(grp, e_, k)
                    e.wait_ge(s_dn.h, B0["dn"] + di + 1)
                    if k >= 4:
                        if e_ < 0:
                            e.wait_ge(s_ma.h, B0["ma"] + di - 3)
                        else:
                            e.wait_ge(s_mm.h, B0["mm"] + mm_i(grp, e_, k - 4) + 1)
                    e.activation(dn_sb[:, (di % 4) * 512:(di % 4 + 1) * 512],
                                 pc[2 + k % 2][:], AF.Identity).then_inc(s_dne.h, 1)
                    s_dne.inc()

            sg_block(16, -1)
            dn_block(-1)
            for eidx in range(E):
                sg_block(8, eidx)
                dn_block(eidx)
        block.scalar(act_main)

        def pool_main(e, grp=grp):
            km0 = grp * E * 16
            for k in range(16):
                gi = gu_i(grp, -1, k)
                e.wait_ge(s_sg.h, B0["sg"] + gi + 1)
                e.wait_ge(s_su.h, B0["su"] + gi + 1)
                settle(e, 256)
                fsc, ts = divmod(k, SPG)
                e.tensor_tensor(hs_sb[:, fsc * TPG + ts * 512:fsc * TPG + (ts + 1) * 512],
                                sg_sb[:, (gi % 2) * 512:(gi % 2 + 1) * 512],
                                su_sb[:, (gi % 2) * 512:(gi % 2 + 1) * 512],
                                ALU.mult).then_inc(s_h.h, 1)
                s_h.inc()
            ew_load(e, grp * (1 + E) + 1)
            for eidx in range(E):
                for k in range(8):
                    gi = gu_i(grp, eidx, k)
                    e.wait_ge(s_sg.h, B0["sg"] + gi + 1)
                    e.wait_ge(s_su.h, B0["su"] + gi + 1)
                    settle(e, 256)
                    fc, ts = divmod(k, SPG)
                    e.tensor_tensor(hb_sb[:, fc * TPG + ts * 512:fc * TPG + (ts + 1) * 512],
                                    sg_sb[:, (gi % 2) * 512:(gi % 2 + 1) * 512],
                                    su_sb[:, (gi % 2) * 512:(gi % 2 + 1) * 512],
                                    ALU.mult).then_inc(s_h.h, 1)
                    s_h.inc()
                e.wait_ge(s_wre.h, B0["wre"] + wr_i(grp, eidx, SPG - 1) + 1)
                settle(e)
                for k in range(16):
                    di = dn_i(grp, eidx, k)
                    km = km0 + eidx * 16 + k
                    e.wait_ge(s_dne.h, B0["dne"] + di + 1)
                    settle(e, 256)
                    if km >= 2:
                        km2 = km - 2
                        i2, k2 = divmod(km2, 16)
                        g2, e2 = divmod(i2, E)
                        e.wait_ge(s_ma.h, B0["ma"] + dn_i(g2, e2, k2) + 1)
                    hc, ts = divmod(k, SPG)
                    e.tensor_tensor(mtmp[:, (km % 2) * 512:(km % 2 + 1) * 512],
                                    dn_sb[:, (di % 4) * 512:(di % 4 + 1) * 512],
                                    wrep[:, ts * 512:(ts + 1) * 512],
                                    ALU.mult).then_inc(s_mm.h, 1)
                    s_mm.inc()
                if eidx + 2 <= E:
                    ew_load(e, grp * (1 + E) + eidx + 2)
            for il in range(TLG):
                i = grp * TLG + il
                e.wait_ge(s_ye.h, ye0 + i + 1)
                sl = s_st[i % 2]
                e.dma_start(out=xo_hbm[i * 128:(i + 1) * 128, :],
                            in_=yt[:, (i % 2) * C:(i % 2 + 1) * C]).then_inc(sl.h, 16)
                sl.inc(16)
            if grp == NG - 1:
                e.wait_ge(s_st[0].h, s_st[0].n)
                e.wait_ge(s_st[1].h, s_st[1].n)
        block.gpsimd(pool_main)

        def dve_main(e, grp=grp):
            def su_block(nk, e_):
                for k in range(nk):
                    gi = gu_i(grp, e_, k)
                    e.wait_ge(s_gu.h, B0["gu"] + gi + 1)
                    settle(e, 256)
                    if gi >= 2:
                        e.wait_ge(s_h.h, B0["h"] + gi - 1)
                    e.tensor_copy(su_sb[:, (gi % 2) * 512:(gi % 2 + 1) * 512],
                                  pc[1][:]).then_inc(s_su.h, 1)
                    s_su.inc()

            su_block(16, -1)
            if grp >= 1:
                e.wait_ge(s_tr.h, B0["tr"] + grp * TLG)
            for k in range(16):
                di = dn_i(grp, -1, k)
                e.wait_ge(s_dne.h, B0["dne"] + di + 1)
                settle(e, 256)
                hc, ts = divmod(k, SPG)
                e.tensor_copy(macc[:, hc * TPG + ts * 512:hc * TPG + (ts + 1) * 512],
                              dn_sb[:, (di % 4) * 512:(di % 4 + 1) * 512]).then_inc(s_ma.h, 1)
                s_ma.inc()
            for eidx in range(E):
                su_block(8, eidx)
                for ts in range(SPG):
                    wi = wr_i(grp, eidx, ts)
                    e.wait_ge(s_wrp.h, B0["wrp"] + wi + 1)
                    settle(e)
                    e.tensor_copy(wrep[:, ts * 512:(ts + 1) * 512],
                                  pc[4 + ts % 2][:]).then_inc(s_wre.h, 1)
                    s_wre.inc()
                for k in range(16):
                    mi = mm_i(grp, eidx, k)
                    e.wait_ge(s_mm.h, B0["mm"] + mi + 1)
                    settle(e, 256)
                    hc, ts = divmod(k, SPG)
                    dst = macc[:, hc * TPG + ts * 512:hc * TPG + (ts + 1) * 512]
                    e.tensor_tensor(dst, dst, mtmp[:, (mi % 2) * 512:(mi % 2 + 1) * 512],
                                    ALU.add).then_inc(s_ma.h, 1)
                    s_ma.inc()
            for il in range(TLG):
                i = grp * TLG + il
                e.wait_ge(s_tr.h, B0["tr"] + i + 1)
                e.wait_ge(s_xr2[il % 4].h, xr20[il % 4] + grp * 16 * (TLG // 4) +
                          16 * (il // 4 + 1))
                settle(e, 256)
                if i >= 2:
                    e.wait_ge(s_st[i % 2].h, st0[i % 2] + 16 * (i // 2))
                sl = yt[:, (i % 2) * C:(i % 2 + 1) * C]
                e.scalar_tensor_tensor(sl[:, 0:512], pc[4][:], scale,
                                       xr2[:, (il % 4) * C:(il % 4) * C + 512],
                                       ALU.mult, ALU.add)
                e.scalar_tensor_tensor(sl[:, 512:1024], pc[5][:], scale,
                                       xr2[:, (il % 4) * C + 512:(il % 4 + 1) * C],
                                       ALU.mult, ALU.add).then_inc(s_ye.h, 1)
                s_ye.inc()
        block.vector(dve_main)


def f2_host_weights(z):
    gw = z["L2_ffn_w_gate"]
    w1e, w3e, w2e = z["L2_ffn_w1_e"], z["L2_ffn_w3_e"], z["L2_ffn_w2_e"]
    w1s, w3s, w2s = z["L2_ffn_w1_s"], z["L2_ffn_w3_s"], z["L2_ffn_w2_s"]
    gwT = np.ascontiguousarray(gw.T)
    w13eT = np.ascontiguousarray(np.concatenate(
        [np.transpose(w1e, (0, 2, 1)), np.transpose(w3e, (0, 2, 1))], axis=2))
    w2eT = np.ascontiguousarray(np.transpose(w2e, (0, 2, 1)))
    w13sT = np.ascontiguousarray(np.concatenate([w1s.T, w3s.T], axis=1))
    w2sT = np.ascontiguousarray(w2s.T)
    bc16 = np.zeros((16, 2048), np.float32)
    for e in range(16):
        bc16[e, e * 128:(e + 1) * 128] = 1.0
    return dict(gwT=gwT, w13eT=w13eT, w2eT=w2eT, w13sT=w13sT, w2sT=w2sT, bc16=bc16)


def build_F2_program(z):
    b = Bld("f2")
    hw = f2_host_weights(z)
    x_in = b.inp("x_in", np.zeros((NTOK, C), np.float32))
    W = {k: b.inp("f2_" + k, v) for k, v in hw.items()}
    xo = b.out("x_out", (NTOK, C), F32)
    with b:
        emit_F2(b, b.block, x_in, xo, W, scale=0.5)
        b.finish()
    return b, hw


def emit_A1(b, block, x_hbm, xo_hbm, W):
    nc = b.nc
    H = 16
    trA = b.ps("a1_trA", [128, 1024], BF16)
    trB = b.ps("a1_trB", [128, 1024], BF16)
    pc = [b.ps(f"a1_pc{k}", [128, 512], F32) for k in range(6)]
    n_fm = b.sb("a1_nfm", [128, 8 * T], BF16)
    wq_sb = b.sb("a1_wq", [128, 8 * 1024], BF16)
    wk_sb = b.sb("a1_wk", [128, 8 * 1024], BF16)
    wv_sb = b.sb("a1_wv", [128, 8 * 1024], BF16)
    wc_sb = b.sb("a1_wc", [128, 8 * 1024], BF16)
    lt_sb = b.sb("a1_lt", [128, 16 * 64], F32)
    trm_sb = b.sb("a1_trm", [128, 128], F32)
    lm_sb = b.sb("a1_lm", [128, 16 * 64], BF16)
    bm_sb = b.sb("a1_bm", [128, 16 * 64], BF16)
    qf_sb = b.sb("a1_qf", [128, 1024], BF16)
    kf_sb = b.sb("a1_kf", [128, 1024], BF16)
    kt_sb = b.sb("a1_kt", [128, 1024], BF16)
    vt_sb = b.sb("a1_vt", [128, 1024], BF16)
    atm_sb = b.sb("a1_atm", [128, 128], BF16)
    sacc = b.sb("a1_sacc", [128, 16 * 64], F32)
    sb_sb = b.sb("a1_sb", [128, 64], BF16)
    yb_sb = b.sb("a1_yb", [128, 64], BF16)
    yfm_sb = b.sb("a1_yfm", [128, 128], BF16)
    ao_fm = b.sb("a1_ao", [128, 8 * T], BF16)
    yt = b.sb("a1_yt", [128, 2 * C], F32)

    s_w = b.sem("a1_w")
    s_pj = b.sem("a1_pj")
    s_ye = b.sem("a1_ye")
    s_st = [b.sem(f"a1_st{k}") for k in range(2)]
    s_xr2 = [b.sem(f"a1_xr2{k}") for k in range(4)]

    def wloads(e):
        for nm, dst in (("wqT", wq_sb), ("wkT", wk_sb), ("wvT", wv_sb), ("wcT", wc_sb)):
            e.dma_start(out=dst[:], in_=AP(W[nm].tensor, 0,
                        [[1024, 128], [131072, 8], [1, 1024]])).then_inc(s_w.h, 16)
            s_w.inc(16)
        e.dma_start(out=lt_sb[0:64, :], in_=AP(W["LtriT"].tensor, 0,
                    [[64, 64], [4096, 16], [1, 64]])).then_inc(s_w.h, 16)
        s_w.inc(16)
    block.gpsimd(wloads)

    def trmload(e):
        e.dma_start(out=trm_sb[:], in_=W["triu"]).then_inc(s_w.h, 16)
        s_w.inc(16)
    block.gpsimd(trmload)
    w_done = s_w.n

    bmc = Stepper(b, block, "a1bm", extra_waits=[(s_w, w_done)])

    def lmf(e):
        for h in range(H):
            ins = e.tensor_tensor(lm_sb[0:64, h * 64:(h + 1) * 64],
                                  lt_sb[0:64, h * 64:(h + 1) * 64],
                                  trm_sb[0:64, 0:64], ALU.mult)
        return ins
    bmc.step("vector", lmf)
    for q in range(2):
        def bmm(e, q=q):
            for hh in range(8):
                h = q * 8 + hh
                ins = e.matmul(pc[3][0:64, hh * 64:(hh + 1) * 64],
                               lm_sb[0:64, h * 64:(h + 1) * 64],
                               lm_sb[0:64, h * 64:(h + 1) * 64],
                               start=True, stop=True)
            return ins
        bmc.step("tensor", bmm)
        bmc.step("vector", lambda e, q=q: e.tensor_copy(
            bm_sb[0:64, q * 512:(q + 1) * 512], pc[3][0:64, 0:512]))
    bm_done = bmc.s.n
    s_bmc = bmc.s

    ye0 = s_ye.n
    pj0 = s_pj.n
    st0 = [x.n for x in s_st]
    xr20 = [x.n for x in s_xr2]

    for sq in range(SEQ_PER_CORE):
        pre = [(s_ye, ye0 + sq * TPS)] if sq else []
        s_nfm, xr2, s_lnnr = emit_ln_fm(b, block, x_hbm, n_fm, "a1", trA, trB,
                                        t0=sq * TPS, nt=TPS, pre_waits=pre)
        nfm_done = s_nfm.n
        lnnr_done = s_lnnr.n

        def xr2_loads(e, sq=sq, lnnr_done=lnnr_done):
            e.wait_ge(s_lnnr.h, lnnr_done)
            for il in range(TPS):
                i = sq * TPS + il
                if i >= 4:
                    e.wait_ge(s_ye.h, ye0 + i - 3)
                sl = s_xr2[il % 4]
                e.dma_start(out=xr2[:, (il % 4) * C:(il % 4 + 1) * C],
                            in_=x_hbm[i * 128:(i + 1) * 128, :]).then_inc(sl.h, 16)
                sl.inc(16)
        block.sync(xr2_loads)

        ch_ = Stepper(b, block, "a1m",
                      extra_waits=[(s_nfm, nfm_done), (s_w, w_done),
                                   (s_bmc, bm_done),
                                   (b.s_cstg, b.s_cstg.n)] +
                                  ([(s_ye, ye0 + sq * TPS)] if sq else []))

        for it in range(TPS):
            def mm_tm(e, wsb=None, it=it):
                for sl2 in range(2):
                    bank = pc[4 + sl2]
                    for ci in range(8):
                        ins = e.matmul(bank[:],
                                       n_fm[:, ci * T + it * 128:ci * T + (it + 1) * 128],
                                       wsb[:, ci * 1024 + sl2 * 512:ci * 1024 + (sl2 + 1) * 512],
                                       start=(ci == 0), stop=(ci == 7))
                return ins

            def mm_fm(e, wsb=None, it=it):
                for ch in range(8):
                    bank = pc[4 + ch // 4]
                    for ci in range(8):
                        ins = e.matmul(bank[:, (ch % 4) * 128:(ch % 4 + 1) * 128],
                                       wsb[:, ci * 1024 + ch * 128:ci * 1024 + (ch + 1) * 128],
                                       n_fm[:, ci * T + it * 128:ci * T + (it + 1) * 128],
                                       start=(ci == 0), stop=(ci == 7))
                return ins

            def ev2(e, dst=None):
                e.tensor_copy(dst[:, 0:512], pc[4][:])
                return e.tensor_copy(dst[:, 512:1024], pc[5][:])

            ch_.step("tensor", lambda e, f=mm_tm: f(e, wsb=wk_sb))
            ch_.step("vector", lambda e: ev2(e, dst=kt_sb))
            ch_.step("tensor", lambda e, f=mm_tm: f(e, wsb=wv_sb))
            ch_.step("vector", lambda e: ev2(e, dst=vt_sb))
            ch_.step("tensor", lambda e, f=mm_fm: f(e, wsb=wq_sb))
            ch_.step("vector", lambda e: ev2(e, dst=qf_sb))
            ch_.step("tensor", lambda e, f=mm_fm: f(e, wsb=wk_sb))
            ch_.step("vector", lambda e: ev2(e, dst=kf_sb))

            for h in range(H):
                po, co = (h % 2) * 64, (h // 2) * 128

                def at_mm(e, h=h, po=po, co=co):
                    return e.matmul(pc[0][:, 0:128], kf_sb[po:po + 64, co:co + 128],
                                    qf_sb[po:po + 64, co:co + 128],
                                    start=True, stop=True)
                ch_.step("tensor", at_mm)
                ch_.step("vector", lambda e: e.tensor_tensor(
                    atm_sb[:], pc[0][:, 0:128], trm_sb[:], ALU.mult))
                if it > 0:
                    ch_.step("vector", lambda e, h=h, po=po: e.tensor_copy(
                        sb_sb[po:po + 64, :], sacc[po:po + 64, h * 64:(h + 1) * 64]))

                def y_mm(e, h=h, po=po, co=co, it=it):
                    if it == 0:
                        return e.matmul(pc[1][:, 0:64], atm_sb[:],
                                        vt_sb[:, h * 64:(h + 1) * 64],
                                        start=True, stop=True)
                    e.matmul(pc[1][:, 0:64], atm_sb[:],
                             vt_sb[:, h * 64:(h + 1) * 64], start=True, stop=False)
                    return e.matmul(pc[1][:, 0:64], qf_sb[po:po + 64, co:co + 128],
                                    sb_sb[po:po + 64, :], start=False, stop=True)
                ch_.step("tensor", y_mm)
                ch_.step("vector", lambda e: e.tensor_copy(yb_sb[:, 0:64],
                                                           pc[1][:, 0:64]))

                def sd_mm(e, h=h, po=po):
                    return e.matmul(pc[3][po:po + 64, 0:64],
                                    kt_sb[:, h * 64:(h + 1) * 64],
                                    vt_sb[:, h * 64:(h + 1) * 64],
                                    start=True, stop=True,
                                    tile_position=(0, po))
                ch_.step("tensor", sd_mm)
                if it == 0:
                    ch_.step("vector", lambda e, h=h, po=po: e.tensor_copy(
                        sacc[po:po + 64, h * 64:(h + 1) * 64], pc[3][po:po + 64, 0:64]))
                else:
                    ch_.step("vector", lambda e, h=h, po=po: e.tensor_tensor(
                        sacc[po:po + 64, h * 64:(h + 1) * 64],
                        sacc[po:po + 64, h * 64:(h + 1) * 64],
                        pc[3][po:po + 64, 0:64], ALU.add))
                ch_.step("tensor", lambda e: e.transpose(trA[0:64, 0:128],
                         yb_sb[:, 0:64], b.idb[:]))
                ch_.step("vector", lambda e: e.tensor_copy(yfm_sb[0:64, 0:128],
                                                           trA[0:64, 0:128]))

                def y2_mm(e, h=h):
                    return e.matmul(pc[2][0:64, 0:128], bm_sb[0:64, h * 64:(h + 1) * 64],
                                    yfm_sb[0:64, 0:128], start=True, stop=True)
                ch_.step("tensor", y2_mm)
                ch_.step("vector", lambda e, h=h, po=po, it=it: e.tensor_copy(
                    ao_fm[po:po + 64, (h // 2) * T + it * 128:(h // 2) * T + (it + 1) * 128],
                    pc[2][0:64, 0:128]))

        chain_done = ch_.s.n
        s_chm = ch_.s

        def pe_proj(e, sq=sq, chain_done=chain_done):
            e.wait_ge(s_chm.h, chain_done)
            settle(e)
            for it in range(TPS):
                i = sq * TPS + it
                for ch2 in range(2):
                    bank = pc[4 + ch2]
                    if i >= 1:
                        e.wait_ge(s_ye.h, ye0 + i)
                    for k1 in range(8):
                        ins = e.matmul(bank[:],
                                       ao_fm[:, k1 * T + it * 128:k1 * T + (it + 1) * 128],
                                       wc_sb[:, k1 * 1024 + ch2 * 512:k1 * 1024 + (ch2 + 1) * 512],
                                       start=(k1 == 0), stop=(k1 == 7))
                    ins.then_inc(s_pj.h, 1)
                    s_pj.inc()
        block.tensor(pe_proj)

        def dve_ye(e, sq=sq):
            for it in range(TPS):
                i = sq * TPS + it
                e.wait_ge(s_pj.h, pj0 + 2 * (i + 1))
                il = it
                e.wait_ge(s_xr2[il % 4].h, xr20[il % 4] + sq * 16 * (TPS // 4) +
                          16 * (il // 4 + 1))
                settle(e, 256)
                if i >= 2:
                    e.wait_ge(s_st[i % 2].h, st0[i % 2] + 16 * (i // 2))
                sl = yt[:, (i % 2) * C:(i % 2 + 1) * C]
                e.tensor_tensor(sl[:, 0:512], pc[4][:],
                                xr2[:, (il % 4) * C:(il % 4) * C + 512], ALU.add)
                e.tensor_tensor(sl[:, 512:1024], pc[5][:],
                                xr2[:, (il % 4) * C + 512:(il % 4 + 1) * C],
                                ALU.add).then_inc(s_ye.h, 1)
                s_ye.inc()
        block.vector(dve_ye)

    def stores(e):
        for i in range(NT):
            e.wait_ge(s_ye.h, ye0 + i + 1)
            sl = s_st[i % 2]
            e.dma_start(out=xo_hbm[i * 128:(i + 1) * 128, :],
                        in_=yt[:, (i % 2) * C:(i % 2 + 1) * C]).then_inc(sl.h, 16)
            sl.inc(16)
        e.wait_ge(s_st[0].h, s_st[0].n)
        e.wait_ge(s_st[1].h, s_st[1].n)
    block.gpsimd(stores)


def a1_host_weights(z):
    wqT = np.ascontiguousarray(z["L1_attn_Wq"].T)
    wkT = np.ascontiguousarray(z["L1_attn_Wk"].T)
    wvT = np.ascontiguousarray(z["L1_attn_Wv"].T)
    wcT = np.ascontiguousarray(z["L1_attn_Wc"].T)
    LtriT = np.ascontiguousarray(np.transpose(z["L1_attn_Ltri"], (0, 2, 1)))
    triu = np.triu(np.ones((128, 128), np.float32))
    return dict(wqT=wqT, wkT=wkT, wvT=wvT, wcT=wcT, LtriT=LtriT, triu=triu)


def build_A1_program(z):
    b = Bld("a1")
    hw = a1_host_weights(z)
    x_in = b.inp("x_in", np.zeros((NTOK, C), np.float32))
    W = {k: b.inp("a1_" + k, v) for k, v in hw.items()}
    xo = b.out("x_out", (NTOK, C), F32)
    with b:
        emit_A1(b, b.block, x_in, xo, W)
        b.finish()
    return b, hw


def emit_F3(b, block, x_hbm, xo_hbm, W, scale=0.5):
    nc = b.nc
    E = 16
    ATT_SCALE = 1.0 / math.sqrt(128.0)
    trA = b.ps("f3_trA", [128, 1024], BF16)
    trB = b.ps("f3_trB", [128, 1024], BF16)
    pc = [b.ps(f"f3_pc{k}", [128, 512], F32) for k in range(6)]
    n_fm = b.sb("f3_nfm", [128, 8 * NTOK], BF16)
    acc = b.sb("f3_acc", [128, 32 * 128], F32)
    qf = b.sb("f3_qf", [128, NTOK], BF16)
    kf = b.sb("f3_kf", [128, NTOK], BF16)
    vt = b.sb("f3_vt", [128, 32 * 128], BF16)
    ot = b.sb("f3_ot", [128, 32 * 128], BF16)
    gwk = b.sb("f3_gwk", [128, 2112], F32)
    RAW, GA, SCR, WSEL, M32a, M32b = 0, 512, 1024, 1536, 2048, 2080
    gb2 = b.sb("f3_gb2", [128, 512], F32)
    simb = b.sb("f3_simb", [128, 8 * 16], BF16)
    gat_sb = b.sb("f3_gat", [128, 16], F32)
    sq2 = b.sb("f3_sq2", [128, 512], BF16)
    cnb = b.sb("f3_cnb", [1, 16], BF16)
    rn_sb = b.sb("f3_rn", [1, 512], BF16)
    rniv = b.sb("f3_rniv", [128, 32], F32)
    crep = b.sb("f3_crep", [128, 16], F32)
    sig = b.sb("f3_sig", [128, 16], F32)
    pw = b.sb("f3_pw", [128, 2 * 1024], BF16)
    opw = b.sb("f3_opw", [128, 16 * 1024], BF16)
    sc_sb = acc
    pt_sb = b.sb("f3_pt", [128, 2048], BF16)
    ptT = b.sb("f3_ptT", [128, 128], BF16)
    ofm1 = b.sb("f3_ofm1", [128, 128], BF16)
    mcol = b.sb("f3_mcol", [128, 2], F32)
    oacc = b.sb("f3_oacc", [128, 1024], F32)
    yt = b.sb("f3_yt", [128, 2 * C], F32)

    sems = {}
    for nm in ("w qa qae att ate oy oye ye").split():
        sems[nm] = b.sem("f3_" + nm)
    s_pwl = [b.sem(f"f3_pwl{k}") for k in range(2)]
    s_w, s_qa, s_qae = sems["w"], sems["qa"], sems["qae"]
    s_att, s_ate, s_oy, s_oye, s_ye = (sems["att"], sems["ate"], sems["oy"],
                                       sems["oye"], sems["ye"])
    s_st = [b.sem(f"f3_st{k}") for k in range(2)]
    s_xr2 = [b.sem(f"f3_xr2{k}") for k in range(4)]

    def wloads(e):
        e.dma_start(out=simb[:], in_=AP(W["sim"].tensor, 0,
                    [[16, 128], [2048, 8], [1, 16]])).then_inc(s_w.h, 16)
        e.dma_start(out=gat_sb[:], in_=W["gatesb"]).then_inc(s_w.h, 16)
        e.dma_start(out=opw[:], in_=AP(W["opT"].tensor, 0,
                    [[1024, 128], [131072, 16], [1, 1024]])).then_inc(s_w.h, 16)
        s_w.inc(48)
    block.gpsimd(wloads)
    w_done = s_w.n

    s_nfm, xr2, s_lnnr = emit_ln_fm(b, block, x_hbm, n_fm, "f3", trA, trB)
    nfm_done = s_nfm.n
    lnnr_done = s_lnnr.n
    B0 = {nm: sems[nm].n for nm in sems}
    st0 = [x.n for x in s_st]
    xr20 = [x.n for x in s_xr2]
    ye0 = B0["ye"]

    def xr2_loads(e):
        e.wait_ge(s_lnnr.h, lnnr_done)
        for i in range(NT):
            if i >= 4:
                e.wait_ge(s_ye.h, ye0 + i - 3)
            sl = s_xr2[i % 4]
            e.dma_start(out=xr2[:, (i % 4) * C:(i % 4 + 1) * C],
                        in_=x_hbm[i * 128:(i + 1) * 128, :]).then_inc(sl.h, 16)
            sl.inc(16)
    block.sync(xr2_loads)

    g = Stepper(b, block, "f3n", extra_waits=[(s_nfm, nfm_done), (s_w, w_done),
                                              (b.s_cstg, b.s_cstg.n)])
    g.step("vector", lambda e: e.tensor_tensor(sq2[:, 0:128], simb[:], simb[:],
           ALU.mult))

    def cnmm(e):
        for ch in range(8):
            ins = e.matmul(pc[0][0:1, 0:16], b.cst_ones_b[:, 0:1],
                           sq2[:, ch * 16:(ch + 1) * 16],
                           start=(ch == 0), stop=(ch == 7))
        return ins
    g.step("tensor", cnmm)
    g.step("scalar", lambda e: e.activation(gwk[0:1, 0:16], pc[0][0:1, 0:16],
           AF.Sqrt))
    g.step("vector", lambda e: e.reciprocal(gwk[0:1, 16:32], gwk[0:1, 0:16]))
    g.step("vector", lambda e: e.tensor_copy(cnb[0:1, :], gwk[0:1, 16:32]))
    g.step("tensor", lambda e: e.matmul(pc[0][0:128, 0:16], b.cst_ones_b[0:1, 0:128],
           cnb[0:1, :], start=True, stop=True))
    g.step("vector", lambda e: e.tensor_copy(crep[:], pc[0][:, 0:16]))
    g.step("scalar", lambda e: e.activation(sig[:], gat_sb[:], AF.Sigmoid))
    for sl8 in range(8):
        for ch in range(8):
            g.step("vector", lambda e, sl8=sl8, ch=ch: e.tensor_tensor(sq2[:],
                   n_fm[:, ch * NTOK + sl8 * 512:ch * NTOK + (sl8 + 1) * 512],
                   n_fm[:, ch * NTOK + sl8 * 512:ch * NTOK + (sl8 + 1) * 512],
                   ALU.mult), settle_cycles=128)
            g.step("tensor", lambda e, ch=ch: e.matmul(pc[1][0:1, 0:512],
                   b.cst_ones_b[:, 0:1], sq2[:], start=(ch == 0), stop=(ch == 7)),
                   settle_cycles=128)
        g.step("scalar", lambda e: e.activation(gwk[0:1, 32:544], pc[1][0:1, 0:512],
               AF.Sqrt))
        g.step("vector", lambda e: e.reciprocal(gwk[0:1, 32:544], gwk[0:1, 32:544]))
        g.step("vector", lambda e: e.tensor_copy(rn_sb[0:1, :], gwk[0:1, 32:544]))

        def rtr(e, sl8=sl8):
            for j in range(4):
                ins = e.matmul(pc[2][:, j:j + 1], rn_sb[0:1, j * 128:(j + 1) * 128],
                               b.cst_ones_b[0:1, 0:1], start=True, stop=True)
            return ins
        g.step("tensor", rtr)
        g.step("vector", lambda e, sl8=sl8: e.tensor_copy(
            rniv[:, sl8 * 4:(sl8 + 1) * 4], pc[2][:, 0:4]))
    for i in range(NT):
        def rawmm(e, i=i):
            for ch in range(8):
                ins = e.matmul(pc[3][:, (i % 2) * 16:(i % 2) * 16 + 16],
                               n_fm[:, ch * NTOK + i * 128:ch * NTOK + (i + 1) * 128],
                               simb[:, ch * 16:(ch + 1) * 16],
                               start=(ch == 0), stop=(ch == 7))
            return ins
        g.step("tensor", rawmm, settle_cycles=128)
        g.step("vector", lambda e, i=i: e.scalar_tensor_tensor(
            gwk[:, RAW + i * 16:RAW + (i + 1) * 16],
            pc[3][:, (i % 2) * 16:(i % 2) * 16 + 16], rniv[:, i:i + 1],
            crep[:], ALU.mult, ALU.mult), settle_cycles=128)
    g.step("vector", lambda e: e.tensor_tensor(gwk[:, RAW:RAW + 512],
           gwk[:, RAW:RAW + 512],
           AP(sig, 0, [[16, 128], [0, 32], [1, 16]]), ALU.subtract))
    g.step("scalar", lambda e: e.activation(gwk[:, GA:GA + 512], gwk[:, RAW:RAW + 512],
           AF.Relu))
    g.step("vector", lambda e: e.tensor_scalar(gwk[:, SCR:SCR + 512],
           gwk[:, GA:GA + 512], 0.0, None, ALU.is_gt))

    def v16g(col):
        return AP(gwk, col, [[2112, 128], [16, 32], [1, 16]])

    def rep_ie(e, dstc, srcc):
        src = AP(gwk, srcc, [[2112, 128], [1, 32], [0, 16]])
        dst = AP(gb2, dstc, [[512, 128], [16, 32], [1, 16]])
        return e.tensor_copy(dst, src)

    g.step("vector", lambda e: e.tensor_reduce(gwk[:, M32a:M32a + 32], v16g(SCR),
           axis=AX.X, op=ALU.add))
    g.step("vector", lambda e: e.tensor_scalar(gwk[:, M32b:M32b + 32],
           gwk[:, M32a:M32a + 32], 0.5, None, ALU.is_lt))
    g.step("vector", lambda e: e.tensor_reduce(gwk[:, M32a:M32a + 32], v16g(RAW),
           axis=AX.X, op=ALU.max))
    g.step("vector", lambda e: rep_ie(e, 0, M32a))
    g.step("vector", lambda e: e.tensor_tensor(gwk[:, WSEL:WSEL + 512],
           gwk[:, RAW:RAW + 512], gb2[:], ALU.is_lt))
    g.step("vector", lambda e: e.tensor_scalar(gwk[:, WSEL:WSEL + 512],
           gwk[:, WSEL:WSEL + 512], -1.0, 1e32, ALU.add, ALU.mult))
    g.step("vector", lambda e: e.tensor_tensor(gwk[:, WSEL:WSEL + 512],
           gwk[:, RAW:RAW + 512], gwk[:, WSEL:WSEL + 512], ALU.add))
    g.step("vector", lambda e: e.tensor_reduce(gwk[:, M32a:M32a + 32], v16g(WSEL),
           axis=AX.X, op=ALU.max))
    g.step("vector", lambda e: rep_ie(e, 0, M32a))
    g.step("vector", lambda e: e.tensor_tensor(gwk[:, WSEL:WSEL + 512],
           gwk[:, RAW:RAW + 512], gb2[:], ALU.is_ge))
    g.step("vector", lambda e: rep_ie(e, 0, M32b))
    g.step("vector", lambda e: e.tensor_tensor(gwk[:, WSEL:WSEL + 512],
           gwk[:, WSEL:WSEL + 512], gb2[:], ALU.mult))
    g.step("vector", lambda e: e.tensor_tensor(gwk[:, WSEL:WSEL + 512],
           gwk[:, WSEL:WSEL + 512], gwk[:, SCR:SCR + 512], ALU.add))
    g.step("vector", lambda e: e.tensor_scalar(gwk[:, SCR:SCR + 512],
           gwk[:, WSEL:WSEL + 512], -1.0, -1e32, ALU.add, ALU.mult))
    g.step("vector", lambda e: e.tensor_tensor(gwk[:, SCR:SCR + 512],
           gwk[:, GA:GA + 512], gwk[:, SCR:SCR + 512], ALU.add))
    g.step("vector", lambda e: e.tensor_reduce(gwk[:, M32a:M32a + 32], v16g(SCR),
           axis=AX.X, op=ALU.max))
    g.step("vector", lambda e: rep_ie(e, 0, M32a))
    g.step("vector", lambda e: e.tensor_tensor(gwk[:, SCR:SCR + 512],
           gwk[:, SCR:SCR + 512], gb2[:], ALU.subtract))

    def expblk(e):
        for i in range(NT):
            ins = e.activation(gwk[:, SCR + i * 16:SCR + (i + 1) * 16],
                               gwk[:, SCR + i * 16:SCR + (i + 1) * 16], AF.Exp,
                               accum_out=gwk[:, M32a + i:M32a + i + 1])
        return ins
    g.step("scalar", expblk)
    g.step("vector", lambda e: e.reciprocal(gwk[:, M32b:M32b + 32],
           gwk[:, M32a:M32a + 32]))
    g.step("vector", lambda e: rep_ie(e, 0, M32b))
    gate_done = g.step("vector", lambda e: e.tensor_tensor(gwk[:, WSEL:WSEL + 512],
                       gwk[:, SCR:SCR + 512], gb2[:], ALU.mult))
    s_g = g.s

    qa0, qae0 = B0["qa"], B0["qae"]
    pwl0 = [x.n for x in s_pwl]

    def pool_pw(e):
        kk = 0
        for pj in range(3):
            for eidx in range(E):
                if kk >= 2:
                    e.wait_ge(s_qa.h, qa0 + (kk - 1) * NT)
                sl = s_pwl[kk % 2]
                e.dma_start(out=pw[:, (kk % 2) * 1024:(kk % 2 + 1) * 1024],
                            in_=AP(W["qkvpT"].tensor, (pj * E + eidx) * 131072,
                                   [[128, 128], [16384, 8], [1, 128]])).then_inc(sl.h, 16)
                sl.inc(16)
                kk += 1
    block.gpsimd(pool_pw)

    def pe_qkv(e):
        e.wait_ge(s_g.h, gate_done)
        settle(e)
        kk = 0
        for pj in range(3):
            for eidx in range(E):
                e.wait_ge(s_pwl[kk % 2].h, pwl0[kk % 2] + 16 * (kk // 2 + 1))
                for i in range(NT):
                    gi = kk * NT + i
                    bank = pc[4 + gi % 2]
                    if gi >= 2:
                        e.wait_ge(s_qae.h, qae0 + gi - 1 + pj)
                    for ch in range(8):
                        ins = e.matmul(bank[:, 0:128],
                                       n_fm[:, ch * NTOK + i * 128:ch * NTOK + (i + 1) * 128],
                                       pw[:, (kk % 2) * 1024 + ch * 128:
                                          (kk % 2) * 1024 + (ch + 1) * 128],
                                       start=(ch == 0), stop=(ch == 7))
                    ins.then_inc(s_qa.h, 1)
                    s_qa.inc()
                kk += 1
    block.tensor(pe_qkv)

    def dve_qkv(e):
        e.wait_ge(s_g.h, gate_done)
        settle(e)
        kk = 0
        for pj in range(3):
            e.wait_ge(s_qae.h, s_qae.n)
            settle(e)
            for eidx in range(E):
                for i in range(NT):
                    gi = kk * NT + i
                    bank = pc[4 + gi % 2]
                    e.wait_ge(s_qa.h, qa0 + gi + 1)
                    settle(e, 256)
                    dst = acc[:, i * 128:(i + 1) * 128]
                    wc = gwk[:, WSEL + i * 16 + eidx:WSEL + i * 16 + eidx + 1]
                    if eidx == 0:
                        e.tensor_scalar(dst, bank[:, 0:128], wc, None,
                                        ALU.mult).then_inc(s_qae.h, 1)
                    else:
                        e.scalar_tensor_tensor(dst, bank[:, 0:128], wc, dst,
                                               ALU.mult, ALU.add).then_inc(s_qae.h, 1)
                    s_qae.inc()
                kk += 1
            e.wait_ge(s_qae.h, s_qae.n)
            settle(e)
            if pj == 2:
                ins = e.tensor_copy(vt[:], acc[:])
            else:
                ins = e.tensor_copy(ot[:], acc[:])
            ins.then_inc(s_qae.h, 1)
            s_qae.inc()
    block.vector(dve_qkv)

    att0, ate0 = B0["att"], B0["ate"]

    def pe_qtr(e):
        for pj in range(2):
            e.wait_ge(s_qae.h, qae0 + (pj + 1) * (E * NT + 1))
            settle(e)
            for i in range(NT):
                gi = pj * NT + i
                bank = trA if gi % 2 == 0 else trB
                if gi >= 2:
                    e.wait_ge(s_ate.h, ate0 + gi - 1)
                e.transpose(bank[:, 0:128], ot[:, i * 128:(i + 1) * 128],
                            b.idb[:]).then_inc(s_att.h, 1)
                s_att.inc()
    block.tensor(pe_qtr)

    def dve_qtr(e):
        for pj in range(2):
            dstb = qf if pj == 0 else kf
            for i in range(NT):
                gi = pj * NT + i
                bank = trA if gi % 2 == 0 else trB
                e.wait_ge(s_att.h, att0 + gi + 1)
                settle(e, 256)
                e.tensor_copy(dstb[:, i * 128:(i + 1) * 128],
                              bank[:, 0:128]).then_inc(s_ate.h, 1)
                s_ate.inc()
    block.vector(dve_qtr)
    qtr_att = s_att.n
    qtr_ate = s_ate.n

    ac = Stepper(b, block, "f3a", extra_waits=[(s_ate, qtr_ate),
                                               (s_qae, s_qae.n)])
    for sq in range(SEQ_PER_CORE):
        for qt in range(TPS):
            i = sq * TPS + qt

            def smm1(e, sq=sq, i=i):
                e.matmul(pc[0][:], qf[:, i * 128:(i + 1) * 128],
                         kf[:, sq * T:sq * T + 512], start=True, stop=True)
                return e.matmul(pc[1][:], qf[:, i * 128:(i + 1) * 128],
                                kf[:, sq * T + 512:sq * T + 1024],
                                start=True, stop=True)
            ac.step("tensor", smm1)
            ac.step("vector", lambda e: (e.tensor_copy(sc_sb[:, 0:512], pc[0][:]),
                                         e.tensor_copy(sc_sb[:, 512:1024],
                                                       pc[1][:]))[-1])

            def smm2(e, sq=sq, i=i):
                e.matmul(pc[0][:], qf[:, i * 128:(i + 1) * 128],
                         kf[:, sq * T + 1024:sq * T + 1536], start=True, stop=True)
                return e.matmul(pc[1][:], qf[:, i * 128:(i + 1) * 128],
                                kf[:, sq * T + 1536:sq * T + 2048],
                                start=True, stop=True)
            ac.step("tensor", smm2)
            ac.step("vector", lambda e: (e.tensor_copy(sc_sb[:, 1024:1536], pc[0][:]),
                                         e.tensor_copy(sc_sb[:, 1536:2048],
                                                       pc[1][:]))[-1])
            ac.step("vector", lambda e: e.reduce_max(mcol[:, 0:1],
                    sc_sb[:, 0:2048], axis=AX.X))
            ac.step("vector", lambda e: e.tensor_scalar_mul(mcol[:, 1:2],
                    mcol[:, 0:1], -ATT_SCALE))
            ac.step("scalar", lambda e: e.activation(pt_sb[:], sc_sb[:, 0:2048],
                    AF.Exp, bias=mcol[:, 1:2], scale=ATT_SCALE,
                    accum_out=mcol[:, 0:1]))
            ac.step("vector", lambda e: e.reciprocal(mcol[:, 1:2], mcol[:, 0:1]))
            for kt in range(TPS):
                ac.step("tensor", lambda e, kt=kt: e.transpose(trA[:, 0:128],
                        pt_sb[:, kt * 128:(kt + 1) * 128], b.idb[:]),
                        settle_cycles=128)
                ac.step("vector", lambda e: e.tensor_copy(ptT[:], trA[:, 0:128]),
                        settle_cycles=128)
                ac.step("tensor", lambda e, sq=sq, kt=kt: e.matmul(
                        pc[2][:, 0:128], ptT[:],
                        vt[:, (sq * TPS + kt) * 128:(sq * TPS + kt + 1) * 128],
                        start=(kt == 0), stop=(kt == TPS - 1),
                        skip_group_check=True), settle_cycles=128)
            ac.step("vector", lambda e, i=i: e.tensor_scalar(
                ot[:, i * 128:(i + 1) * 128], pc[2][:, 0:128], mcol[:, 1:2],
                None, ALU.mult))
    att_done = ac.s.n
    s_ac = ac.s

    oy0, oye0 = B0["oy"], B0["oye"]

    def pe_out(e):
        e.wait_ge(s_ac.h, att_done)
        e.wait_ge(s_w.h, w_done)
        settle(e)
        for i in range(NT):
            bank = trA if i % 2 == 0 else trB
            if i >= 2:
                e.wait_ge(s_ate.h, qtr_ate + i - 1)
            e.transpose(bank[:, 0:128], ot[:, i * 128:(i + 1) * 128],
                        b.idb[:]).then_inc(s_att.h, 1)
            s_att.inc()
            e.wait_ge(s_ate.h, qtr_ate + i + 1)
            settle(e)
            for eidx in range(E):
                gi = i * E + eidx
                if gi >= 1:
                    e.wait_ge(s_oye.h, oye0 + 2 * gi)
                e.matmul(pc[4][:], ofm1[:],
                         opw[:, eidx * 1024:eidx * 1024 + 512],
                         start=True, stop=True).then_inc(s_oy.h, 1)
                s_oy.inc()
                e.matmul(pc[5][:], ofm1[:],
                         opw[:, eidx * 1024 + 512:(eidx + 1) * 1024],
                         start=True, stop=True).then_inc(s_oy.h, 1)
                s_oy.inc()
    block.tensor(pe_out)

    def dve_out(e):
        for i in range(NT):
            e.wait_ge(s_att.h, qtr_att + i + 1)
            settle(e, 256)
            bank = trA if i % 2 == 0 else trB
            e.tensor_copy(ofm1[:], bank[:, 0:128]).then_inc(s_ate.h, 1)
            s_ate.inc()
            for eidx in range(E):
                gi = i * E + eidx
                e.wait_ge(s_oy.h, oy0 + 2 * gi + 2)
                settle(e, 256)
                wc = gwk[:, WSEL + i * 16 + eidx:WSEL + i * 16 + eidx + 1]
                if eidx == 0:
                    e.tensor_scalar(oacc[:, 0:512], pc[4][:], wc, None, ALU.mult)
                    e.tensor_scalar(oacc[:, 512:1024], pc[5][:], wc, None,
                                    ALU.mult).then_inc(s_oye.h, 2)
                else:
                    e.scalar_tensor_tensor(oacc[:, 0:512], pc[4][:], wc,
                                           oacc[:, 0:512], ALU.mult, ALU.add)
                    e.scalar_tensor_tensor(oacc[:, 512:1024], pc[5][:], wc,
                                           oacc[:, 512:1024], ALU.mult,
                                           ALU.add).then_inc(s_oye.h, 2)
                s_oye.inc(2)
            e.wait_ge(s_oye.h, s_oye.n)
            e.wait_ge(s_xr2[i % 4].h, xr20[i % 4] + 16 * (i // 4 + 1))
            settle(e, 256)
            if i >= 2:
                e.wait_ge(s_st[i % 2].h, st0[i % 2] + 16 * (i // 2))
            sl = yt[:, (i % 2) * C:(i % 2 + 1) * C]
            e.scalar_tensor_tensor(sl[:], oacc[:], scale,
                                   xr2[:, (i % 4) * C:(i % 4 + 1) * C],
                                   ALU.mult, ALU.add).then_inc(s_ye.h, 1)
            s_ye.inc()
    block.vector(dve_out)

    def stores(e):
        for i in range(NT):
            e.wait_ge(s_ye.h, ye0 + i + 1)
            sl = s_st[i % 2]
            e.dma_start(out=xo_hbm[i * 128:(i + 1) * 128, :],
                        in_=yt[:, (i % 2) * C:(i % 2 + 1) * C]).then_inc(sl.h, 16)
            sl.inc(16)
        e.wait_ge(s_st[0].h, s_st[0].n)
        e.wait_ge(s_st[1].h, s_st[1].n)
    block.gpsimd(stores)


def f3_host_weights(z):
    sim = np.ascontiguousarray(z["L3_ffn_sim_matrix"])
    gatesb = np.ascontiguousarray(
        np.broadcast_to(z["L3_ffn_gates"], (128, 16))).astype(np.float32)
    qkvpT = np.ascontiguousarray(np.concatenate(
        [z["L3_ffn_q_proj"], z["L3_ffn_k_proj"], z["L3_ffn_v_proj"]], axis=0))
    opT = np.ascontiguousarray(z["L3_ffn_o_proj"])
    return dict(sim=sim, gatesb=gatesb, qkvpT=qkvpT, opT=opT)


def build_F3_program(z):
    b = Bld("f3")
    hw = f3_host_weights(z)
    x_in = b.inp("x_in", np.zeros((NTOK, C), np.float32))
    W = {k: b.inp("f3_" + k, v) for k, v in hw.items()}
    xo = b.out("x_out", (NTOK, C), F32)
    with b:
        emit_F3(b, b.block, x_in, xo, W, scale=0.5)
        b.finish()
    return b, hw


def emit_A3(b, block, x_hbm, xo_hbm, W):
    nc = b.nc
    H = 16
    SC8 = 0.125
    trA = b.ps("a3_trA", [128, 1024], BF16)
    trB = b.ps("a3_trB", [128, 1024], BF16)
    pc = [b.ps(f"a3_pc{k}", [128, 512], F32) for k in range(6)]
    nb = b.sb("a3_nb", [128, 8 * T], BF16)
    kfb = b.sb("a3_kf", [128, 8 * T], BF16)
    vtb = b.sb("a3_vt", [128, 16 * 1024], BF16)
    wq_sb = b.sb("a3_wq", [128, 8 * 1024], BF16)
    wsh = b.sb("a3_wsh", [128, 16 * 1024], BF16)
    qtb = b.sb("a3_qt", [128, 8 * 128], BF16)
    aob = b.sb("a3_ao", [128, 8 * 128], BF16)
    aof = b.sb("a3_aof", [128, 8 * 128], BF16)
    stg = b.sb("a3_stg", [128, 8 * 128], BF16)
    ohfm = b.sb("a3_oh", [8, NTOK], BF16)
    ohtm = b.sb("a3_ohtm", [128, 8], BF16)
    cntb = b.sb("a3_cnt", [8, 2], BF16)
    i1tmp = b.sb("a3_i1t", [1, 512], BF16)
    i1col = b.sb("a3_i1c", [128, 32], BF16)
    i1rep = b.sb("a3_i1rep", [128, 128], F32)
    i0rep = b.sb("a3_i0rep", [128, 128], F32)
    pt_sb = b.sb("a3_pt", [128, 2048], BF16)
    ptT = b.sb("a3_ptT", [128, 1024], BF16)
    mcol = b.sb("a3_mcol", [128, 2], F32)
    xr_ln = b.sb("lnf_xr_a3", [128, 4 * C], F32)
    scb = xr_ln
    binq = b.sb("a3_binqs", [128, 16], F32)
    bink = b.sb("a3_binks", [128, 16], F32)
    binv = b.sb("a3_binvs", [128, 1024], BF16)
    bout = b.sb("a3_bouts", [128, 16], F32)
    fbr = b.sb("a3_fbr", [128, 1024], BF16)
    prot = b.sb("a3_prot", [128, 8 * 8], BF16)
    yt = b.sb("a3_yt", [128, C], F32)

    n_scr = b.scratch("a3_nscr", [128, 8 * NTOK], BF16)
    l_scr = b.scratch("a3_lscr", [128, 8 * NTOK], BF16)
    s_scr = b.scratch("a3_sscr", [128, 8 * NTOK], BF16)

    s_w = b.sem("a3_w")
    s_wp = [b.sem(f"a3_wp{k}") for k in range(2)]
    s_ns = b.sem("a3_ns")
    s_ls = b.sem("a3_ls")
    s_ll = b.sem("a3_ll")
    s_ye = b.sem("a3_ye")
    s_st = [b.sem(f"a3_st{k}") for k in range(2)]
    s_xr2 = [b.sem(f"a3_xr2{k}") for k in range(4)]

    def wloads(e):
        e.dma_start(out=prot[:], in_=AP(W["protT"].tensor, 0,
                    [[8, 128], [1024, 8], [1, 8]])).then_inc(s_w.h, 16)
        e.dma_start(out=binq[:], in_=W["binq"]).then_inc(s_w.h, 16)
        e.dma_start(out=bink[:], in_=W["bink"]).then_inc(s_w.h, 16)
        e.dma_start(out=bout[:], in_=W["bout"]).then_inc(s_w.h, 16)
        e.dma_start(out=fbr[:], in_=W["fbrep"]).then_inc(s_w.h, 16)
        s_w.inc(80)
    block.gpsimd(wloads)
    w_done = s_w.n

    ye0 = s_ye.n
    st0 = [x.n for x in s_st]
    xr20 = [x.n for x in s_xr2]
    xr_holder = [None]

    seg = Stepper(b, block, "a3s")
    for sq in range(SEQ_PER_CORE):
        pre = [(seg.s, seg.s.n), (s_ns, s_ns.n)] if sq else []
        s_nfm, xr2, s_lnnr = emit_ln_fm(b, block, x_hbm, nb, "a3", trA, trB,
                                        t0=sq * TPS, nt=TPS, pre_waits=pre)
        xr_holder[0] = xr2
        nfm_done = s_nfm.n
        if sq == 0:
            seg.extra = [(s_nfm, nfm_done), (s_w, w_done), (b.s_cstg, b.s_cstg.n)]
        else:
            seg.extra = [(s_nfm, nfm_done)]

        def nstore(e, sq=sq, nfm_done=nfm_done):
            e.wait_ge(s_nfm.h, nfm_done)
            for ch in range(8):
                e.dma_start(out=AP(n_scr.tensor, ch * NTOK + sq * T,
                                   [[8 * NTOK, 128], [1, T]]),
                            in_=nb[:, ch * T:(ch + 1) * T]).then_inc(s_ns.h, 16)
                s_ns.inc(16)
        block.sync(nstore)

        for it in range(TPS):
            def pmm(e, it=it):
                for ch in range(8):
                    ins = e.matmul(pc[0][:, 0:8],
                                   nb[:, ch * T + it * 128:ch * T + (it + 1) * 128],
                                   prot[:, ch * 8:(ch + 1) * 8],
                                   start=(ch == 0), stop=(ch == 7))
                return ins
            seg.step("tensor", pmm, settle_cycles=128)
            seg.step("vector", lambda e: e.tensor_copy(scb[:, 0:8], pc[0][:, 0:8]),
                     settle_cycles=128)
            seg.step("vector", lambda e: e.reduce_max(mcol[:, 0:1], scb[:, 0:8],
                     axis=AX.X), settle_cycles=128)
            seg.step("vector", lambda e: e.tensor_scalar(ohtm[:], scb[:, 0:8],
                     mcol[:, 0:1], None, ALU.is_ge), settle_cycles=128)
            seg.step("tensor", lambda e: e.transpose(trA[0:8, 0:128], ohtm[:],
                     b.idb[:]), settle_cycles=128)
            seg.step("vector", lambda e, sq=sq, it=it: e.tensor_copy(
                ohfm[0:8, sq * T + it * 128:sq * T + (it + 1) * 128],
                trA[0:8, 0:128]), settle_cycles=128)
        seg.step("vector", lambda e, sq=sq: e.reduce_sum(
            scb[0:8, 0:1], ohfm[0:8, sq * T:(sq + 1) * T], axis=AX.X),
            settle_cycles=128)
        seg.step("vector", lambda e, sq=sq: e.tensor_copy(cntb[0:8, sq:sq + 1],
                 scb[0:8, 0:1]), settle_cycles=128)
        for s4 in range(4):
            seg.step("tensor", lambda e, sq=sq, s4=s4: e.matmul(
                pc[1][0:1, 0:512], cntb[0:8, sq:sq + 1],
                ohfm[0:8, sq * T + s4 * 512:sq * T + (s4 + 1) * 512],
                start=True, stop=True), settle_cycles=128)
            seg.step("vector", lambda e: e.tensor_scalar(
                i1tmp[0:1, :], pc[1][0:1, 0:512], 1.5, None, ALU.is_lt),
                settle_cycles=128)

            def i1tr(e, s4=s4):
                for j in range(4):
                    ins = e.matmul(pc[1][:, j:j + 1],
                                   i1tmp[0:1, j * 128:(j + 1) * 128],
                                   b.cst_ones_b[0:1, 0:1], start=True, stop=True)
                return ins
            seg.step("tensor", i1tr, settle_cycles=128)
            seg.step("vector", lambda e, sq=sq, s4=s4: e.tensor_copy(
                i1col[:, sq * 16 + s4 * 4:sq * 16 + (s4 + 1) * 4],
                pc[1][:, 0:4]), settle_cycles=128)
    seg_done = seg.s.n
    s_seg = seg.s

    mha = Stepper(b, block, "a3m", extra_waits=[(s_seg, seg_done),
                                                (s_ns, s_ns.n)])
    wp_cnt = [0, 0]

    for mi, pref in enumerate(("lt", "st")):
        for sq in range(SEQ_PER_CORE):
            def phw(e, mi=mi, nwait=mha.s.n):
                e.wait_ge(mha.s.h, nwait)
                e.dma_start(out=wsh[:], in_=AP(W[("lt" if mi == 0 else "st") + "_wkvT"].tensor,
                            0, [[2048, 128], [262144, 8], [1, 2048]])).then_inc(s_wp[0].h, 16)
                e.dma_start(out=wq_sb[:], in_=AP(W[("lt" if mi == 0 else "st") + "_wqT"].tensor,
                            0, [[1024, 128], [131072, 8], [1, 1024]])).then_inc(s_wp[1].h, 16)
                e.dma_start(out=binv[:], in_=AP(W["binv"].tensor, mi * 1024,
                            [[2048, 128], [1, 1024]])).then_inc(s_wp[1].h, 16)
            block.gpsimd(phw)
            s_wp[0].inc(16)
            s_wp[1].inc(32)
            def nload(e, sq=sq, nwait=mha.s.n, nsw=s_ns.n):
                e.wait_ge(s_seg.h, seg_done)
                e.wait_ge(s_ns.h, nsw)
                e.wait_ge(mha.s.h, nwait)
                e.dma_start(out=nb[:], in_=AP(n_scr.tensor, sq * T,
                            [[8 * NTOK, 128], [NTOK, 8], [1, T]])).then_inc(s_ll.h, 16)
                s_ll.inc(16)
            block.gpsimd(nload)
            mha.extra = [(s_wp[0], s_wp[0].n), (s_wp[1], s_wp[1].n),
                         (s_ll, s_ll.n), (s_ls, s_ls.n)]
            for ch in range(8):
                for sl4 in range(4):
                    def kmm(e, ch=ch, sl4=sl4):
                        for ci in range(8):
                            ins = e.matmul(pc[0][:],
                                           wsh[:, ci * 2048 + ch * 128:ci * 2048 + (ch + 1) * 128],
                                           nb[:, ci * T + sl4 * 512:ci * T + (sl4 + 1) * 512],
                                           start=(ci == 0), stop=(ci == 7))
                        return ins
                    mha.step("tensor", kmm, settle_cycles=128)
                    mha.step("scalar", lambda e, ch=ch, sl4=sl4, mi=mi:
                             e.activation(kfb[:, ch * T + sl4 * 512:ch * T + (sl4 + 1) * 512],
                                          pc[0][:], AF.Identity,
                                          bias=bink[:, mi * 8 + ch:mi * 8 + ch + 1]),
                             settle_cycles=128)
            for it in range(TPS):
                for sl2 in range(2):
                    def vmm(e, it=it, sl2=sl2):
                        for ci in range(8):
                            ins = e.matmul(pc[1][:],
                                           nb[:, ci * T + it * 128:ci * T + (it + 1) * 128],
                                           wsh[:, ci * 2048 + 1024 + sl2 * 512:
                                               ci * 2048 + 1024 + (sl2 + 1) * 512],
                                           start=(ci == 0), stop=(ci == 7))
                        return ins
                    mha.step("tensor", vmm, settle_cycles=128)
                    mha.step("vector", lambda e, it=it, sl2=sl2, mi=mi:
                             e.tensor_tensor(vtb[:, it * 1024 + sl2 * 512:
                                                 it * 1024 + (sl2 + 1) * 512],
                                             pc[1][:],
                                             binv[:, sl2 * 512:(sl2 + 1) * 512],
                                             ALU.add), settle_cycles=128)
            def phw2(e, mi=mi, nwait=mha.s.n):
                e.wait_ge(mha.s.h, nwait)
                e.dma_start(out=wsh[:, 0:8 * 1024],
                            in_=AP(W[("lt" if mi == 0 else "st") + "_woutT"].tensor,
                                   0, [[1024, 128], [131072, 8], [1, 1024]])
                            ).then_inc(s_wp[0].h, 16)
            block.gpsimd(phw2)
            s_wp[0].inc(16)
            mha.extra = [(s_wp[0], s_wp[0].n)]

            for qt in range(TPS):
                def qmm(e, qt=qt):
                    for ch in range(8):
                        bank = pc[4 + ch // 4]
                        for ci in range(8):
                            ins = e.matmul(bank[:, (ch % 4) * 128:(ch % 4 + 1) * 128],
                                           wq_sb[:, ci * 1024 + ch * 128:ci * 1024 + (ch + 1) * 128],
                                           nb[:, ci * T + qt * 128:ci * T + (qt + 1) * 128],
                                           start=(ci == 0), stop=(ci == 7))
                    return ins
                mha.step("tensor", qmm, settle_cycles=128)

                def qev(e, mi=mi):
                    for hb in range(2):
                        for ch4 in range(4):
                            ch = hb * 4 + ch4
                            ins = e.activation(qtb[:, ch * 128:(ch + 1) * 128],
                                               pc[4 + hb][:, ch4 * 128:(ch4 + 1) * 128],
                                               AF.Identity,
                                               bias=binq[:, mi * 8 + ch:mi * 8 + ch + 1])
                    return ins
                mha.step("scalar", qev, settle_cycles=128)
                for h in range(H):
                    po, co = (h % 2) * 64, (h // 2) * 128
                    for w2 in range(2):
                        def smm(e, h=h, po=po, co=co, w2=w2):
                            e.matmul(pc[0][:], qtb[po:po + 64, co:co + 128],
                                     kfb[po:po + 64, (h // 2) * T + (2 * w2) * 512:
                                         (h // 2) * T + (2 * w2 + 1) * 512],
                                     start=True, stop=True)
                            return e.matmul(pc[1][:], qtb[po:po + 64, co:co + 128],
                                            kfb[po:po + 64, (h // 2) * T + (2 * w2 + 1) * 512:
                                                (h // 2) * T + (2 * w2 + 2) * 512],
                                            start=True, stop=True)
                        mha.step("tensor", smm, settle_cycles=128)
                        if pref == "st":
                            def samm(e, qt=qt, w2=w2, sq=sq):
                                e.matmul(pc[2][:],
                                         ohfm[0:8, sq * T + qt * 128:sq * T + (qt + 1) * 128],
                                         ohfm[0:8, sq * T + (2 * w2) * 512:
                                              sq * T + (2 * w2 + 1) * 512],
                                         start=True, stop=True)
                                return e.matmul(pc[3][:],
                                                ohfm[0:8, sq * T + qt * 128:sq * T + (qt + 1) * 128],
                                                ohfm[0:8, sq * T + (2 * w2 + 1) * 512:
                                                     sq * T + (2 * w2 + 2) * 512],
                                                start=True, stop=True)
                            mha.step("tensor", samm, settle_cycles=128)

                            def smsk1(e, w2=w2):
                                e.tensor_copy(scb[:, (2 * w2) * 512:(2 * w2 + 1) * 512],
                                              pc[0][:])
                                return e.tensor_copy(scb[:, (2 * w2 + 1) * 512:
                                                         (2 * w2 + 2) * 512], pc[1][:])
                            mha.step("vector", smsk1, settle_cycles=128)

                            def smsk2(e, w2=w2):
                                for j in range(2):
                                    c0 = (2 * w2 + j) * 512
                                    ins = e.scalar_tensor_tensor(scb[:, c0:c0 + 512],
                                                                 pc[2 + j][:], 1e32,
                                                                 scb[:, c0:c0 + 512],
                                                                 ALU.mult, ALU.add)
                                return ins
                            mha.step("vector", smsk2, settle_cycles=128)

                            def smsk3(e, w2=w2):
                                for j in range(2):
                                    c0 = (2 * w2 + j) * 512
                                    ins = e.tensor_scalar_add(scb[:, c0:c0 + 512],
                                                              scb[:, c0:c0 + 512],
                                                              -1e32)
                                return ins
                            mha.step("vector", smsk3, settle_cycles=128)
                        else:
                            def sev(e, w2=w2):
                                e.tensor_copy(scb[:, (2 * w2) * 512:(2 * w2 + 1) * 512],
                                              pc[0][:])
                                return e.tensor_copy(scb[:, (2 * w2 + 1) * 512:
                                                         (2 * w2 + 2) * 512], pc[1][:])
                            mha.step("vector", sev, settle_cycles=128)
                    mha.step("vector", lambda e: e.reduce_max(mcol[:, 0:1],
                             scb[:, 0:2048], axis=AX.X), settle_cycles=128)
                    mha.step("vector", lambda e: e.tensor_scalar_mul(mcol[:, 1:2],
                             mcol[:, 0:1], -SC8), settle_cycles=128)
                    mha.step("scalar", lambda e: e.activation(pt_sb[:],
                             scb[:, 0:2048], AF.Exp, bias=mcol[:, 1:2], scale=SC8,
                             accum_out=mcol[:, 0:1]), settle_cycles=128)
                    mha.step("vector", lambda e: e.reciprocal(mcol[:, 1:2],
                             mcol[:, 0:1]), settle_cycles=128)
                    for g8 in range(2):
                        def ptr8(e, g8=g8):
                            for j in range(8):
                                ins = e.transpose(trA[:, j * 128:(j + 1) * 128],
                                                  pt_sb[:, (g8 * 8 + j) * 128:
                                                        (g8 * 8 + j + 1) * 128],
                                                  b.idb[:])
                            return ins
                        mha.step("tensor", ptr8, settle_cycles=128)
                        mha.step("vector", lambda e: e.tensor_copy(ptT[:], trA[:]),
                                 settle_cycles=128)

                        def pv8(e, g8=g8, h=h):
                            for j in range(8):
                                kt = g8 * 8 + j
                                ins = e.matmul(pc[2][:, 0:64],
                                               ptT[:, j * 128:(j + 1) * 128],
                                               vtb[:, kt * 1024 + h * 64:kt * 1024 + (h + 1) * 64],
                                               start=(kt == 0), stop=(kt == 15),
                                               skip_group_check=True)
                            return ins
                        mha.step("tensor", pv8, settle_cycles=128)
                    mha.step("vector", lambda e, h=h: e.tensor_scalar(
                        aob[:, h * 64:(h + 1) * 64], pc[2][:, 0:64],
                        mcol[:, 1:2], None, ALU.mult), settle_cycles=128)
                def aotr(e):
                    for j in range(8):
                        ins = e.transpose(trB[:, j * 128:(j + 1) * 128],
                                          aob[:, j * 128:(j + 1) * 128], b.idb[:])
                    return ins
                mha.step("tensor", aotr, settle_cycles=128)
                mha.step("vector", lambda e: e.tensor_copy(aof[:], trB[:]),
                         settle_cycles=128)
                for c2 in range(2):
                    def omm(e, c2=c2):
                        for ch in range(4):
                            cg = c2 * 4 + ch
                            for k1 in range(8):
                                ins = e.matmul(pc[4 + c2][:, ch * 128:(ch + 1) * 128],
                                               wsh[:, k1 * 1024 + cg * 128:
                                                   k1 * 1024 + (cg + 1) * 128],
                                               aof[:, k1 * 128:(k1 + 1) * 128],
                                               start=(k1 == 0), stop=(k1 == 7))
                        return ins
                    mha.step("tensor", omm, settle_cycles=128)

                    def oev(e, c2=c2, mi=mi):
                        for ch in range(4):
                            cg = c2 * 4 + ch
                            ins = e.activation(stg[:, cg * 128:(cg + 1) * 128],
                                               pc[4 + c2][:, ch * 128:(ch + 1) * 128],
                                               AF.Identity,
                                               bias=bout[:, mi * 8 + cg:mi * 8 + cg + 1])
                        return ins
                    mha.step("scalar", oev, settle_cycles=128)

                def stg_store(e, mi=mi, sq=sq, qt=qt, nwait=mha.s.n):
                    e.wait_ge(mha.s.h, nwait)
                    scr = l_scr if mi == 0 else s_scr
                    e.dma_start(out=AP(scr.tensor, sq * T + qt * 128,
                                       [[8 * NTOK, 128], [NTOK, 8], [1, 128]]),
                                in_=AP(stg, 0, [[1024, 128], [128, 8], [1, 128]])
                                ).then_inc(s_ls.h, 16)
                    s_ls.inc(16)
                block.sync(stg_store)
                mha.extra = [(s_ls, s_ls.n)]
    mha_done = mha.s.n
    s_mha = mha.s

    def xr2_loads(e):
        e.wait_ge(s_mha.h, mha_done)
        xr2 = xr_holder[0]
        for i in range(NT):
            if i >= 4:
                e.wait_ge(s_ye.h, ye0 + i - 3)
            sl = s_xr2[i % 4]
            e.dma_start(out=xr2[:, (i % 4) * C:(i % 4 + 1) * C],
                        in_=x_hbm[i * 128:(i + 1) * 128, :]).then_inc(sl.h, 16)
            sl.inc(16)
    block.sync(xr2_loads)

    def fus_wload(e):
        e.wait_ge(s_mha.h, mha_done)
        e.dma_start(out=wsh[:], in_=AP(W["fusT"].tensor, 0,
                    [[1024, 128], [131072, 16], [1, 1024]])).then_inc(s_wp[0].h, 16)
        s_wp[0].inc(16)
    block.gpsimd(fus_wload)
    fusw_done = s_wp[0].n

    fus = Stepper(b, block, "a3f", extra_waits=[(s_mha, mha_done),
                                               (s_wp[0], fusw_done),
                                               (s_ls, s_ls.n)])
    xr2v = xr_holder[0]
    for i in range(NT):
        def tl_load(e, i=i, nwait=fus.s.n):
            e.wait_ge(fus.s.h, nwait)
            e.dma_start(out=qtb[:], in_=AP(l_scr.tensor, i * 128,
                        [[8 * NTOK, 128], [NTOK, 8], [1, 128]])).then_inc(s_ll.h, 16)
            e.dma_start(out=aob[:], in_=AP(s_scr.tensor, i * 128,
                        [[8 * NTOK, 128], [NTOK, 8], [1, 128]])).then_inc(s_ll.h, 16)
            e.dma_start(out=aof[:], in_=AP(n_scr.tensor, i * 128,
                        [[8 * NTOK, 128], [NTOK, 8], [1, 128]])).then_inc(s_ll.h, 16)
            s_ll.inc(48)
        block.gpsimd(tl_load)
        fus.extra = fus.extra + [(s_ll, s_ll.n), (s_st[0], s_st[0].n)]
        sq = i // TPS
        def i1mk(e, i=i):
            src = AP(i1col, i, [[32, 128], [0, 128]])
            return e.tensor_copy(i1rep[:], src)
        fus.step("vector", i1mk, settle_cycles=128)
        fus.step("vector", lambda e: e.tensor_scalar(i0rep[:], i1rep[:], -1.0,
                 -1.0, ALU.mult, ALU.subtract), settle_cycles=128)

        def blend1(e, i=i):
            for ch in range(8):
                e.tensor_tensor(stg[:, ch * 128:(ch + 1) * 128],
                                aob[:, ch * 128:(ch + 1) * 128],
                                i0rep[:], ALU.mult)
            for ch in range(8):
                ins = e.tensor_tensor(yt[:, ch * 128:(ch + 1) * 128],
                                      aof[:, ch * 128:(ch + 1) * 128], i1rep[:],
                                      ALU.mult)
            return ins
        fus.step("vector", blend1)

        def blend2(e, i=i):
            for ch in range(8):
                ins = e.tensor_tensor(stg[:, ch * 128:(ch + 1) * 128],
                                      stg[:, ch * 128:(ch + 1) * 128],
                                      yt[:, ch * 128:(ch + 1) * 128], ALU.add)
            return ins
        fus.step("vector", blend2)

        def fmm(e, i=i):
            for c2 in range(2):
                for k1 in range(8):
                    e.matmul(pc[4 + c2][:], qtb[:, k1 * 128:(k1 + 1) * 128],
                             wsh[:, k1 * 1024 + c2 * 512:k1 * 1024 + (c2 + 1) * 512],
                             start=(k1 == 0), stop=False)
                for k1 in range(8):
                    ins = e.matmul(pc[4 + c2][:], stg[:, k1 * 128:(k1 + 1) * 128],
                                   wsh[:, (8 + k1) * 1024 + c2 * 512:
                                       (8 + k1) * 1024 + (c2 + 1) * 512],
                                   start=False, stop=(k1 == 7))
            return ins
        fus.step("tensor", fmm)

        def yev(e, i=i):
            e.wait_ge(s_xr2[i % 4].h, xr20[i % 4] + 16 * (i // 4 + 1))
            sl = yt[:, 0:C]
            e.tensor_tensor(sl[:, 0:512], pc[4][:],
                            xr2v[:, (i % 4) * C:(i % 4) * C + 512], ALU.add)
            return e.tensor_tensor(sl[:, 512:1024], pc[5][:],
                                   xr2v[:, (i % 4) * C + 512:(i % 4 + 1) * C],
                                   ALU.add)
        fus.step("vector", yev)
        fus.step("vector", lambda e: e.tensor_tensor(yt[:, 0:C], yt[:, 0:C],
                 fbr[:], ALU.add))

        def st_store(e, i=i, nwait=fus.s.n):
            e.wait_ge(fus.s.h, nwait)
            sl = s_st[0]
            e.dma_start(out=xo_hbm[i * 128:(i + 1) * 128, :],
                        in_=yt[:, 0:C]).then_inc(sl.h, 16)
            sl.inc(16)
            s_ye.inc()
            e.sem_inc(s_ye.h, 1)
        block.gpsimd(st_store)
    def fin(e):
        e.wait_ge(s_st[0].h, s_st[0].n)
        e.wait_ge(s_st[1].h, s_st[1].n)
    block.gpsimd(fin)


def a3_host_weights(z):
    def T_(a):
        return np.ascontiguousarray(a.T)
    out = {}
    for p in ("lt", "st"):
        wi = z[f"L3_attn_{p}_w_in"]
        bi = z[f"L3_attn_{p}_b_in"]
        out[p + "_wqT"] = T_(wi[0:1024])
        out[p + "_wkvT"] = T_(wi[1024:3072])
        out[p + "_woutT"] = T_(z[f"L3_attn_{p}_w_out"])
        out[p + "_binq"] = np.ascontiguousarray(bi[0:1024].reshape(8, 128).T)
        out[p + "_bink"] = np.ascontiguousarray(bi[1024:2048].reshape(8, 128).T)
        out[p + "_binv"] = np.ascontiguousarray(
            np.broadcast_to(bi[2048:3072], (128, 1024)))
        out[p + "_bout"] = np.ascontiguousarray(
            z[f"L3_attn_{p}_b_out"].reshape(8, 128).T)
    binq = np.concatenate([out["lt_binq"], out["st_binq"]], axis=1)
    bink = np.concatenate([out["lt_bink"], out["st_bink"]], axis=1)
    binv = np.concatenate([out["lt_binv"], out["st_binv"]], axis=1)
    bout = np.concatenate([out["lt_bout"], out["st_bout"]], axis=1)
    hw = dict(protT=T_(z["L3_attn_protos"]), fusT=T_(z["L3_attn_fus_w"]),
              fbrep=np.ascontiguousarray(
                  np.broadcast_to(z["L3_attn_fus_b"], (128, 1024))),
              binq=binq, bink=bink, binv=binv, bout=bout)
    for p in ("lt", "st"):
        hw[p + "_wqT"] = out[p + "_wqT"]
        hw[p + "_wkvT"] = out[p + "_wkvT"]
        hw[p + "_woutT"] = out[p + "_woutT"]
    return hw


def build_A3_program(z):
    b = Bld("a3")
    hw = a3_host_weights(z)
    x_in = b.inp("x_in", np.zeros((NTOK, C), np.float32))
    W = {k: b.inp("a3_" + k, v) for k, v in hw.items()}
    xo = b.out("x_out", (NTOK, C), F32)
    with b:
        emit_A3(b, b.block, x_in, xo, W)
        b.finish()
    return b, hw


RMS_EPS = 1e-6
NEG30 = -1e30


def emit_F0(b, block, x_hbm, xo_hbm, W, scale=0.5):
    nc = b.nc
    H = 16
    SCA = 0.125
    NG = 4
    TPG = NTOK // NG
    TLG = TPG // 128

    trA = b.ps("f0_trA", [128, 1024], BF16)
    trB = b.ps("f0_trB", [128, 1024], BF16)
    pc = [b.ps(f"f0_pc{k}", [128, 512], F32) for k in range(6)]

    AR = 49152
    arena = b.sb("f0_ar", [128, AR], BF16)
    FAR = 10240
    far = b.sb("f0_far", [128, FAR], F32)
    wbig = b.sb("f0_wb", [128, 8192], BF16)
    wb2 = b.sb("f0_wb2", [128, 8192], BF16)
    hr = b.sb("f0_hr", [128, 2 * C], F32)
    rope_c = b.sb("f0_rc", [128, 1024], F32)
    rope_s = b.sb("f0_rs", [128, 1024], F32)
    rmsw = b.sb("f0_rmsw", [128, 1024], F32)
    pt_sb = b.sb("f0_pt", [128, 2048], BF16)
    ptT = b.sb("f0_ptT", [128, 1024], BF16)
    ao = b.sb("f0_ao", [128, 1024], BF16)
    aof = b.sb("f0_aof", [128, 1024], BF16)
    qt_fm = b.sb("f0_qt", [128, 1024], BF16)
    addm = b.sb("f0_admsk", [128, 128], F32)
    wfm_g = b.sb("f0_wfm", [8, 1024], BF16)
    bc_sb = b.sb("f0_bc", [8, 1024], BF16)
    gb = b.sb("f0_gb", [128, 320], F32)
    glog = b.sb("f0_glog", [128, 64], F32)
    stats = b.sb("f0_st2", [128, 16], F32)
    mcol = b.sb("f0_mc", [128, 4], F32)

    h_scr = b.scratch("f0_hscr", [NTOK, C], F32)

    s_w = b.sem("f0_w")
    s_wl = b.sem("f0_wl")
    s_hld = b.sem("f0_hld")
    s_hst = b.sem("f0_hst")
    s_rpc = b.sem("f0_rpc")

    def boot(e):
        e.dma_start(out=addm[:], in_=W["addm"]).then_inc(s_w.h, 16)
        e.dma_start(out=bc_sb[:], in_=W["bc8"]).then_inc(s_w.h, 16)
        s_w.inc(32)
    block.gpsimd(boot)

    ch = Stepper(b, block, "f0", extra_waits=[
        (b.s_cst, b.s_cst.n), (b.s_cstg, b.s_cstg.n), (s_w, s_w.n)])

    def sync_load(dmas):
        def f(e, nwait=ch.s.n, dmas=dmas):
            e.wait_ge(ch.s.h, nwait)
            for dst, src in dmas:
                e.dma_start(out=dst, in_=src).then_inc(s_hld.h, 16)
                s_hld.inc(16)
        block.sync(f)
        ch.extra = ch.extra + [(s_hld, s_hld.n)]

    def gp_load(dmas, sem):
        def f(e, nwait=ch.s.n, dmas=dmas):
            e.wait_ge(ch.s.h, nwait)
            for dst, src in dmas:
                e.dma_start(out=dst, in_=src).then_inc(sem.h, 16)
                sem.inc(16)
        block.gpsimd(f)
        ch.extra = ch.extra + [(sem, sem.n)]

    def gp_store(dmas):
        def f(e, nwait=ch.s.n, dmas=dmas):
            e.wait_ge(ch.s.h, nwait)
            for dst, src in dmas:
                e.dma_start(out=dst, in_=src).then_inc(s_hst.h, 16)
                s_hst.inc(16)
        block.gpsimd(f)
        ch.extra = ch.extra + [(s_hst, s_hst.n)]

    for i in range(NT):
        sl = hr[:, (i % 2) * C:(i % 2 + 1) * C]
        sync_load([(sl, x_hbm[i * 128:(i + 1) * 128, :])])
        ch.step("scalar", lambda e, sl=sl: e.activation(
            far[:, 8192:9216], sl, AF.Square, accum_out=stats[:, 1:2]))
        ch.step("vector", lambda e, sl=sl: e.reduce_sum(
            stats[:, 0:1], sl, axis=AX.X))
        ch.step("vector", lambda e: e.tensor_scalar_mul(
            stats[:, 2:3], stats[:, 0:1], 1.0 / C))
        ch.step("vector", lambda e: e.tensor_scalar_mul(
            stats[:, 3:4], stats[:, 1:2], 1.0 / C))
        ch.step("vector", lambda e: e.tensor_tensor(
            stats[:, 4:5], stats[:, 2:3], stats[:, 2:3], ALU.mult))
        ch.step("vector", lambda e: e.tensor_tensor(
            stats[:, 3:4], stats[:, 3:4], stats[:, 4:5], ALU.subtract))
        ch.step("scalar", lambda e: e.activation(
            stats[:, 5:6], stats[:, 3:4], AF.Sqrt,
            bias=b.cstcol("ln_eps", LN_EPS)))
        ch.step("vector", lambda e: e.reciprocal(stats[:, 6:7], stats[:, 5:6]))
        ch.step("vector", lambda e: e.tensor_tensor(
            stats[:, 7:8], stats[:, 2:3], stats[:, 6:7], ALU.mult))
        ch.step("vector", lambda e: e.tensor_scalar_mul(
            stats[:, 7:8], stats[:, 7:8], -1.0))
        ch.step("scalar", lambda e, sl=sl: e.activation(
            far[:, 0:1024], sl, AF.Identity,
            scale=stats[:, 6:7], bias=stats[:, 7:8]))
        gp_store([(h_scr[i * 128:(i + 1) * 128, :], far[:, 0:1024])])

    def rms_tile(src_tile, dst_bf, scr0):
        ch.step("scalar", lambda e: e.activation(
            far[:, scr0:scr0 + 1024], src_tile, AF.Square,
            accum_out=stats[:, 0:1]))
        ch.step("vector", lambda e: e.tensor_scalar_mul(
            stats[:, 1:2], stats[:, 0:1], 1.0 / C))
        ch.step("scalar", lambda e: e.activation(
            stats[:, 2:3], stats[:, 1:2], AF.Sqrt,
            bias=b.cstcol("rms_eps", RMS_EPS)))
        ch.step("vector", lambda e: e.reciprocal(stats[:, 3:4], stats[:, 2:3]))
        ch.step("scalar", lambda e: e.activation(
            far[:, scr0:scr0 + 1024], src_tile, AF.Identity,
            scale=stats[:, 3:4]))
        ch.step("vector", lambda e: e.tensor_tensor(
            dst_bf, far[:, scr0:scr0 + 1024], rmsw[:], ALU.mult))

    def tr8_to(dst_ap, src_sb, src0=0):
        def f(e):
            for j in range(8):
                ins = e.transpose(trA[:, j * 128:(j + 1) * 128],
                                  src_sb[:, src0 + j * 128:src0 + (j + 1) * 128],
                                  b.idb[:])
            return ins
        ch.step("tensor", f, settle_cycles=128)
        ch.step("vector", lambda e: e.tensor_copy(
            dst_ap, AP(trA, 0, [[1024, 128], [128, 8], [1, 128]])),
            settle_cycles=128)

    def proj_tile(wbuf, it, evac):
        def f(e, it=it):
            for sl2 in range(2):
                for ci in range(8):
                    ins = e.matmul(
                        pc[4 + sl2][:],
                        arena[:, ci * 2048 + it * 128:ci * 2048 + (it + 1) * 128],
                        wbuf[:, ci * 1024 + sl2 * 512:ci * 1024 + (sl2 + 1) * 512],
                        start=(ci == 0), stop=(ci == 7))
            return ins
        ch.step("tensor", f, settle_cycles=128)
        ch.step("vector", evac)

    def rope_steps():
        ch.step("vector", lambda e: e.tensor_tensor(
            far[:, 3072:4096], far[:, 4096:5120], rope_c[:], ALU.mult))

        def halves(e):
            e.tensor_tensor(
                AP(far, 2048, [[FAR, 128], [64, 16], [1, 32]]),
                AP(far, 4096 + 32, [[FAR, 128], [64, 16], [1, 32]]),
                AP(rope_s, 0, [[1024, 128], [64, 16], [1, 32]]), ALU.mult)
            return e.tensor_tensor(
                AP(far, 2048 + 32, [[FAR, 128], [64, 16], [1, 32]]),
                AP(far, 4096, [[FAR, 128], [64, 16], [1, 32]]),
                AP(rope_s, 32, [[1024, 128], [64, 16], [1, 32]]), ALU.mult)
        ch.step("vector", halves)
        ch.step("vector", lambda e: e.tensor_tensor(
            pt_sb[:, 0:1024], far[:, 3072:4096], far[:, 2048:3072], ALU.add))

    WQ = [[1024, 128], [131072, 8], [1, 1024]]

    gp_load([(rmsw[:], W["n1rep"])], s_wl)
    for sq in range(SEQ_PER_CORE):
        for it in range(TPS):
            gt = sq * TPS + it
            sl = hr[:, (it % 2) * C:(it % 2 + 1) * C]
            sync_load([(sl, h_scr[gt * 128:(gt + 1) * 128, :])])
            rms_tile(sl, pt_sb[:, 0:1024], 4096)
            tr8_to(AP(arena, it * 128, [[AR, 128], [2048, 8], [1, 128]]),
                   pt_sb, 0)
        gp_load([(wb2[:], AP(W["wkT"].tensor, 0, WQ))], s_wl)
        for it in range(TPS):
            sync_load([(rope_c[:], W["cosr"][it * 128:(it + 1) * 128, :]),
                       (rope_s[:], W["sinr"][it * 128:(it + 1) * 128, :])])
            proj_tile(wb2, it, lambda e: (
                e.tensor_copy(far[:, 4096:4608], pc[4][:]),
                e.tensor_copy(far[:, 4608:5120], pc[5][:]))[-1])
            rope_steps()
            tr8_to(AP(arena, 16384 + it * 128, [[AR, 128], [2048, 8], [1, 128]]),
                   pt_sb, 0)
        gp_load([(wb2[:], AP(W["wvT"].tensor, 0, WQ))], s_wl)
        for it in range(TPS):
            proj_tile(wb2, it, lambda e, it=it: (
                e.tensor_copy(arena[:, 32768 + it * 1024:32768 + it * 1024 + 512],
                              pc[4][:]),
                e.tensor_copy(arena[:, 32768 + it * 1024 + 512:
                                    32768 + (it + 1) * 1024], pc[5][:]))[-1])
        gp_load([(wbig[:], AP(W["wqT"].tensor, 0, WQ)),
                 (wb2[:], AP(W["wpT"].tensor, 0, WQ))], s_wl)
        for qt in range(TPS):
            gt = sq * TPS + qt
            nkt = qt + 1
            nsl = (nkt * 128 + 511) // 512
            sync_load([(rope_c[:], W["cosr"][qt * 128:(qt + 1) * 128, :]),
                       (rope_s[:], W["sinr"][qt * 128:(qt + 1) * 128, :])])
            proj_tile(wbig, qt, lambda e: (
                e.tensor_copy(far[:, 4096:4608], pc[4][:]),
                e.tensor_copy(far[:, 4608:5120], pc[5][:]))[-1])
            rope_steps()
            tr8_to(AP(qt_fm, 0, [[1024, 128], [128, 8], [1, 128]]), pt_sb, 0)
            for h in range(H):
                po, co = (h % 2) * 64, (h // 2) * 128
                for s0 in range(0, nsl, 2):
                    npair = min(2, nsl - s0)

                    def smm(e, h=h, po=po, co=co, s0=s0, npair=npair):
                        for k in range(npair):
                            ss = s0 + k
                            ins = e.matmul(
                                pc[k][:], qt_fm[po:po + 64, co:co + 128],
                                arena[po:po + 64,
                                      16384 + (h // 2) * 2048 + ss * 512:
                                      16384 + (h // 2) * 2048 + (ss + 1) * 512],
                                start=True, stop=True)
                        return ins
                    ch.step("tensor", smm, settle_cycles=128)

                    def sev(e, s0=s0, npair=npair):
                        for k in range(npair):
                            ins = e.tensor_copy(
                                far[:, (s0 + k) * 512:(s0 + k + 1) * 512],
                                pc[k][:])
                        return ins
                    ch.step("vector", sev, settle_cycles=128)
                ch.step("vector", lambda e, qt=qt: e.tensor_tensor(
                    far[:, qt * 128:(qt + 1) * 128],
                    far[:, qt * 128:(qt + 1) * 128], addm[:], ALU.add),
                    settle_cycles=128)
                ch.step("vector", lambda e, nkt=nkt: e.reduce_max(
                    mcol[:, 0:1], far[:, 0:nkt * 128], axis=AX.X),
                    settle_cycles=128)
                ch.step("vector", lambda e: e.tensor_scalar_mul(
                    mcol[:, 1:2], mcol[:, 0:1], -SCA))
                ch.step("scalar", lambda e, nkt=nkt: e.activation(
                    pt_sb[:, 0:nkt * 128], far[:, 0:nkt * 128], AF.Exp,
                    bias=mcol[:, 1:2], scale=SCA, accum_out=mcol[:, 0:1]))
                ch.step("vector", lambda e: e.reciprocal(
                    mcol[:, 1:2], mcol[:, 0:1]))
                for g8 in range(0, nkt, 8):
                    j1 = min(nkt, g8 + 8)

                    def ptr(e, g8=g8, j1=j1):
                        for j in range(g8, j1):
                            ins = e.transpose(
                                trA[:, (j - g8) * 128:(j - g8 + 1) * 128],
                                pt_sb[:, j * 128:(j + 1) * 128], b.idb[:])
                        return ins
                    ch.step("tensor", ptr, settle_cycles=128)
                    ch.step("vector", lambda e, g8=g8, j1=j1: e.tensor_copy(
                        ptT[:, 0:(j1 - g8) * 128], trA[:, 0:(j1 - g8) * 128]),
                        settle_cycles=128)

                    def pv(e, g8=g8, j1=j1, h=h, nkt=nkt):
                        for j in range(g8, j1):
                            ins = e.matmul(
                                pc[2][:, 0:64],
                                ptT[:, (j - g8) * 128:(j - g8 + 1) * 128],
                                arena[:, 32768 + j * 1024 + h * 64:
                                      32768 + j * 1024 + (h + 1) * 64],
                                start=(j == 0), stop=(j == nkt - 1),
                                skip_group_check=True)
                        return ins
                    ch.step("tensor", pv, settle_cycles=128)
                ch.step("vector", lambda e, h=h: e.tensor_scalar(
                    ao[:, h * 64:(h + 1) * 64], pc[2][:, 0:64],
                    mcol[:, 1:2], None, ALU.mult))

            def aotr(e):
                for j in range(8):
                    ins = e.transpose(trB[:, j * 128:(j + 1) * 128],
                                      ao[:, j * 128:(j + 1) * 128], b.idb[:])
                return ins
            ch.step("tensor", aotr, settle_cycles=128)
            ch.step("vector", lambda e: e.tensor_copy(aof[:], trB[:]),
                    settle_cycles=128)

            def pmm(e):
                for c2 in range(2):
                    for k1 in range(8):
                        ins = e.matmul(
                            pc[4 + c2][:], aof[:, k1 * 128:(k1 + 1) * 128],
                            wb2[:, k1 * 1024 + c2 * 512:k1 * 1024 + (c2 + 1) * 512],
                            start=(k1 == 0), stop=(k1 == 7))
                return ins
            ch.step("tensor", pmm, settle_cycles=128)
            sync_load([(hr[:, 0:C], h_scr[gt * 128:(gt + 1) * 128, :])])

            def h1ev(e):
                e.tensor_tensor(hr[:, C:C + 512], pc[4][:], hr[:, 0:512],
                                ALU.add)
                return e.tensor_tensor(hr[:, C + 512:2 * C], pc[5][:],
                                       hr[:, 512:1024], ALU.add)
            ch.step("vector", h1ev)
            gp_store([(h_scr[gt * 128:(gt + 1) * 128, :], hr[:, C:2 * C])])

    gp_load([(rmsw[:], W["n2rep"]),
             (arena[:, 24576:24640],
              AP(W["rT"].tensor, 0, [[8, 128], [1024, 8], [1, 8]]))], s_wl)
    for grp in range(NG):
        for il in range(TLG):
            gt = grp * TLG + il
            sl = hr[:, (il % 2) * C:(il % 2 + 1) * C]
            sync_load([(sl, h_scr[gt * 128:(gt + 1) * 128, :])])
            rms_tile(sl, pt_sb[:, 0:1024], 8192)
            tr8_to(AP(arena, il * 128, [[AR, 128], [1024, 8], [1, 128]]),
                   pt_sb, 0)
        for il in range(TLG):
            def glmm(e, il=il):
                for ci in range(8):
                    ins = e.matmul(
                        pc[il % 2][:, 0:8],
                        arena[:, ci * 1024 + il * 128:ci * 1024 + (il + 1) * 128],
                        arena[:, 24576 + ci * 8:24576 + (ci + 1) * 8],
                        start=(ci == 0), stop=(ci == 7))
                return ins
            ch.step("tensor", glmm, settle_cycles=128)
            ch.step("vector", lambda e, il=il: e.tensor_copy(
                glog[:, il * 8:(il + 1) * 8], pc[il % 2][:, 0:8]),
                settle_cycles=128)
        EXPc, PRBc, SCRc, WWc = 0, 64, 128, 192
        ESUMc, RSUMc, M1c, M2c = 256, 264, 272, 280

        def v8(col):
            return AP(gb, col, [[320, 128], [8, 8], [1, 8]])

        def rep8(e, dstc, srcc):
            return e.tensor_copy(AP(gb, dstc, [[320, 128], [8, 8], [1, 8]]),
                                 AP(gb, srcc, [[320, 128], [1, 8], [0, 8]]))

        def expst(e):
            for il in range(8):
                ins = e.activation(gb[:, EXPc + il * 8:EXPc + (il + 1) * 8],
                                   glog[:, il * 8:(il + 1) * 8], AF.Exp,
                                   accum_out=gb[:, ESUMc + il:ESUMc + il + 1])
            return ins
        ch.step("scalar", expst)
        ch.step("vector", lambda e: e.reciprocal(gb[:, RSUMc:RSUMc + 8],
                                                 gb[:, ESUMc:ESUMc + 8]))
        ch.step("vector", lambda e: rep8(e, SCRc, RSUMc))
        ch.step("vector", lambda e: e.tensor_tensor(
            gb[:, PRBc:PRBc + 64], gb[:, EXPc:EXPc + 64],
            gb[:, SCRc:SCRc + 64], ALU.mult))
        ch.step("vector", lambda e: e.tensor_reduce(
            gb[:, M1c:M1c + 8], v8(PRBc), axis=AX.X, op=ALU.max))
        ch.step("vector", lambda e: rep8(e, SCRc, M1c))
        ch.step("vector", lambda e: e.tensor_tensor(
            gb[:, WWc:WWc + 64], gb[:, PRBc:PRBc + 64],
            gb[:, SCRc:SCRc + 64], ALU.is_lt))
        ch.step("vector", lambda e: e.tensor_tensor(
            gb[:, WWc:WWc + 64], gb[:, WWc:WWc + 64],
            gb[:, PRBc:PRBc + 64], ALU.mult))
        ch.step("vector", lambda e: e.tensor_reduce(
            gb[:, M2c:M2c + 8], v8(WWc), axis=AX.X, op=ALU.max))
        ch.step("vector", lambda e: rep8(e, SCRc, M2c))
        ch.step("vector", lambda e: e.tensor_tensor(
            gb[:, WWc:WWc + 64], gb[:, PRBc:PRBc + 64],
            gb[:, SCRc:SCRc + 64], ALU.is_ge))
        ch.step("vector", lambda e: e.tensor_tensor(
            gb[:, WWc:WWc + 64], gb[:, WWc:WWc + 64],
            gb[:, PRBc:PRBc + 64], ALU.mult))
        ch.step("vector", lambda e: e.tensor_reduce(
            gb[:, M1c:M1c + 8], v8(WWc), axis=AX.X, op=ALU.add))
        ch.step("vector", lambda e: e.reciprocal(gb[:, M2c:M2c + 8],
                                                 gb[:, M1c:M1c + 8]))
        ch.step("vector", lambda e: rep8(e, SCRc, M2c))
        ch.step("vector", lambda e: e.tensor_tensor(
            gb[:, WWc:WWc + 64], gb[:, WWc:WWc + 64],
            gb[:, SCRc:SCRc + 64], ALU.mult))
        for q in range(2):
            def wtx(e, q=q):
                for j in range(4):
                    ins = e.transpose(
                        pc[1][0:8, j * 128:(j + 1) * 128],
                        gb[:, WWc + (q * 4 + j) * 8:WWc + (q * 4 + j + 1) * 8],
                        b.idf[:])
                return ins
            ch.step("tensor", wtx, settle_cycles=128)
            ch.step("vector", lambda e, q=q: e.tensor_copy(
                wfm_g[0:8, q * 512:(q + 1) * 512], pc[1][0:8, 0:512]),
                settle_cycles=128)
        for eidx in range(8):
            first_e = (eidx == 0)
            for ts in range(2):
                ch.step("tensor", lambda e, eidx=eidx, ts=ts: e.matmul(
                    pc[3][:], bc_sb[0:8, eidx * 128:(eidx + 1) * 128],
                    wfm_g[0:8, ts * 512:(ts + 1) * 512],
                    start=True, stop=True), settle_cycles=128)
                ch.step("vector", lambda e, ts=ts: e.tensor_copy(
                    far[:, 9216 + ts * 512:9216 + (ts + 1) * 512], pc[3][:]),
                    settle_cycles=128)
            for fb4 in range(4):
                gp_load([
                    (wbig[:], AP(W["wgT"].tensor,
                                 eidx * 4194304 + fb4 * 1024,
                                 [[4096, 128], [524288, 8], [1, 1024]])),
                    (wb2[:], AP(W["wuT"].tensor,
                                eidx * 4194304 + fb4 * 1024,
                                [[4096, 128], [524288, 8], [1, 1024]])),
                    (arena[:, 16384:24576],
                     AP(W["wdT"].tensor, eidx * 4194304 + fb4 * 1048576,
                        [[1024, 128], [131072, 8], [1, 1024]]))], s_wl)
                for fc8 in range(8):
                    def gumm(e, fc8=fc8):
                        for dst, wbuf in ((0, wbig), (2, wb2)):
                            for ts in range(2):
                                for ci in range(8):
                                    ins = e.matmul(
                                        pc[dst + ts][:],
                                        wbuf[:, ci * 1024 + fc8 * 128:
                                             ci * 1024 + (fc8 + 1) * 128],
                                        arena[:, ci * 1024 + ts * 512:
                                              ci * 1024 + (ts + 1) * 512],
                                        start=(ci == 0), stop=(ci == 7))
                        return ins
                    ch.step("tensor", gumm, settle_cycles=128)
                    ch.step("scalar", lambda e: (
                        e.activation(far[:, 8192:8704], pc[0][:], AF.Silu),
                        e.activation(far[:, 8704:9216], pc[1][:], AF.Silu))[-1],
                        settle_cycles=128)
                    ch.step("vector", lambda e, fc8=fc8: (
                        e.tensor_tensor(
                            arena[:, 8192 + fc8 * 1024:8192 + fc8 * 1024 + 512],
                            far[:, 8192:8704], pc[2][:], ALU.mult),
                        e.tensor_tensor(
                            arena[:, 8192 + fc8 * 1024 + 512:
                                  8192 + (fc8 + 1) * 1024],
                            far[:, 8704:9216], pc[3][:], ALU.mult))[-1],
                        settle_cycles=128)
                for m0 in range(0, 16, 2):
                    def dnm(e, m0=m0):
                        for k in range(2):
                            cc, ts = divmod(m0 + k, 2)
                            for fc8 in range(8):
                                ins = e.matmul(
                                    pc[4 + k][:],
                                    arena[:, 16384 + fc8 * 1024 + cc * 128:
                                          16384 + fc8 * 1024 + (cc + 1) * 128],
                                    arena[:, 8192 + fc8 * 1024 + ts * 512:
                                          8192 + fc8 * 1024 + (ts + 1) * 512],
                                    start=(fc8 == 0), stop=(fc8 == 7))
                        return ins
                    ch.step("tensor", dnm, settle_cycles=128)

                    init = (first_e and fb4 == 0)

                    def dnmul(e, m0=m0, init=init):
                        for k in range(2):
                            cc, ts = divmod(m0 + k, 2)
                            wsl = far[:, 9216 + ts * 512:9216 + (ts + 1) * 512]
                            dst = (far[:, cc * 1024 + ts * 512:
                                       cc * 1024 + (ts + 1) * 512] if init
                                   else far[:, 8192 + k * 512:
                                            8192 + (k + 1) * 512])
                            ins = e.tensor_tensor(dst, pc[4 + k][:], wsl,
                                                  ALU.mult)
                        return ins
                    ch.step("vector", dnmul, settle_cycles=128)
                    if not init:
                        def dnadd(e, m0=m0):
                            for k in range(2):
                                cc, ts = divmod(m0 + k, 2)
                                dst = far[:, cc * 1024 + ts * 512:
                                          cc * 1024 + (ts + 1) * 512]
                                ins = e.tensor_tensor(
                                    dst, dst,
                                    far[:, 8192 + k * 512:8192 + (k + 1) * 512],
                                    ALU.add)
                            return ins
                        ch.step("vector", dnadd, settle_cycles=128)
        for il in range(TLG):
            gt = grp * TLG + il

            def mtr(e, il=il):
                for cc in range(8):
                    ins = e.transpose(
                        pc[4 + cc // 4][:, (cc % 4) * 128:(cc % 4 + 1) * 128],
                        far[:, cc * 1024 + il * 128:cc * 1024 + (il + 1) * 128],
                        b.idf[:])
                return ins
            ch.step("tensor", mtr, settle_cycles=128)
            sync_load([(hr[:, 0:C], h_scr[gt * 128:(gt + 1) * 128, :]),
                       (far[:, 8192:9216], x_hbm[gt * 128:(gt + 1) * 128, :])])
            ch.step("vector", lambda e: (
                e.tensor_tensor(far[:, 9216:9728], hr[:, 0:512], pc[4][:],
                                ALU.add),
                e.tensor_tensor(far[:, 9728:10240], hr[:, 512:1024], pc[5][:],
                                ALU.add))[-1])
            ch.step("vector", lambda e: (
                e.scalar_tensor_tensor(hr[:, C:C + 512], far[:, 9216:9728],
                                       scale, far[:, 8192:8704],
                                       ALU.mult, ALU.add),
                e.scalar_tensor_tensor(hr[:, C + 512:2 * C],
                                       far[:, 9728:10240], scale,
                                       far[:, 8704:9216],
                                       ALU.mult, ALU.add))[-1])
            gp_store([(xo_hbm[gt * 128:(gt + 1) * 128, :], hr[:, C:2 * C])])
    def drain(e, n=None):
        e.wait_ge(s_hst.h, s_hst.n)
        e.wait_ge(ch.s.h, ch.s.n)
    block.gpsimd(drain)


def f0_host_weights(z):
    qkv = z["L0_ffn_qkv_w"]
    hw = dict(
        wqT=np.ascontiguousarray(qkv[0:1024].T),
        wkT=np.ascontiguousarray(qkv[1024:2048].T),
        wvT=np.ascontiguousarray(qkv[2048:3072].T),
        wpT=np.ascontiguousarray(z["L0_ffn_proj_w"].T),
        rT=np.ascontiguousarray(z["L0_ffn_router_w"].T),
        wgT=np.ascontiguousarray(np.transpose(z["L0_ffn_wg"], (0, 2, 1))),
        wuT=np.ascontiguousarray(np.transpose(z["L0_ffn_wu"], (0, 2, 1))),
        wdT=np.ascontiguousarray(np.transpose(z["L0_ffn_wd"], (0, 2, 1))),
        n1rep=np.ascontiguousarray(
            np.broadcast_to(z["L0_ffn_norm1_w"], (128, 1024))),
        n2rep=np.ascontiguousarray(
            np.broadcast_to(z["L0_ffn_norm2_w"], (128, 1024))),
    )
    inv = 1.0 / (10000.0 ** (np.arange(0, 64, 2, dtype=np.float64) / 64.0))
    t = np.arange(T, dtype=np.float64)
    fr = np.outer(t, inv)
    emb = np.concatenate([fr, fr], axis=1)
    cos, sin = np.cos(emb), np.sin(emb)
    sin_signed = np.concatenate([-sin[:, :32], sin[:, 32:]], axis=1)
    hw["cosr"] = np.ascontiguousarray(np.tile(cos, (1, 16)).astype(np.float32))
    hw["sinr"] = np.ascontiguousarray(
        np.tile(sin_signed, (1, 16)).astype(np.float32))
    q = np.arange(128)
    hw["addm"] = np.where(q[None, :] > q[:, None], NEG30, 0.0).astype(np.float32)
    hw["bc8"] = np.repeat(np.eye(8, dtype=np.float32), 128, axis=1)
    return hw


def build_F0_program(z):
    b = Bld("f0")
    hw = f0_host_weights(z)
    x_in = b.inp("x_in", np.zeros((NTOK, C), np.float32))
    W = {k: b.inp("f0_" + k, v) for k, v in hw.items()}
    xo = b.out("x_out", (NTOK, C), F32)
    with b:
        emit_F0(b, b.block, x_in, xo, W, scale=0.5)
        b.finish()
    return b, hw


def build_FUSED_program(z):
    b = Bld("mkf")
    hws = {"f0": f0_host_weights(z), "a0": a0_host_weights(z),
           "f1": f1_host_weights(z), "a1": a1_host_weights(z),
           "f2": f2_host_weights(z), "f3": f3_host_weights(z),
           "a3": a3_host_weights(z)}
    b.debug = False
    x_in = b.inp("x_in", np.zeros((NTOK, C), np.float32))
    filt = b.inp("filt", np.zeros((T,), np.float32))
    Wd = {pfx: {k: b.inp(pfx + "_" + k, v) for k, v in hw.items()}
          for pfx, hw in hws.items()}
    xo = b.out("x_out", (NTOK, C), F32)
    with b:
        scr = [b.scratch(f"xh{i}", [NTOK, C], F32) for i in range(11)]
        aps = [x_in] + scr + [xo]
        chain = ("f0", "a0", "f0", "f1", "a1", "f1",
                 "f2", "a2", "f2", "f3", "a3", "f3")
        for i, name in enumerate(chain):
            b.begin_sublayer()
            xi, xoi = aps[i], aps[i + 1]
            if name == "f0":
                emit_F0(b, b.block, xi, xoi, Wd["f0"], scale=0.5)
            elif name == "a0":
                emit_A0(b, b.block, xi, xoi, Wd["a0"])
            elif name == "f1":
                emit_F1(b, b.block, xi, xoi, Wd["f1"], scale=0.5)
            elif name == "a1":
                emit_A1(b, b.block, xi, xoi, Wd["a1"])
            elif name == "f2":
                emit_F2(b, b.block, xi, xoi, Wd["f2"], scale=0.5)
            elif name == "a2":
                emit_A2(b, b.block, xi, xoi, filt)
            elif name == "f3":
                emit_F3(b, b.block, xi, xoi, Wd["f3"], scale=0.5)
            elif name == "a3":
                emit_A3(b, b.block, xi, xoi, Wd["a3"])
        b.finish()
    return b, hws


FUSED = True


def kernel(**inputs):
    if FUSED:
        z = {k: np.asarray(v) for k, v in inputs.items()}
        x = np.ascontiguousarray(z["L0_attn_x"])
        b, hws = build_FUSED_program(z)
        eye = np.eye(128, dtype=np.float32)
        ones = np.ones((128, 128), np.float32)
        in_maps = []
        for c in range(NCORES):
            m = {"x_in": np.ascontiguousarray(
                    x[2 * c:2 * c + 2].reshape(NTOK, C)),
                 "filt": np.ascontiguousarray(z["L2_attn_filt"]),
                 "csts": b.cst_host_array(), "ident": eye, "ones64": ones}
            for pfx, hw in hws.items():
                m.update({pfx + "_" + k: v for k, v in hw.items()})
            in_maps.append(m)
        res = run_bass_kernel_spmd(b.nc, in_maps, core_ids=list(range(NCORES)))
        return np.stack([r["x_out"].reshape(SEQ_PER_CORE, T, C)
                         for r in res.results]).reshape(B_FULL, T, C)
    return _kernel_unfused(inputs)


def _kernel_unfused(inputs):
    z = {k: np.asarray(v) for k, v in inputs.items()}
    x = np.ascontiguousarray(z["L0_attn_x"])
    eye = np.eye(128, dtype=np.float32)
    ones = np.ones((128, 128), np.float32)
    builders = {
        "f0": build_F0_program, "a0": build_A0_program,
        "f1": build_F1_program, "a1": build_A1_program,
        "f2": build_F2_program, "a2": None,
        "f3": build_F3_program, "a3": build_A3_program,
    }
    progs = {}

    def get(name):
        if name not in progs:
            if name == "a2":
                progs[name] = (build_A2_program(None), {})
            else:
                progs[name] = builders[name](z)
        return progs[name]

    def launch(name, xcur):
        b, hw = get(name)
        in_maps = []
        for c in range(NCORES):
            m = {"x_in": np.ascontiguousarray(
                    xcur[2 * c:2 * c + 2].reshape(NTOK, C)),
                 "csts": b.cst_host_array(), "ident": eye, "ones64": ones}
            if name == "a2":
                m["filt"] = np.ascontiguousarray(z["L2_attn_filt"])
            else:
                m.update({name + "_" + k: v for k, v in hw.items()})
            in_maps.append(m)
        res = run_bass_kernel_spmd(b.nc, in_maps, core_ids=list(range(NCORES)))
        return np.stack([r["x_out"].reshape(SEQ_PER_CORE, T, C)
                         for r in res.results]).reshape(B_FULL, T, C)

    for name in ("f0", "a0", "f0", "f1", "a1", "f1",
                 "f2", "a2", "f2", "f3", "a3", "f3"):
        x = launch(name, x)
    return x
